# version 16
# baseline (speedup 1.0000x reference)
import numpy as np
from contextlib import ExitStack

import concourse.bass as bass
import concourse.mybir as mybir
from concourse.bass_utils import run_bass_kernel_spmd

F32 = mybir.dt.float32
BF16 = mybir.dt.bfloat16
I32 = mybir.dt.int32
AF = mybir.ActivationFunctionType
ALU = mybir.AluOpType

D = 1024
T = 2048
HID = 3072
KC = 8
NT = 4
HB = 256
NHB = HID // HB
S_FFN = 1.0 / 127.0
C_FFN = 0.5 * S_FFN * S_FFN
EPS_LN = 1e-5
EPS_AT = 1e-6

ENGS = ("sync", "tensor", "vector", "scalar", "gpsimd")


class Plan:

    def __init__(self, sems, cnt):
        self.q = {e: [] for e in ENGS}
        self.sems = sems
        self.cnt = cnt

    def wait(self, eng, sem, val):
        if val is None or val <= 0:
            return
        assert val <= self.cnt[sem], f"wait {sem}>={val} but only {self.cnt[sem]} emitted"
        h = self.sems[sem]
        self.q[eng].append(lambda e, h=h, v=val: e.wait_ge(h, v))

    def op(self, eng, name, *args, inc=None, inck=1, **kw):
        h = self.sems[inc] if inc else None

        def f(e, name=name, args=args, kw=kw, h=h, inck=inck):
            ins = getattr(e, name)(*args, **kw)
            if h is not None:
                ins.then_inc(h, inck)
        self.q[eng].append(f)
        if inc:
            self.cnt[inc] += inck
            return self.cnt[inc]
        return None

    def run(self, nc):
        with nc.Block(no_gpsimd_drain=True) as b:
            for e in ENGS:
                if self.q[e]:
                    getattr(b, e)(lambda eng, q=tuple(self.q[e]): [f(eng) for f in q])


def build_nc():
    nc = bass.Bass("TRN2", target_bir_lowering=False)

    xd = nc.dram_tensor("x", [D, T], F32, kind="ExternalInput").ap()
    wqd = nc.dram_tensor("wq", [D, D], F32, kind="ExternalInput").ap()
    wkd = nc.dram_tensor("wk", [D, D], F32, kind="ExternalInput").ap()
    wvd = nc.dram_tensor("wv", [D, D], F32, kind="ExternalInput").ap()
    wod = nc.dram_tensor("wo", [D, D], F32, kind="ExternalInput").ap()
    bod = nc.dram_tensor("bo", [128, 8], F32, kind="ExternalInput").ap()
    wgd = nc.dram_tensor("wg", [D, HID], I32, kind="ExternalInput").ap()
    wud = nc.dram_tensor("wu", [D, HID], I32, kind="ExternalInput").ap()
    wdd = nc.dram_tensor("wd", [HID, D], I32, kind="ExternalInput").ap()
    cnd = nc.dram_tensor("cons", [128, 130], F32, kind="ExternalInput").ap()
    yd = nc.dram_tensor("y", [D, T], F32, kind="ExternalOutput").ap()

    cc_in = nc.dram_tensor("cc_in", [128, 520], F32)
    cc_out = nc.dram_tensor("cc_out", [128, 520], F32)

    st = ExitStack()
    with st:
        xs = st.enter_context(nc.sbuf_tensor("xs", [128, KC * T], F32))
        wstg = st.enter_context(nc.sbuf_tensor("wstg", [128, 2048], I32))
        fstg = st.enter_context(nc.sbuf_tensor("fstg", [128, 1024], F32))
        cons = st.enter_context(nc.sbuf_tensor("cons_sb", [128, 130], F32))
        bo_sb = st.enter_context(nc.sbuf_tensor("bo_sb", [128, 8], F32))
        m2r2 = st.enter_context(nc.sbuf_tensor("m2r2", [1, 2 * T], F32))
        rows = st.enter_context(nc.sbuf_tensor("rows", [128, 2048], F32))
        kvks = st.enter_context(nc.sbuf_tensor("kvks", [128, 520], F32))

        ones_col = cons[:, 0:1]
        ones_row = cons[0:1, 2:130]
        nc.const_aps.register(F32, float(EPS_AT), cons[:, 1:2])

        sem_names = ("IN", "I2", "I3", "I4", "LD", "GP", "CC", "MM", "VE", "SC")
        sems = {n: st.enter_context(nc.semaphore(n)) for n in sem_names}
        cnt = {n: 0 for n in sem_names}

        bank_free = {}
        wstg_last = [0, 0]
        xs_last = {}
        m2r2_tok = {}
        ln_war = {"split_ve": 0}

        def wait_bank(p, bname):
            if bname in bank_free:
                s, t = bank_free[bname]
                p.wait("tensor", s, t)

        bar = {n: 0 for n in sem_names}

        def barrier():
            bar.update({n: cnt[n] for n in sem_names})

        def phase_entry(p, tensor_too=True):
            engs = ("tensor", "vector", "scalar") if tensor_too else ("vector", "scalar")
            for e in engs:
                for s in ("MM", "VE", "SC"):
                    p.wait(e, s, bar[s])

        P = Plan(sems, cnt)
        p0 = P
        tok_x = []
        for kc in range(KC):
            tok_x.append(p0.op("sync", "dma_start", out=xs[:, kc * T:(kc + 1) * T],
                               in_=xd[kc * 128:(kc + 1) * 128, :], inc="LD", inck=16))
        tok_cons = p0.op("sync", "dma_start", out=cons[:, :], in_=cnd, inc="LD", inck=16)
        tok_bo = p0.op("sync", "dma_start", out=bo_sb[:, :], in_=bod, inc="LD", inck=16)
        barrier()

        m_r = rows[0:1, 0:512]
        a_r = rows[0:1, 512:1024]
        b_r = rows[0:1, 1024:1536]
        s5 = rows[:, 1536:2048]

        def emit_ln_stats(p, s1, s2, mb, rb, nt, sq0, sq1, first_ln, save, lnexp=False):
            c0 = nt * 512
            wait_bank(p, s1[0]); wait_bank(p, s2[0])
            p.wait("scalar", "VE", ln_war["split_ve"])
            for kc in range(KC):
                p.wait("scalar", "VE", xs_last.get((kc, nt), 0))
                p.wait("tensor", "VE", xs_last.get((kc, nt), 0))
            tok_sq = []
            for kc in range(KC):
                sq = sq0 if kc % 2 == 0 else sq1
                if first_ln and kc == 0:
                    p.wait("scalar", "LD", tok_bo)
                if kc >= 2:
                    p.wait("scalar", "MM", tok_sq[kc - 2])
                p.op("scalar", "square", out=sq[:, :], in_=xs[:, kc * T + c0:kc * T + c0 + 512],
                     inc="SC")
                p.wait("tensor", "SC", cnt["SC"])
                if first_ln and kc == 0:
                    p.wait("tensor", "LD", tok_bo)
                p.op("tensor", "matmul", s1[1][0:1, :], ones_col,
                     xs[:, kc * T + c0:kc * T + c0 + 512],
                     start=(kc == 0), stop=(kc == KC - 1))
                tok_sq.append(p.op("tensor", "matmul", s2[1][0:1, :], ones_col, sq[:, :],
                                   start=(kc == 0), stop=(kc == KC - 1), inc="MM"))
            p.wait("vector", "MM", tok_sq[-1])
            tm = p.op("vector", "tensor_scalar_mul", m_r, s1[1][0:1, :], 1.0 / D, inc="VE")
            ta = p.op("vector", "tensor_scalar_mul", a_r, s2[1][0:1, :], 1.0 / D, inc="VE")
            bank_free[s1[0]] = ("VE", cnt["VE"])
            bank_free[s2[0]] = ("VE", cnt["VE"])
            p.wait("vector", "VE", tm)
            tb = p.op("vector", "tensor_mul", b_r, m_r, m_r, inc="VE")
            p.wait("vector", "VE", tb)
            tv = p.op("vector", "tensor_sub", a_r, a_r, b_r, inc="VE")
            p.wait("vector", "VE", tv)
            tok_v = p.op("vector", "tensor_scalar_add", b_r, a_r, EPS_LN, inc="VE")
            p.wait("scalar", "VE", tok_v)
            p.wait("scalar", "MM", ln_war.get("bc_mm", 0))
            if lnexp:
                p.op("scalar", "activation", out=a_r, in_=b_r, func=AF.Ln, inc="SC")
                p.wait("scalar", "SC", cnt["SC"])
                tok_q = p.op("scalar", "activation", out=b_r, in_=a_r, func=AF.Exp,
                             scale=-0.5, inc="SC")
                tok_rv = None
            else:
                tok_q = p.op("scalar", "sqrt", out=a_r, in_=b_r, inc="SC")
                p.wait("vector", "SC", tok_q)
                p.wait("vector", "MM", ln_war.get("bc_mm", 0))
                tok_rv = p.op("vector", "reciprocal", out=b_r, in_=a_r, inc="VE")
            if save:
                p.op("vector", "tensor_copy", out=m2r2[0:1, c0:c0 + 512], in_=m_r, inc="VE")
                if tok_rv is None:
                    p.wait("vector", "SC", tok_q)
                else:
                    p.wait("vector", "VE", tok_rv)
                tok_rv = m2r2_tok[nt] = p.op("vector", "tensor_copy",
                                             out=m2r2[0:1, T + c0:T + c0 + 512], in_=b_r,
                                             inc="VE")
            wait_bank(p, mb[0]); wait_bank(p, rb[0])
            if tok_rv is not None:
                p.wait("tensor", "VE", tok_rv)
            else:
                p.wait("tensor", "SC", tok_q)
                p.wait("tensor", "VE", tm)
            p.wait("tensor", "LD", tok_bo)
            p.op("tensor", "matmul", mb[1][:, :], ones_row, m_r, start=True, stop=True)
            ln_war["bc_mm"] = p.op("tensor", "matmul", rb[1][:, :], ones_row, b_r,
                                   start=True, stop=True, inc="MM")
            return ln_war["bc_mm"]

        def emit_ln_split(p, mb, rb, nt, off, w, y_hi, y_lo, ycol, t1, t2, tok_bc, war_mm):
            c0 = nt * 512 + off
            p.wait("vector", "MM", tok_bc)
            if war_mm:
                p.wait("vector", "MM", war_mm)
                p.wait("scalar", "MM", war_mm)
            last = None
            for kc in range(KC):
                xsl = xs[:, kc * T + c0:kc * T + c0 + w]
                p.wait("vector", "SC", cnt["SC"])
                ts_ = p.op("vector", "tensor_sub", t1[:, 0:w], xsl, mb[1][:, off:off + w], inc="VE")
                p.wait("vector", "VE", ts_)
                tok_t2 = p.op("vector", "tensor_mul", t2[:, 0:w], t1[:, 0:w],
                              rb[1][:, off:off + w], inc="VE")
                p.wait("scalar", "VE", tok_t2)
                p.op("scalar", "copy", out=y_hi[:, ycol + kc * w:ycol + (kc + 1) * w],
                     in_=t2[:, 0:w], inc="SC")
                p.wait("vector", "SC", cnt["SC"])
                last = p.op("vector", "tensor_sub", y_lo[:, ycol + kc * w:ycol + (kc + 1) * w],
                            t2[:, 0:w], y_hi[:, ycol + kc * w:ycol + (kc + 1) * w], inc="VE")
            bank_free[mb[0]] = ("VE", last)
            bank_free[rb[0]] = ("VE", last)
            return last

        def emit_ln_split_ffn(p, mb, rb, nt, y_hi, y_lo, t1, t2, tok_bc):
            c0 = nt * 512
            p.wait("vector", "MM", tok_bc)
            last = None
            t1d = {0: t1, 1: s5}
            for kc in range(KC):
                xsl = xs[:, kc * T + c0:kc * T + c0 + 512]
                t1k = t1d[kc % 2]
                p.wait("vector", "VE", xs_last.get((kc, nt), 0))
                p.wait("vector", "VE", ln_war.get(f"t1rd{kc % 2}", 0))
                ts_ = p.op("vector", "tensor_sub", t1k[:, :], xsl, mb[1][:, :], inc="VE")
                p.wait("vector", "VE", ts_)
                p.op("vector", "tensor_mul", y_hi[:, kc * T + c0:kc * T + c0 + 512],
                     t1k[:, :], rb[1][:, :], inc="VE")
                p.wait("vector", "VE", ln_war.get("t2_rd_ve", 0))
                tf = p.op("vector", "tensor_mul", t2[:, :], t1k[:, :], rb[1][:, :], inc="VE")
                ln_war[f"t1rd{kc % 2}"] = tf
                p.wait("vector", "VE", tf)
                last = p.op("vector", "tensor_sub", y_lo[:, kc * T + c0:kc * T + c0 + 512],
                            t2[:, :], y_hi[:, kc * T + c0:kc * T + c0 + 512], inc="VE")
                ln_war["split_ve"] = last
                ln_war["t2_rd_ve"] = last
            bank_free[mb[0]] = ("VE", last)
            bank_free[rb[0]] = ("VE", last)
            return last

        def stage_split2(p, mat_d, hi, lo, split_tok, key, chain_state, war_mm):
            csem = ("I2", "I3")
            for s in range(16):
                c = s % 2
                half = fstg[:, c * 512:(c + 1) * 512]
                if chain_state[c]:
                    p.wait("sync", "VE", chain_state[c])
                tok_d = p.op("sync", "dma_start",
                             out=half.rearrange("p (a c) -> p a c", a=8),
                             in_=mat_d[:, s * 64:(s + 1) * 64].rearrange(
                                 "(a p) c -> p a c", p=128),
                             inc=csem[c], inck=16)
                hi3 = hi[:, :].rearrange("p (a c) -> p a c", a=8)[:, :, s * 64:(s + 1) * 64]
                lo3 = lo[:, :].rearrange("p (a c) -> p a c", a=8)[:, :, s * 64:(s + 1) * 64]
                f3 = half.rearrange("p (a c) -> p a c", a=8)
                p.wait("scalar", csem[c], tok_d)
                p.wait("scalar", "MM", war_mm)
                tok_h = p.op("scalar", "copy", out=hi3, in_=f3, inc="SC")
                p.wait("vector", "SC", tok_h)
                p.wait("vector", "MM", war_mm)
                tok_l = p.op("vector", "tensor_sub", lo3, f3, hi3, inc="VE")
                split_tok[(key, s)] = (tok_h, tok_l)
                chain_state[c] = tok_l

        def emit_ffn2(p, ci, out_dma):
            with ExitStack() as fs:
                y_hi = fs.enter_context(nc.sbuf_tensor(f"yh{ci}", [128, KC * T], BF16))
                y_lo = fs.enter_context(nc.sbuf_tensor(f"yl{ci}", [128, KC * T], BF16))
                wg_t = fs.enter_context(nc.sbuf_tensor(f"wg{ci}", [128, 2 * 2048], BF16))
                wu_t = fs.enter_context(nc.sbuf_tensor(f"wu{ci}", [128, 2 * 2048], BF16))
                wd_t = fs.enter_context(nc.sbuf_tensor(f"wdt{ci}", [128, 2 * 2048], BF16))
                h_hi = fs.enter_context(nc.sbuf_tensor(f"hh{ci}", [128, 2 * 1024], BF16))
                h_lo = fs.enter_context(nc.sbuf_tensor(f"hl{ci}", [128, 2 * 1024], BF16))
                scr = [fs.enter_context(nc.sbuf_tensor(f"sc{ci}_{i}", [128, 512], F32))
                       for i in range(4)]
                pbk = [(f"b{i}", fs.enter_context(nc.psum_tensor(f"fp{ci}_{i}",
                                                                   [128, 512], F32)))
                       for i in range(8)]

                wsrc = {"g": wgd, "u": wud, "d": wdd}
                wtile = {"g": wg_t, "u": wu_t}
                conv_tok = {}
                slot_last_mm = {}
                chainF = {0: wstg_last[0], 1: wstg_last[1]}
                csemF = ("IN", "I4")

                dma_toks = {}

                def emit_stage_dma(hb, m):
                    toks = []
                    for h in range(2):
                        if chainF[h]:
                            p.wait("sync", "SC", chainF[h])
                        half = wstg[:, h * 1024:(h + 1) * 1024]
                        if m == "d":
                            srcap = wsrc[m][hb * HB + h * 128:hb * HB + (h + 1) * 128, :]
                            dstap = half
                        else:
                            srcap = wsrc[m][:, hb * HB + h * 128:hb * HB + (h + 1) * 128]\
                                .rearrange("(a p) c -> p a c", p=128)
                            dstap = half.rearrange("p (a c) -> p a c", a=8)
                        toks.append(p.op("sync", "dma_start", out=dstap, in_=srcap,
                                         inc=csemF[h], inck=16))
                        chainF[h] = 0
                    dma_toks[(hb, m)] = toks

                def emit_stage_conv(hb, m):
                    slot = hb % 2
                    toks = dma_toks[(hb, m)]
                    key = (m, slot)
                    tok_c = None
                    for h in range(2):
                        p.wait("scalar", csemF[h], toks[h])
                        if key in slot_last_mm:
                            p.wait("scalar", "MM", slot_last_mm[key])
                        half = wstg[:, h * 1024:(h + 1) * 1024]
                        if m == "d":
                            tok_c = p.op("scalar", "copy",
                                         out=wd_t[:, slot * 2048 + h * 1024:
                                                  slot * 2048 + (h + 1) * 1024],
                                         in_=half, inc="SC")
                        else:
                            cdst = wtile[m][:, slot * 2048:(slot + 1) * 2048].rearrange(
                                "p (k c) -> p k c", k=8)[:, :, h * 128:(h + 1) * 128]
                            tok_c = p.op("scalar", "copy", out=cdst,
                                         in_=half.rearrange("p (k c) -> p k c", k=8),
                                         inc="SC")
                        chainF[h] = tok_c
                    conv_tok[(hb, m)] = tok_c

                def emit_stage(hb, m):
                    emit_stage_dma(hb, m)
                    emit_stage_conv(hb, m)

                y_tok = {}

                def emit_ln_tile(nt):
                    tb = emit_ln_stats(p, pbk[6], pbk[7], pbk[6], pbk[7], nt,
                                       scr[2], scr[3], first_ln=(ci == 0), save=False)
                    y_tok[nt] = emit_ln_split_ffn(p, pbk[6], pbk[7], nt, y_hi, y_lo,
                                                  scr[2], scr[3], tb)

                emit_stage_dma(0, "g")
                emit_ln_tile(0)
                emit_stage_conv(0, "g")
                emit_stage(0, "u")
                emit_stage(0, "d")
                emit_stage(1, "g")
                emit_stage(1, "u")
                emit_stage(1, "d")

                gu_i = [0]
                d_i = [0]
                sil_rd = [0]
                hf_rd = [0]
                h_tok = {}
                h_slot_rd = {}
                stt_tok = {}
                bank_mm = {}

                def emit_gu2(hb, nt, hslot):
                    slot = hb % 2
                    sil, hf = scr[0], scr[1]
                    for hl in range(2):
                        bg = pbk[gu_i[0] % 4]; gu_i[0] += 1
                        bu = pbk[gu_i[0] % 4]; gu_i[0] += 1
                        for m, bank in (("g", bg), ("u", bu)):
                            wait_bank(p, bank[0])
                            p.wait("tensor", "SC", conv_tok[(hb, m)])
                            p.wait("tensor", "VE", y_tok[nt])
                            n = 0
                            for ysrc in (y_hi, y_lo):
                                for kc in range(KC):
                                    tok = p.op(
                                        "tensor", "matmul", bank[1][:, :],
                                        wtile[m][:, slot * 2048 + kc * 256 + hl * 128:
                                                 slot * 2048 + kc * 256 + (hl + 1) * 128],
                                        ysrc[:, kc * T + nt * 512:kc * T + nt * 512 + 512],
                                        start=(n == 0), stop=(n == 15),
                                        inc=("MM" if n == 15 else None))
                                    n += 1
                            slot_last_mm[(m, slot)] = tok
                            bank_mm[bank[0]] = tok
                        hc0 = hslot * 1024 + hl * 512
                        p.wait("scalar", "MM", bank_mm[bg[0]])
                        p.wait("scalar", "VE", sil_rd[0])
                        if hslot in h_slot_rd:
                            p.wait("scalar", "MM", h_slot_rd[hslot])
                        tok_sil = p.op("scalar", "activation", out=sil[:, :], in_=bg[1][:, :],
                                       func=AF.Silu, scale=float(S_FFN), inc="SC")
                        bank_free[bg[0]] = ("SC", tok_sil)
                        p.wait("vector", "SC", tok_sil)
                        p.wait("vector", "MM", bank_mm[bu[0]])
                        if hslot in h_slot_rd:
                            p.wait("vector", "MM", h_slot_rd[hslot])
                        p.wait("vector", "VE", hf_rd[0])
                        tok_hf = p.op("vector", "tensor_mul", hf[:, :], sil[:, :], bu[1][:, :],
                                      inc="VE")
                        sil_rd[0] = tok_hf
                        bank_free[bu[0]] = ("VE", tok_hf)
                        p.wait("scalar", "VE", tok_hf)
                        tok_hi = p.op("scalar", "copy", out=h_hi[:, hc0:hc0 + 512],
                                      in_=hf[:, :], inc="SC")
                        p.wait("vector", "SC", tok_hi)
                        h_tok[(hb, nt)] = p.op("vector", "tensor_sub", h_lo[:, hc0:hc0 + 512],
                                               hf[:, :], h_hi[:, hc0:hc0 + 512], inc="VE")
                        hf_rd[0] = h_tok[(hb, nt)]

                def emit_down(hb, nt, hslot):
                    dslot = hb % 2
                    last_mm = None
                    for oc in range(8):
                        bank = pbk[4 + d_i[0] % 2]; d_i[0] += 1
                        wait_bank(p, bank[0])
                        p.wait("tensor", "SC", conv_tok[(hb, "d")])
                        p.wait("tensor", "VE", h_tok[(hb, nt)])
                        n = 0
                        for hsrc in (h_hi, h_lo):
                            for hl in range(2):
                                tok = p.op(
                                    "tensor", "matmul", bank[1][:, :],
                                    wd_t[:, dslot * 2048 + hl * 1024 + oc * 128:
                                         dslot * 2048 + hl * 1024 + (oc + 1) * 128],
                                    hsrc[:, hslot * 1024 + hl * 512:hslot * 1024 + (hl + 1) * 512],
                                    start=(n == 0), stop=(n == 3),
                                    inc=("MM" if n == 3 else None))
                                n += 1
                        last_mm = tok
                        slot_last_mm[("d", dslot)] = tok
                        xsl = xs[:, oc * T + nt * 512:oc * T + nt * 512 + 512]
                        p.wait("vector", "MM", tok)
                        if (oc, nt) in stt_tok:
                            p.wait("vector", "VE", stt_tok[(oc, nt)])
                        stt_tok[(oc, nt)] = p.op(
                            "vector", "scalar_tensor_tensor", out=xsl, in0=bank[1][:, :],
                            scalar=float(C_FFN), in1=xsl, op0=ALU.mult, op1=ALU.add, inc="VE")
                        xs_last[(oc, nt)] = stt_tok[(oc, nt)]
                        bank_free[bank[0]] = ("VE", stt_tok[(oc, nt)])
                    h_slot_rd[hslot] = last_mm

                pend = None
                items = []
                for nt in range(NT):
                    items.append((0, nt))
                    items.append((1, nt))
                for hb in range(2, NHB):
                    for nt in range(NT):
                        items.append((hb, nt))
                for idx, (hb, nt) in enumerate(items):
                    if idx == 1:
                        emit_stage_dma(2, "g")
                    if idx == 7:
                        emit_stage_conv(2, "g")
                        emit_stage_dma(2, "u")
                    if hb == 2 and nt == 0:
                        emit_stage_conv(2, "u")
                        emit_stage_dma(2, "d")
                    if hb >= 2 and nt == 1 and hb + 1 < NHB:
                        emit_stage_conv(hb + 1, "g")
                        emit_stage_dma(hb + 1, "u")
                    if hb >= 2 and nt == 2 and hb + 1 < NHB:
                        emit_stage_conv(hb + 1, "u")
                        emit_stage_dma(hb + 1, "d")
                    emit_gu2(hb, nt, idx % 2)
                    if hb == 0 and nt + 1 < NT:
                        emit_ln_tile(nt + 1)
                    if pend is not None:
                        emit_down(*pend)
                    pend = (hb, nt, idx % 2)
                    if hb >= 2 and nt == 0:
                        emit_stage_conv(hb, "d")
                        if hb + 1 < NHB:
                            emit_stage_dma(hb + 1, "g")
                emit_down(*pend)

                wstg_last[0] = chainF[0]
                wstg_last[1] = chainF[1]
                if out_dma:
                    for nt in range(NT):
                        for oc in range(8):
                            p.wait("gpsimd", "VE", stt_tok[(oc, nt)])
                            p.op("gpsimd", "dma_start",
                                 out=yd[oc * 128:(oc + 1) * 128, nt * 512:(nt + 1) * 512],
                                 in_=xs[:, oc * T + nt * 512:oc * T + nt * 512 + 512],
                                 inc="GP", inck=16)
                    p.wait("gpsimd", "GP", cnt["GP"])

                barrier()

        pf1 = P
        phase_entry(pf1)
        emit_ffn2(pf1, 0, out_dma=False)

        pa = P
        p = pa
        phase_entry(pa, tensor_too=False)
        with ExitStack() as fs:
            wk_hi = fs.enter_context(nc.sbuf_tensor("wk_hi", [128, 8192], BF16))
            wk_lo = fs.enter_context(nc.sbuf_tensor("wk_lo", [128, 8192], BF16))
            wv_hi = fs.enter_context(nc.sbuf_tensor("wv_hi", [128, 8192], BF16))
            wv_lo = fs.enter_context(nc.sbuf_tensor("wv_lo", [128, 8192], BF16))
            y2h = fs.enter_context(nc.sbuf_tensor("y2h", [128, 2 * 2048], BF16))
            y2l = fs.enter_context(nc.sbuf_tensor("y2l", [128, 2 * 2048], BF16))
            kfT = fs.enter_context(nc.sbuf_tensor("kfT", [128, 2048], F32))
            vT = fs.enter_context(nc.sbuf_tensor("vT", [128, 2048], F32))
            scrA = [fs.enter_context(nc.sbuf_tensor(f"sA{i}", [128, 512], F32)) for i in range(4)]
            pbA = [(f"b{i}", fs.enter_context(nc.psum_tensor(f"ap{i}", [128, 512], F32)))
                   for i in range(8)]
            kv_b, ks_b = pbA[4], pbA[5]

            split_tok = {}
            chainA = [0, 0]
            stage_split2(p, wkd, wk_hi, wk_lo, split_tok, "k", chainA, bar["MM"])

            def emit_ln_split_pair(nt, off, w, mb, rb, t1, t2, tok_bc, war_mm, slot):
                c0 = nt * 512 + off
                s0 = slot * 2048
                p.wait("vector", "MM", tok_bc)
                if war_mm:
                    p.wait("vector", "MM", war_mm)
                    p.wait("scalar", "MM", war_mm)
                last = None
                t1d = {0: t1, 1: s5}
                for kc in range(KC):
                    xsl = xs[:, kc * T + c0:kc * T + c0 + w]
                    t1k = t1d[kc % 2]
                    p.wait("vector", "VE", xs_last.get((kc, nt), 0))
                    p.wait("vector", "VE", ln_war.get(f"t1rd{kc % 2}", 0))
                    ts_ = p.op("vector", "tensor_sub", t1k[:, 0:w], xsl, mb[1][:, off:off + w],
                               inc="VE")
                    p.wait("vector", "VE", ts_)
                    p.op("vector", "tensor_mul", y2h[:, s0 + kc * w:s0 + (kc + 1) * w],
                         t1k[:, 0:w], rb[1][:, off:off + w], inc="VE")
                    p.wait("vector", "VE", ln_war.get("t2_rd_ve", 0))
                    tf = p.op("vector", "tensor_mul", t2[:, 0:w], t1k[:, 0:w],
                              rb[1][:, off:off + w], inc="VE")
                    ln_war[f"t1rd{kc % 2}"] = tf
                    p.wait("vector", "VE", tf)
                    last = p.op("vector", "tensor_sub", y2l[:, s0 + kc * w:s0 + (kc + 1) * w],
                                t2[:, 0:w], y2h[:, s0 + kc * w:s0 + (kc + 1) * w], inc="VE")
                    ln_war["split_ve"] = last
                    ln_war["t2_rd_ve"] = last
                bank_free[mb[0]] = ("VE", last)
                bank_free[rb[0]] = ("VE", last)
                return last

            projA_i = [0]
            eluA = {"t1_rd": 0, "t2_rd": 0}
            kv_started = {0: False, 64: False}
            ks_started = [False]
            mm_proj_last = [bar["MM"], bar["MM"]]
            halves = [(nt, h2) for nt in range(NT) for h2 in range(2)]
            kvacc_tok = [bar["MM"]]

            def emit_projA_mm(key, hi, lo, y2tok, slot):
                banks = []
                s0 = slot * 2048
                for tp in range(2):
                    for dh in range(2):
                        bank = pbA[projA_i[0] % 4]; projA_i[0] += 1
                        wait_bank(p, bank[0])
                        p.wait("tensor", "VE", y2tok)
                        p.wait("tensor", "SC", split_tok[(key, 8 * dh + 7)][0])
                        p.wait("tensor", "VE", split_tok[(key, 8 * dh + 7)][1])
                        n = 0
                        for ysrc, wsrc_ in ((y2h, hi), (y2l, hi), (y2h, lo)):
                            for kc in range(KC):
                                tok = p.op(
                                    "tensor", "matmul", bank[1][:, :],
                                    ysrc[:, s0 + kc * 256 + tp * 128:
                                         s0 + kc * 256 + (tp + 1) * 128],
                                    wsrc_[:, kc * 1024 + dh * 512:kc * 1024 + (dh + 1) * 512],
                                    start=(n == 0), stop=(n == 23),
                                    inc=("MM" if n == 23 else None))
                                n += 1
                        banks.append((tp, dh, bank, tok))
                mm_proj_last[slot] = banks[-1][3]
                return banks

            def emit_projA_evac(key, banks):
                last = None
                for (tp, dh, bank, tok) in banks:
                    dstc = tp * 1024 + dh * 512
                    dst = kfT if key == "k" else vT
                    p.wait("vector", "MM", tok)
                    p.wait("vector", "MM", kvacc_tok[0])
                    if key == "k":
                        t1, t2 = scrA[0], scrA[1]
                        p.wait("vector", "SC", eluA["t1_rd"])
                        tok_min = p.op("vector", "tensor_scalar_min", t1[:, :], bank[1][:, :],
                                       0.0, inc="VE")
                        p.wait("scalar", "VE", tok_min)
                        p.wait("scalar", "VE", eluA["t2_rd"])
                        eluA["t1_rd"] = p.op("scalar", "activation", out=t2[:, :], in_=t1[:, :],
                                             func=AF.Exp, inc="SC")
                        p.wait("vector", "SC", eluA["t1_rd"])
                        last = p.op("vector", "scalar_tensor_tensor", out=dst[:, dstc:dstc + 512],
                                    in0=bank[1][:, :], scalar=0.0, in1=t2[:, :],
                                    op0=ALU.max, op1=ALU.add, inc="VE")
                        eluA["t2_rd"] = last
                    else:
                        last = p.op("vector", "tensor_copy", out=dst[:, dstc:dstc + 512],
                                    in_=bank[1][:, :], inc="VE")
                    bank_free[bank[0]] = ("VE", last)
                return last

            def emit_kvacc(ev_tok, is_last):
                p.wait("tensor", "VE", ev_tok)
                tok = None
                for tp in range(2):
                    for hp in range(8):
                        for par in (0, 64):
                            h = 2 * hp + (1 if par else 0)
                            stt_ = not kv_started[par]
                            kv_started[par] = True
                            p.op("tensor", "matmul",
                                 kv_b[1][par:par + 64, hp * 64:(hp + 1) * 64],
                                 kfT[:, tp * 1024 + h * 64:tp * 1024 + (h + 1) * 64],
                                 vT[:, tp * 1024 + h * 64:tp * 1024 + (h + 1) * 64],
                                 start=stt_, stop=(is_last and tp == 1 and hp == 7),
                                 tile_position=(0, par),
                                 skip_group_check=True)
                    for hp in range(8):
                        stt_ = not ks_started[0]
                        ks_started[0] = True
                        tok = p.op("tensor", "matmul", ks_b[1][:, hp:hp + 1],
                                   kfT[:, tp * 1024 + hp * 128:tp * 1024 + (hp + 1) * 128],
                                   ones_col,
                                   start=stt_, stop=(is_last and tp == 1 and hp == 7),
                                   skip_group_check=True,
                                   inc=("MM" if (tp == 1 and hp == 7) else None))
                kvacc_tok[0] = tok
                return tok

            pendA = None
            tok_bcA = {}
            tok_bcA[0] = emit_ln_stats(p, pbA[6], pbA[7], pbA[6], pbA[7], 0,
                                       scrA[2], scrA[3], first_ln=False, save=True,
                                       lnexp=True)
            for hi_, (nt, h2) in enumerate(halves):
                slot = hi_ % 2
                if pendA is not None and h2 == 1:
                    emit_kvacc(pendA, is_last=False)
                    pendA = None
                y2tok = emit_ln_split_pair(nt, h2 * 256, 256, pbA[6], pbA[7],
                                           scrA[2], scrA[3], tok_bcA[nt],
                                           mm_proj_last[slot], slot)
                bk = emit_projA_mm("k", wk_hi, wk_lo, y2tok, slot)
                if hi_ == 0:
                    stage_split2(p, wvd, wv_hi, wv_lo, split_tok, "v", chainA, bar["MM"])
                if h2 == 1 and nt + 1 < NT:
                    tok_bcA[nt + 1] = emit_ln_stats(p, pbA[6], pbA[7], pbA[6], pbA[7],
                                                    nt + 1, scrA[2], scrA[3],
                                                    first_ln=False, save=True,
                                                    lnexp=True)
                if pendA is not None:
                    emit_kvacc(pendA, is_last=False)
                    pendA = None
                evk = emit_projA_evac("k", bk)
                bv = emit_projA_mm("v", wv_hi, wv_lo, y2tok, slot)
                evv = emit_projA_evac("v", bv)
                pendA = evv
            emit_kvacc(pendA, is_last=True)

            p.wait("vector", "MM", cnt["MM"])
            p.op("vector", "tensor_copy", out=scrA[0][:, :], in_=kv_b[1][:, :], inc="VE")
            tok_ccs = p.op("vector", "tensor_copy", out=scrA[1][:, 0:8], in_=ks_b[1][:, 0:8],
                           inc="VE")
            p.wait("gpsimd", "VE", tok_ccs)
            p.op("gpsimd", "dma_start", out=cc_in.ap()[:, 0:512], in_=scrA[0][:, :],
                 inc="GP", inck=16)
            tok_d = p.op("gpsimd", "dma_start", out=cc_in.ap()[:, 512:520],
                         in_=scrA[1][:, 0:8], inc="GP", inck=16)
            p.wait("gpsimd", "GP", tok_d)
            p.op("gpsimd", "collective_compute", "AllReduce", ALU.add,
                 replica_groups=[[0, 1], [2, 3], [4, 5], [6, 7]],
                 ins=[cc_in.ap().opt()], outs=[cc_out.ap().opt()], inc="CC")
            p.wait("gpsimd", "CC", cnt["CC"])
            p.op("gpsimd", "dma_start", out=kvks[:, :], in_=cc_out.ap(), inc="GP", inck=16)

            barrier()
        ar_tok = cnt["GP"]

        pb_ = P
        p = pb_
        phase_entry(pb_, tensor_too=False)
        with ExitStack() as fs:
            wq_hi = fs.enter_context(nc.sbuf_tensor("wq_hi", [128, 8192], BF16))
            wq_lo = fs.enter_context(nc.sbuf_tensor("wq_lo", [128, 8192], BF16))
            wo_hi = fs.enter_context(nc.sbuf_tensor("wo_hi", [128, 8192], BF16))
            wo_lo = fs.enter_context(nc.sbuf_tensor("wo_lo", [128, 8192], BF16))
            y2h = fs.enter_context(nc.sbuf_tensor("y2hB", [128, 2048], BF16))
            y2l = fs.enter_context(nc.sbuf_tensor("y2lB", [128, 2048], BF16))
            qf = fs.enter_context(nc.sbuf_tensor("qf", [128, 2 * 2048], F32))
            at_hi = fs.enter_context(nc.sbuf_tensor("at_hi", [128, 2048], BF16))
            at_lo = fs.enter_context(nc.sbuf_tensor("at_lo", [128, 2048], BF16))
            ksB = fs.enter_context(nc.sbuf_tensor("ksB", [128, 8 * 128], F32))

            ty1 = rows[:, 0:256]
            ty2 = rows[:, 256:512]
            tq1 = rows[:, 512:768]
            tq2 = rows[:, 768:1024]
            ta1 = rows[:, 1024:1280]
            dne = rows[:, 1280:1536]
            scB = {"ty2_rd": 0, "tq1_rd": 0, "tq2_rd": 0, "ta1_rd": 0}
            pbB = [(f"b{i}", fs.enter_context(nc.psum_tensor(f"bp{i}", [128, 512], F32)))
                   for i in range(8)]

            split_tokB = {}
            chainB = [chainA[0], chainA[1]]
            stage_split2(p, wqd, wq_hi, wq_lo, split_tokB, "q", chainB, bar["MM"])

            qp_i = [0]
            op_i = [0]
            mm_qproj_last = [bar["MM"]]
            mm_oproj_last = [bar["MM"]]
            mm_hp_last = [bar["MM"]]

            def emit_y2b(nt, h2):
                c0 = nt * 512 + h2 * 256
                mb, rb = pbB[6], pbB[7]
                wait_bank(p, mb[0]); wait_bank(p, rb[0])
                p.wait("tensor", "VE", m2r2_tok.get(nt, 0))
                p.op("tensor", "matmul", mb[1][:, 0:256], ones_row, m2r2[0:1, c0:c0 + 256],
                     start=True, stop=True)
                tok_bc = p.op("tensor", "matmul", rb[1][:, 0:256], ones_row,
                              m2r2[0:1, T + c0:T + c0 + 256], start=True, stop=True, inc="MM")
                p.wait("vector", "MM", tok_bc)
                p.wait("vector", "MM", mm_qproj_last[0])
                p.wait("scalar", "MM", mm_qproj_last[0])
                last = None
                for kc in range(KC):
                    xsl = xs[:, kc * T + c0:kc * T + c0 + 256]
                    p.wait("vector", "VE", xs_last.get((kc, nt), 0))
                    p.wait("vector", "VE", scB.get("ty1_rd_ve", 0))
                    ts_ = p.op("vector", "tensor_sub", ty1, xsl, mb[1][:, 0:256], inc="VE")
                    p.wait("vector", "VE", ts_)
                    p.op("vector", "tensor_mul", y2h[:, kc * 256:(kc + 1) * 256],
                         ty1, rb[1][:, 0:256], inc="VE")
                    p.wait("vector", "VE", scB.get("ty2_rd_ve", 0))
                    tf = p.op("vector", "tensor_mul", ty2, ty1, rb[1][:, 0:256], inc="VE")
                    scB["ty1_rd_ve"] = tf
                    p.wait("vector", "VE", tf)
                    last = p.op("vector", "tensor_sub", y2l[:, kc * 256:(kc + 1) * 256],
                                ty2, y2h[:, kc * 256:(kc + 1) * 256], inc="VE")
                    scB["ty2_rd_ve"] = last
                bank_free[mb[0]] = ("VE", last)
                bank_free[rb[0]] = ("VE", last)
                return last

            ar_waited = [False]
            at_last = [0]
            p.wait("gpsimd", "GP", ar_tok)
            for _hp in range(8):
                p.op("gpsimd", "tensor_scalar",
                     ksB[:, _hp * 128:(_hp + 1) * 128], kvks[:, 0:128], 0.0,
                     kvks[:, 512 + _hp:513 + _hp],
                     op0=ALU.mult, op1=ALU.add,
                     inc=("CC" if _hp == 7 else None))
            ksb_cc = cnt["CC"]

            def emit_hp_step(hp, qftok, slot):
                qf0 = slot * 2048
                dn, qk = pbB[4], pbB[5]
                if not ar_waited[0]:
                    p.wait("tensor", "GP", ar_tok)
                    p.wait("tensor", "CC", ksb_cc)
                    ar_waited[0] = True
                wait_bank(p, dn[0])
                p.wait("tensor", "VE", qftok)
                kb = ksB[:, hp * 128:(hp + 1) * 128]
                p.op("tensor", "matmul", dn[1][0:64, 0:256], kb[0:64, 0:64],
                     qf[0:64, qf0 + hp * 256:qf0 + (hp + 1) * 256], start=True, stop=True,
                     tile_position=(0, 0), skip_group_check=True)
                tok_dn = p.op("tensor", "matmul", dn[1][64:128, 0:256], kb[64:128, 64:128],
                              qf[64:128, qf0 + hp * 256:qf0 + (hp + 1) * 256],
                              start=True, stop=True,
                              tile_position=(64, 64), skip_group_check=True, inc="MM")
                wait_bank(p, qk[0])
                p.op("tensor", "matmul", qk[1][0:64, 0:256], kvks[0:64, hp * 64:(hp + 1) * 64],
                     qf[0:64, qf0 + hp * 256:qf0 + (hp + 1) * 256], start=True, stop=True,
                     tile_position=(0, 0), skip_group_check=True)
                tok_qk = p.op("tensor", "matmul", qk[1][64:128, 0:256],
                              kvks[64:128, hp * 64:(hp + 1) * 64],
                              qf[64:128, qf0 + hp * 256:qf0 + (hp + 1) * 256],
                              start=True, stop=True,
                              tile_position=(64, 64), skip_group_check=True, inc="MM")
                mm_hp_last[0] = tok_qk
                p.wait("scalar", "MM", tok_dn)
                tok_eps = p.op("scalar", "activation", out=dne, in_=dn[1][:, 0:256],
                               func=AF.Ln, bias=float(EPS_AT), scale=1.0, inc="SC")
                bank_free[dn[0]] = ("SC", tok_eps)
                p.wait("scalar", "VE", scB.get("rcp_rd", 0))
                p.wait("scalar", "SC", tok_eps)
                tok_rcp = p.op("scalar", "activation", out=dne, in_=dne,
                               func=AF.Exp, scale=-1.0, inc="SC")
                p.wait("vector", "MM", tok_qk)
                p.wait("vector", "SC", tok_rcp)
                p.wait("vector", "MM", mm_oproj_last[0])
                p.wait("vector", "SC", scB["ta1_rd"])
                p.wait("vector", "VE", at_last[0])
                tok_at = p.op("vector", "tensor_mul", ta1, qk[1][:, 0:256], dne,
                              inc="VE")
                scB["rcp_rd"] = tok_at
                bank_free[qk[0]] = ("VE", tok_at)
                p.wait("scalar", "VE", tok_at)
                p.wait("scalar", "MM", mm_oproj_last[0])
                scB["ta1_rd"] = p.op("scalar", "copy", out=at_hi[:, hp * 256:(hp + 1) * 256],
                                     in_=ta1, inc="SC")
                p.wait("vector", "SC", scB["ta1_rd"])
                at_last[0] = p.op("vector", "tensor_sub", at_lo[:, hp * 256:(hp + 1) * 256],
                                  ta1, at_hi[:, hp * 256:(hp + 1) * 256], inc="VE")
                return at_last[0]

            def emit_qproj(y2tok, slot, hp_cb=None):
                qf0 = slot * 2048
                last = None
                for oc in range(8):
                    bank = pbB[qp_i[0] % 2]; qp_i[0] += 1
                    wait_bank(p, bank[0])
                    p.wait("tensor", "SC", split_tokB[("q", 2 * oc + 1)][0])
                    p.wait("tensor", "VE", split_tokB[("q", 2 * oc + 1)][1])
                    p.wait("tensor", "VE", y2tok)
                    n = 0
                    for ysrc, wsrc_ in ((y2h, wq_hi), (y2l, wq_hi), (y2h, wq_lo)):
                        for kc in range(KC):
                            tok = p.op("tensor", "matmul", bank[1][:, 0:256],
                                       wsrc_[:, kc * 1024 + oc * 128:kc * 1024 + (oc + 1) * 128],
                                       ysrc[:, kc * 256:(kc + 1) * 256],
                                       start=(n == 0), stop=(n == 23),
                                       inc=("MM" if n == 23 else None))
                            n += 1
                    mm_qproj_last[0] = tok
                    if hp_cb is not None:
                        hp_cb(oc)
                    p.wait("vector", "MM", tok)
                    p.wait("vector", "MM", mm_hp_last[0])
                    p.wait("vector", "SC", scB["tq1_rd"])
                    tok_min = p.op("vector", "tensor_scalar_min", tq1, bank[1][:, 0:256],
                                   0.0, inc="VE")
                    p.wait("scalar", "VE", tok_min)
                    p.wait("scalar", "VE", scB["tq2_rd"])
                    scB["tq1_rd"] = p.op("scalar", "activation", out=tq2, in_=tq1,
                                         func=AF.Exp, inc="SC")
                    p.wait("vector", "SC", scB["tq1_rd"])
                    last = p.op("vector", "scalar_tensor_tensor",
                                out=qf[:, qf0 + oc * 256:qf0 + (oc + 1) * 256],
                                in0=bank[1][:, 0:256], scalar=0.0, in1=tq2,
                                op0=ALU.max, op1=ALU.add, inc="VE")
                    scB["tq2_rd"] = last
                    bank_free[bank[0]] = ("VE", last)
                return last

            def emit_oproj(nt, h2, attok):
                c0 = nt * 512 + h2 * 256
                for oc in range(8):
                    bank = pbB[2 + op_i[0] % 2]; op_i[0] += 1
                    wait_bank(p, bank[0])
                    p.wait("tensor", "SC", split_tokB[("o", 2 * oc + 1)][0])
                    p.wait("tensor", "VE", split_tokB[("o", 2 * oc + 1)][1])
                    p.wait("tensor", "VE", attok)
                    n = 0
                    for asrc, wsrc_ in ((at_hi, wo_hi), (at_lo, wo_hi), (at_hi, wo_lo)):
                        for kc in range(KC):
                            tok = p.op("tensor", "matmul", bank[1][:, 0:256],
                                       wsrc_[:, kc * 1024 + oc * 128:kc * 1024 + (oc + 1) * 128],
                                       asrc[:, kc * 256:(kc + 1) * 256],
                                       start=(n == 0), stop=(n == 23),
                                       inc=("MM" if n == 23 else None))
                            n += 1
                    mm_oproj_last[0] = tok
                    xsl = xs[:, oc * T + c0:oc * T + c0 + 256]
                    p.wait("vector", "MM", tok)
                    p.wait("vector", "LD", tok_bo)
                    tk = p.op("vector", "scalar_tensor_tensor", out=xsl, in0=bank[1][:, 0:256],
                              scalar=bo_sb[:, oc:oc + 1], in1=xsl, op0=ALU.add, op1=ALU.add,
                              inc="VE")
                    xs_last[(oc, nt)] = tk
                    bank_free[bank[0]] = ("VE", tk)

            halvesB = [(nt, h2) for nt in range(NT) for h2 in range(2)]
            pend1 = None
            pend2 = None
            y2next = emit_y2b(*halvesB[0])
            for i, (nt, h2) in enumerate(halvesB):
                y2t = y2next
                slot = i % 2
                if pend2 is None:
                    qt = emit_qproj(y2t, slot)
                    if i == 0:
                        stage_split2(p, wod, wo_hi, wo_lo, split_tokB, "o", chainB, bar["MM"])
                else:
                    (pnt2, ph22, pq2, ps2) = pend2
                    qt = emit_qproj(y2t, slot,
                                    hp_cb=lambda g, pq=pq2, ps=ps2: emit_hp_step(g, pq, ps))
                if i + 1 < len(halvesB):
                    y2next = emit_y2b(*halvesB[i + 1])
                if pend2 is not None:
                    emit_oproj(pnt2, ph22, at_last[0])
                pend2 = pend1
                pend1 = (nt, h2, qt, slot)
            for (pnt2, ph22, pq2, ps2) in (pend2, pend1):
                for hp in range(8):
                    emit_hp_step(hp, pq2, ps2)
                emit_oproj(pnt2, ph22, at_last[0])

            barrier()

        pf2 = P
        phase_entry(pf2, tensor_too=False)
        emit_ffn2(pf2, 1, out_dma=True)

        P.run(nc)

    return nc


_NC_CACHE = []


def kernel(L0_attn_x, L0_attn_wq, L0_attn_wk, L0_attn_wv, L0_attn_wo, L0_attn_bo,
           L0_ffn_w_gate, L0_ffn_w_up, L0_ffn_w_down):
    x = np.asarray(L0_attn_x)
    wqT = np.ascontiguousarray(np.asarray(L0_attn_wq).T)
    wkT = np.ascontiguousarray(np.asarray(L0_attn_wk).T)
    wvT = np.ascontiguousarray(np.asarray(L0_attn_wv).T)
    woT = np.ascontiguousarray(np.asarray(L0_attn_wo).T)
    bo2 = np.ascontiguousarray(np.asarray(L0_attn_bo).reshape(8, 128).T)
    wgT = np.ascontiguousarray(np.asarray(L0_ffn_w_gate).T)
    wuT = np.ascontiguousarray(np.asarray(L0_ffn_w_up).T)
    wdT = np.ascontiguousarray(np.asarray(L0_ffn_w_down).T)

    cons = np.zeros((128, 130), np.float32)
    cons[:, 0] = 1.0
    cons[:, 1] = EPS_AT
    cons[0, 2:130] = 1.0

    in_maps = []
    for c in range(8):
        b, h = c // 2, c % 2
        xsh = np.ascontiguousarray(x[b, h * T:(h + 1) * T, :].T)
        in_maps.append({"x": xsh, "wq": wqT, "wk": wkT, "wv": wvT, "wo": woT,
                        "bo": bo2, "wg": wgT, "wu": wuT, "wd": wdT, "cons": cons})

    if not _NC_CACHE:
        _NC_CACHE.append(build_nc())
    nc = _NC_CACHE[0]
    res = run_bass_kernel_spmd(nc, in_maps, core_ids=list(range(8)))

    out = np.empty((4, 4096, 1024), np.float32)
    for c in range(8):
        b, h = c // 2, c % 2
        out[b, h * T:(h + 1) * T, :] = res.results[c]["y"].T
    return out


# version 17
# speedup vs baseline: 1.1059x; 1.1059x over previous
import numpy as np
from contextlib import ExitStack

import concourse.bass as bass
import concourse.mybir as mybir
from concourse.bass_utils import run_bass_kernel_spmd

F32 = mybir.dt.float32
BF16 = mybir.dt.bfloat16
I32 = mybir.dt.int32
AF = mybir.ActivationFunctionType
ALU = mybir.AluOpType

D = 1024
T = 2048
HID = 3072
KC = 8
NT = 4
HB = 256
NHB = HID // HB
S_FFN = 1.0 / 127.0
C_FFN = 0.5 * S_FFN * S_FFN
EPS_LN = 1e-5
EPS_AT = 1e-6

ENGS = ("sync", "tensor", "vector", "scalar", "gpsimd")


class Plan:

    def __init__(self, sems, cnt):
        self.q = {e: [] for e in ENGS}
        self.sems = sems
        self.cnt = cnt

    def wait(self, eng, sem, val):
        if val is None or val <= 0:
            return
        assert val <= self.cnt[sem], f"wait {sem}>={val} but only {self.cnt[sem]} emitted"
        h = self.sems[sem]
        self.q[eng].append(lambda e, h=h, v=val: e.wait_ge(h, v))

    def op(self, eng, name, *args, inc=None, inck=1, **kw):
        h = self.sems[inc] if inc else None

        def f(e, name=name, args=args, kw=kw, h=h, inck=inck):
            ins = getattr(e, name)(*args, **kw)
            if h is not None:
                ins.then_inc(h, inck)
        self.q[eng].append(f)
        if inc:
            self.cnt[inc] += inck
            return self.cnt[inc]
        return None

    def run(self, nc):
        with nc.Block(no_gpsimd_drain=True) as b:
            for e in ENGS:
                if self.q[e]:
                    getattr(b, e)(lambda eng, q=tuple(self.q[e]): [f(eng) for f in q])


def build_nc():
    nc = bass.Bass("TRN2", target_bir_lowering=False)

    xd = nc.dram_tensor("x", [D, T], F32, kind="ExternalInput").ap()
    wqd = nc.dram_tensor("wq", [D, D], F32, kind="ExternalInput").ap()
    wkd = nc.dram_tensor("wk", [D, D], F32, kind="ExternalInput").ap()
    wvd = nc.dram_tensor("wv", [D, D], F32, kind="ExternalInput").ap()
    wod = nc.dram_tensor("wo", [D, D], F32, kind="ExternalInput").ap()
    bod = nc.dram_tensor("bo", [128, 8], F32, kind="ExternalInput").ap()
    wgd = nc.dram_tensor("wg", [D, HID], I32, kind="ExternalInput").ap()
    wud = nc.dram_tensor("wu", [D, HID], I32, kind="ExternalInput").ap()
    wdd = nc.dram_tensor("wd", [HID, D], I32, kind="ExternalInput").ap()
    cnd = nc.dram_tensor("cons", [128, 130], F32, kind="ExternalInput").ap()
    yd = nc.dram_tensor("y", [D, T], F32, kind="ExternalOutput").ap()

    cc_in = nc.dram_tensor("cc_in", [128, 520], F32)
    cc_out = nc.dram_tensor("cc_out", [128, 520], F32)

    st = ExitStack()
    with st:
        xs = st.enter_context(nc.sbuf_tensor("xs", [128, KC * T], F32))
        wstg = st.enter_context(nc.sbuf_tensor("wstg", [128, 2048], I32))
        fstg = st.enter_context(nc.sbuf_tensor("fstg", [128, 1024], F32))
        cons = st.enter_context(nc.sbuf_tensor("cons_sb", [128, 130], F32))
        bo_sb = st.enter_context(nc.sbuf_tensor("bo_sb", [128, 8], F32))
        m2r2 = st.enter_context(nc.sbuf_tensor("m2r2", [1, 2 * T], F32))
        rows = st.enter_context(nc.sbuf_tensor("rows", [128, 2048], F32))
        kvks = st.enter_context(nc.sbuf_tensor("kvks", [128, 520], F32))

        ones_col = cons[:, 0:1]
        ones_row = cons[0:1, 2:130]
        nc.const_aps.register(F32, float(EPS_AT), cons[:, 1:2])

        sem_names = ("IN", "I2", "I3", "I4", "LD", "GP", "CC", "MM", "VE", "SC")
        sems = {n: st.enter_context(nc.semaphore(n)) for n in sem_names}
        cnt = {n: 0 for n in sem_names}

        bank_free = {}
        wstg_last = [0, 0]
        xs_last = {}
        m2r2_tok = {}
        ln_war = {"split_ve": 0}

        def wait_bank(p, bname):
            if bname in bank_free:
                s, t = bank_free[bname]
                p.wait("tensor", s, t)

        bar = {n: 0 for n in sem_names}

        def barrier():
            bar.update({n: cnt[n] for n in sem_names})

        def phase_entry(p, tensor_too=True):
            engs = ("tensor", "vector", "scalar") if tensor_too else ("vector", "scalar")
            for e in engs:
                for s in ("MM", "VE", "SC"):
                    p.wait(e, s, bar[s])

        P = Plan(sems, cnt)
        p0 = P
        tok_x = []
        for kc in range(KC):
            tok_x.append(p0.op("sync", "dma_start", out=xs[:, kc * T:(kc + 1) * T],
                               in_=xd[kc * 128:(kc + 1) * 128, :], inc="LD", inck=16))
        tok_cons = p0.op("sync", "dma_start", out=cons[:, :], in_=cnd, inc="LD", inck=16)
        tok_bo = p0.op("sync", "dma_start", out=bo_sb[:, :], in_=bod, inc="LD", inck=16)
        barrier()

        m_r = rows[0:1, 0:512]
        a_r = rows[0:1, 512:1024]
        b_r = rows[0:1, 1024:1536]
        s5 = rows[:, 1536:2048]

        def emit_ln_stats(p, s1, s2, mb, rb, nt, sq0, sq1, first_ln, save, lnexp=False):
            c0 = nt * 512
            wait_bank(p, s1[0]); wait_bank(p, s2[0])
            p.wait("scalar", "VE", ln_war["split_ve"])
            for kc in range(KC):
                p.wait("scalar", "VE", xs_last.get((kc, nt), 0))
                p.wait("tensor", "VE", xs_last.get((kc, nt), 0))
            tok_sq = []
            for kc in range(KC):
                sq = sq0 if kc % 2 == 0 else sq1
                if first_ln and kc == 0:
                    p.wait("scalar", "LD", tok_bo)
                if kc >= 2:
                    p.wait("scalar", "MM", tok_sq[kc - 2])
                p.op("scalar", "square", out=sq[:, :], in_=xs[:, kc * T + c0:kc * T + c0 + 512],
                     inc="SC")
                p.wait("tensor", "SC", cnt["SC"])
                if first_ln and kc == 0:
                    p.wait("tensor", "LD", tok_bo)
                p.op("tensor", "matmul", s1[1][0:1, :], ones_col,
                     xs[:, kc * T + c0:kc * T + c0 + 512],
                     start=(kc == 0), stop=(kc == KC - 1))
                tok_sq.append(p.op("tensor", "matmul", s2[1][0:1, :], ones_col, sq[:, :],
                                   start=(kc == 0), stop=(kc == KC - 1), inc="MM"))
            p.wait("vector", "MM", tok_sq[-1])
            tm = p.op("vector", "tensor_scalar_mul", m_r, s1[1][0:1, :], 1.0 / D, inc="VE")
            ta = p.op("vector", "tensor_scalar_mul", a_r, s2[1][0:1, :], 1.0 / D, inc="VE")
            bank_free[s1[0]] = ("VE", cnt["VE"])
            bank_free[s2[0]] = ("VE", cnt["VE"])
            p.wait("vector", "VE", tm)
            tb = p.op("vector", "tensor_mul", b_r, m_r, m_r, inc="VE")
            p.wait("vector", "VE", tb)
            tv = p.op("vector", "tensor_sub", a_r, a_r, b_r, inc="VE")
            p.wait("vector", "VE", tv)
            tok_v = p.op("vector", "tensor_scalar_add", b_r, a_r, EPS_LN, inc="VE")
            p.wait("scalar", "VE", tok_v)
            p.wait("scalar", "MM", ln_war.get("bc_mm", 0))
            if lnexp:
                p.op("scalar", "activation", out=a_r, in_=b_r, func=AF.Ln, inc="SC")
                p.wait("scalar", "SC", cnt["SC"])
                tok_q = p.op("scalar", "activation", out=b_r, in_=a_r, func=AF.Exp,
                             scale=-0.5, inc="SC")
                tok_rv = None
            else:
                tok_q = p.op("scalar", "sqrt", out=a_r, in_=b_r, inc="SC")
                p.wait("vector", "SC", tok_q)
                p.wait("vector", "MM", ln_war.get("bc_mm", 0))
                tok_rv = p.op("vector", "reciprocal", out=b_r, in_=a_r, inc="VE")
            ln_war["rows_sc"] = tok_q
            if save:
                p.op("vector", "tensor_copy", out=m2r2[0:1, c0:c0 + 512], in_=m_r, inc="VE")
                if tok_rv is None:
                    p.wait("vector", "SC", tok_q)
                else:
                    p.wait("vector", "VE", tok_rv)
                tok_rv = m2r2_tok[nt] = p.op("vector", "tensor_copy",
                                             out=m2r2[0:1, T + c0:T + c0 + 512], in_=b_r,
                                             inc="VE")
            wait_bank(p, mb[0]); wait_bank(p, rb[0])
            if tok_rv is not None:
                p.wait("tensor", "VE", tok_rv)
            else:
                p.wait("tensor", "SC", tok_q)
                p.wait("tensor", "VE", tm)
            p.wait("tensor", "LD", tok_bo)
            p.op("tensor", "matmul", mb[1][:, :], ones_row, m_r, start=True, stop=True)
            ln_war["bc_mm"] = p.op("tensor", "matmul", rb[1][:, :], ones_row, b_r,
                                   start=True, stop=True, inc="MM")
            return ln_war["bc_mm"]

        def emit_ln_split(p, mb, rb, nt, off, w, y_hi, y_lo, ycol, t1, t2, tok_bc, war_mm):
            c0 = nt * 512 + off
            p.wait("vector", "MM", tok_bc)
            if war_mm:
                p.wait("vector", "MM", war_mm)
                p.wait("scalar", "MM", war_mm)
            last = None
            for kc in range(KC):
                xsl = xs[:, kc * T + c0:kc * T + c0 + w]
                p.wait("vector", "SC", cnt["SC"])
                ts_ = p.op("vector", "tensor_sub", t1[:, 0:w], xsl, mb[1][:, off:off + w], inc="VE")
                p.wait("vector", "VE", ts_)
                tok_t2 = p.op("vector", "tensor_mul", t2[:, 0:w], t1[:, 0:w],
                              rb[1][:, off:off + w], inc="VE")
                p.wait("scalar", "VE", tok_t2)
                p.op("scalar", "copy", out=y_hi[:, ycol + kc * w:ycol + (kc + 1) * w],
                     in_=t2[:, 0:w], inc="SC")
                p.wait("vector", "SC", cnt["SC"])
                last = p.op("vector", "tensor_sub", y_lo[:, ycol + kc * w:ycol + (kc + 1) * w],
                            t2[:, 0:w], y_hi[:, ycol + kc * w:ycol + (kc + 1) * w], inc="VE")
            bank_free[mb[0]] = ("VE", last)
            bank_free[rb[0]] = ("VE", last)
            return last

        def emit_ln_split_ffn(p, mb, rb, nt, y_hi, y_lo, t1, t2, tok_bc):
            c0 = nt * 512
            p.wait("vector", "MM", tok_bc)
            last = None
            t1d = {0: t1, 1: s5}
            for kc in range(KC):
                xsl = xs[:, kc * T + c0:kc * T + c0 + 512]
                t1k = t1d[kc % 2]
                p.wait("vector", "VE", xs_last.get((kc, nt), 0))
                p.wait("vector", "VE", ln_war.get(f"t1rd{kc % 2}", 0))
                ts_ = p.op("vector", "tensor_sub", t1k[:, :], xsl, mb[1][:, :], inc="VE")
                p.wait("vector", "VE", ts_)
                p.op("vector", "tensor_mul", y_hi[:, kc * T + c0:kc * T + c0 + 512],
                     t1k[:, :], rb[1][:, :], inc="VE")
                p.wait("vector", "VE", ln_war.get("t2_rd_ve", 0))
                tf = p.op("vector", "tensor_mul", t2[:, :], t1k[:, :], rb[1][:, :], inc="VE")
                ln_war[f"t1rd{kc % 2}"] = tf
                p.wait("vector", "VE", tf)
                last = p.op("vector", "tensor_sub", y_lo[:, kc * T + c0:kc * T + c0 + 512],
                            t2[:, :], y_hi[:, kc * T + c0:kc * T + c0 + 512], inc="VE")
                ln_war["split_ve"] = last
                ln_war["t2_rd_ve"] = last
            bank_free[mb[0]] = ("VE", last)
            bank_free[rb[0]] = ("VE", last)
            return last

        ty1 = rows[:, 0:256]
        ty2 = rows[:, 256:512]

        def emit_y2b_core(p, mb, rb, nt, h2, hi_ap, lo_ap, war_hi, war_lo, st):
            c0 = nt * 512 + h2 * 256
            wait_bank(p, mb[0]); wait_bank(p, rb[0])
            p.wait("tensor", "VE", m2r2_tok.get(nt, 0))
            p.op("tensor", "matmul", mb[1][:, 0:256], ones_row, m2r2[0:1, c0:c0 + 256],
                 start=True, stop=True)
            tok_bc = p.op("tensor", "matmul", rb[1][:, 0:256], ones_row,
                          m2r2[0:1, T + c0:T + c0 + 256], start=True, stop=True, inc="MM")
            p.wait("vector", "MM", tok_bc)
            p.wait("vector", "MM", ln_war.get("bc_mm", 0))
            p.wait("vector", "MM", war_lo)
            p.wait("vector", "VE", ln_war["split_ve"])
            p.wait("scalar", "MM", war_hi)
            p.wait("scalar", "VE", ln_war["split_ve"])
            last = None
            for kc in range(KC):
                xsl = xs[:, kc * T + c0:kc * T + c0 + 256]
                p.wait("vector", "VE", xs_last.get((kc, nt), 0))
                p.wait("vector", "VE", st["ty1_rd"])
                ts_ = p.op("vector", "tensor_sub", ty1, xsl, mb[1][:, 0:256], inc="VE")
                p.wait("vector", "VE", ts_)
                p.wait("vector", "SC", st["ty2_sc"])
                p.wait("vector", "VE", st["ty2_ve"])
                tok_t2 = p.op("vector", "tensor_mul", ty2, ty1, rb[1][:, 0:256], inc="VE")
                st["ty1_rd"] = tok_t2
                p.wait("scalar", "VE", tok_t2)
                th = p.op("scalar", "copy", out=hi_ap[:, kc * 256:(kc + 1) * 256],
                          in_=ty2, inc="SC")
                st["ty2_sc"] = th
                p.wait("vector", "SC", th)
                last = p.op("vector", "tensor_sub", lo_ap[:, kc * 256:(kc + 1) * 256],
                            ty2, hi_ap[:, kc * 256:(kc + 1) * 256], inc="VE")
                st["ty2_ve"] = last
            bank_free[mb[0]] = ("VE", last)
            bank_free[rb[0]] = ("VE", last)
            return last

        def stage_split2(p, mat_d, hi, lo, split_tok, key, chain_state, war_mm,
                         s_lo=0, s_hi=16):
            csem = ("I2", "I3")
            for s in range(s_lo, s_hi):
                c = s % 2
                half = fstg[:, c * 512:(c + 1) * 512]
                if chain_state[c]:
                    p.wait("sync", "VE", chain_state[c])
                tok_d = p.op("sync", "dma_start",
                             out=half.rearrange("p (a c) -> p a c", a=8),
                             in_=mat_d[:, s * 64:(s + 1) * 64].rearrange(
                                 "(a p) c -> p a c", p=128),
                             inc=csem[c], inck=16)
                hi3 = hi[:, :].rearrange("p (a c) -> p a c", a=8)[:, :, s * 64:(s + 1) * 64]
                lo3 = lo[:, :].rearrange("p (a c) -> p a c", a=8)[:, :, s * 64:(s + 1) * 64]
                f3 = half.rearrange("p (a c) -> p a c", a=8)
                p.wait("scalar", csem[c], tok_d)
                p.wait("scalar", "MM", war_mm)
                tok_h = p.op("scalar", "copy", out=hi3, in_=f3, inc="SC")
                p.wait("vector", "SC", tok_h)
                p.wait("vector", "MM", war_mm)
                tok_l = p.op("vector", "tensor_sub", lo3, f3, hi3, inc="VE")
                split_tok[(key, s)] = (tok_h, tok_l)
                chain_state[c] = tok_l

        def emit_ffn2(p, ci, out_dma):
            with ExitStack() as fs:
                y_hi = fs.enter_context(nc.sbuf_tensor(f"yh{ci}", [128, KC * T], BF16))
                y_lo = fs.enter_context(nc.sbuf_tensor(f"yl{ci}", [128, KC * T], BF16))
                wg_t = fs.enter_context(nc.sbuf_tensor(f"wg{ci}", [128, 2 * 2048], BF16))
                wu_t = fs.enter_context(nc.sbuf_tensor(f"wu{ci}", [128, 2 * 2048], BF16))
                wd_t = fs.enter_context(nc.sbuf_tensor(f"wdt{ci}", [128, 2 * 2048], BF16))
                h_hi = fs.enter_context(nc.sbuf_tensor(f"hh{ci}", [128, 2 * 1024], BF16))
                h_lo = fs.enter_context(nc.sbuf_tensor(f"hl{ci}", [128, 2 * 1024], BF16))
                scr = [fs.enter_context(nc.sbuf_tensor(f"sc{ci}_{i}", [128, 512], F32))
                       for i in range(4)]
                pbk = [(f"b{i}", fs.enter_context(nc.psum_tensor(f"fp{ci}_{i}",
                                                                   [128, 512], F32)))
                       for i in range(8)]

                wsrc = {"g": wgd, "u": wud, "d": wdd}
                wtile = {"g": wg_t, "u": wu_t}
                conv_tok = {}
                slot_last_mm = {}
                chainF = {0: wstg_last[0], 1: wstg_last[1]}
                csemF = ("IN", "I4")

                dma_toks = {}

                def emit_stage_dma(hb, m):
                    toks = []
                    for h in range(2):
                        if chainF[h]:
                            p.wait("sync", "SC", chainF[h])
                        half = wstg[:, h * 1024:(h + 1) * 1024]
                        if m == "d":
                            srcap = wsrc[m][hb * HB + h * 128:hb * HB + (h + 1) * 128, :]
                            dstap = half
                        else:
                            srcap = wsrc[m][:, hb * HB + h * 128:hb * HB + (h + 1) * 128]\
                                .rearrange("(a p) c -> p a c", p=128)
                            dstap = half.rearrange("p (a c) -> p a c", a=8)
                        toks.append(p.op("sync", "dma_start", out=dstap, in_=srcap,
                                         inc=csemF[h], inck=16))
                        chainF[h] = 0
                    dma_toks[(hb, m)] = toks

                def emit_stage_conv(hb, m):
                    slot = hb % 2
                    toks = dma_toks[(hb, m)]
                    key = (m, slot)
                    tok_c = None
                    for h in range(2):
                        p.wait("scalar", csemF[h], toks[h])
                        if key in slot_last_mm:
                            p.wait("scalar", "MM", slot_last_mm[key])
                        half = wstg[:, h * 1024:(h + 1) * 1024]
                        if m == "d":
                            tok_c = p.op("scalar", "copy",
                                         out=wd_t[:, slot * 2048 + h * 1024:
                                                  slot * 2048 + (h + 1) * 1024],
                                         in_=half, inc="SC")
                        else:
                            cdst = wtile[m][:, slot * 2048:(slot + 1) * 2048].rearrange(
                                "p (k c) -> p k c", k=8)[:, :, h * 128:(h + 1) * 128]
                            tok_c = p.op("scalar", "copy", out=cdst,
                                         in_=half.rearrange("p (k c) -> p k c", k=8),
                                         inc="SC")
                        chainF[h] = tok_c
                    conv_tok[(hb, m)] = tok_c

                def emit_stage(hb, m):
                    emit_stage_dma(hb, m)
                    emit_stage_conv(hb, m)

                y_tok = {}

                def emit_ln_tile(nt):
                    tb = emit_ln_stats(p, pbk[6], pbk[7], pbk[6], pbk[7], nt,
                                       scr[2], scr[3], first_ln=(ci == 0), save=False,
                                       lnexp=True)
                    y_tok[nt] = emit_ln_split_ffn(p, pbk[6], pbk[7], nt, y_hi, y_lo,
                                                  scr[2], scr[3], tb)

                emit_stage_dma(0, "g")
                emit_ln_tile(0)
                emit_stage_conv(0, "g")
                emit_stage(0, "u")
                emit_stage(0, "d")
                emit_stage(1, "g")
                emit_stage(1, "u")
                emit_stage(1, "d")

                gu_i = [0]
                d_i = [0]
                sil_rd = [0]
                hf_rd = [0]
                h_tok = {}
                h_slot_rd = {}
                stt_tok = {}
                bank_mm = {}

                def emit_gu2(hb, nt, hslot):
                    slot = hb % 2
                    sil, hf = scr[0], scr[1]
                    for hl in range(2):
                        bg = pbk[gu_i[0] % 4]; gu_i[0] += 1
                        bu = pbk[gu_i[0] % 4]; gu_i[0] += 1
                        for m, bank in (("g", bg), ("u", bu)):
                            wait_bank(p, bank[0])
                            p.wait("tensor", "SC", conv_tok[(hb, m)])
                            p.wait("tensor", "VE", y_tok[nt])
                            n = 0
                            for ysrc in (y_hi, y_lo):
                                for kc in range(KC):
                                    tok = p.op(
                                        "tensor", "matmul", bank[1][:, :],
                                        wtile[m][:, slot * 2048 + kc * 256 + hl * 128:
                                                 slot * 2048 + kc * 256 + (hl + 1) * 128],
                                        ysrc[:, kc * T + nt * 512:kc * T + nt * 512 + 512],
                                        start=(n == 0), stop=(n == 15),
                                        inc=("MM" if n == 15 else None))
                                    n += 1
                            slot_last_mm[(m, slot)] = tok
                            bank_mm[bank[0]] = tok
                        hc0 = hslot * 1024 + hl * 512
                        p.wait("scalar", "MM", bank_mm[bg[0]])
                        p.wait("scalar", "VE", sil_rd[0])
                        if hslot in h_slot_rd:
                            p.wait("scalar", "MM", h_slot_rd[hslot])
                        tok_sil = p.op("scalar", "activation", out=sil[:, :], in_=bg[1][:, :],
                                       func=AF.Silu, scale=float(S_FFN), inc="SC")
                        bank_free[bg[0]] = ("SC", tok_sil)
                        p.wait("vector", "SC", tok_sil)
                        p.wait("vector", "MM", bank_mm[bu[0]])
                        if hslot in h_slot_rd:
                            p.wait("vector", "MM", h_slot_rd[hslot])
                        p.wait("vector", "VE", hf_rd[0])
                        tok_hf = p.op("vector", "tensor_mul", hf[:, :], sil[:, :], bu[1][:, :],
                                      inc="VE")
                        sil_rd[0] = tok_hf
                        bank_free[bu[0]] = ("VE", tok_hf)
                        p.wait("scalar", "VE", tok_hf)
                        tok_hi = p.op("scalar", "copy", out=h_hi[:, hc0:hc0 + 512],
                                      in_=hf[:, :], inc="SC")
                        p.wait("vector", "SC", tok_hi)
                        h_tok[(hb, nt)] = p.op("vector", "tensor_sub", h_lo[:, hc0:hc0 + 512],
                                               hf[:, :], h_hi[:, hc0:hc0 + 512], inc="VE")
                        hf_rd[0] = h_tok[(hb, nt)]

                def emit_down(hb, nt, hslot):
                    dslot = hb % 2
                    last_mm = None
                    for oc in range(8):
                        bank = pbk[4 + d_i[0] % 2]; d_i[0] += 1
                        wait_bank(p, bank[0])
                        p.wait("tensor", "SC", conv_tok[(hb, "d")])
                        p.wait("tensor", "VE", h_tok[(hb, nt)])
                        n = 0
                        for hsrc in (h_hi, h_lo):
                            for hl in range(2):
                                tok = p.op(
                                    "tensor", "matmul", bank[1][:, :],
                                    wd_t[:, dslot * 2048 + hl * 1024 + oc * 128:
                                         dslot * 2048 + hl * 1024 + (oc + 1) * 128],
                                    hsrc[:, hslot * 1024 + hl * 512:hslot * 1024 + (hl + 1) * 512],
                                    start=(n == 0), stop=(n == 3),
                                    inc=("MM" if n == 3 else None))
                                n += 1
                        last_mm = tok
                        slot_last_mm[("d", dslot)] = tok
                        xsl = xs[:, oc * T + nt * 512:oc * T + nt * 512 + 512]
                        p.wait("vector", "MM", tok)
                        if (oc, nt) in stt_tok:
                            p.wait("vector", "VE", stt_tok[(oc, nt)])
                        stt_tok[(oc, nt)] = p.op(
                            "vector", "scalar_tensor_tensor", out=xsl, in0=bank[1][:, :],
                            scalar=float(C_FFN), in1=xsl, op0=ALU.mult, op1=ALU.add, inc="VE")
                        xs_last[(oc, nt)] = stt_tok[(oc, nt)]
                        bank_free[bank[0]] = ("VE", stt_tok[(oc, nt)])
                    h_slot_rd[hslot] = last_mm

                pend = None
                items = []
                for nt in range(NT):
                    items.append((0, nt))
                    items.append((1, nt))
                for hb in range(2, NHB):
                    for nt in range(NT):
                        items.append((hb, nt))
                for idx, (hb, nt) in enumerate(items):
                    if idx == 1:
                        emit_stage_dma(2, "g")
                    if idx == 7:
                        emit_stage_conv(2, "g")
                        emit_stage_dma(2, "u")
                    if hb == 2 and nt == 0:
                        emit_stage_conv(2, "u")
                        emit_stage_dma(2, "d")
                    if hb >= 2 and nt == 1 and hb + 1 < NHB:
                        emit_stage_conv(hb + 1, "g")
                        emit_stage_dma(hb + 1, "u")
                    if hb >= 2 and nt == 2 and hb + 1 < NHB:
                        emit_stage_conv(hb + 1, "u")
                        emit_stage_dma(hb + 1, "d")
                    emit_gu2(hb, nt, idx % 2)
                    if hb == 0 and nt + 1 < NT:
                        emit_ln_tile(nt + 1)
                    if pend is not None:
                        emit_down(*pend)
                    pend = (hb, nt, idx % 2)
                    if hb >= 2 and nt == 0:
                        emit_stage_conv(hb, "d")
                        if hb + 1 < NHB:
                            emit_stage_dma(hb + 1, "g")
                emit_down(*pend)

                wstg_last[0] = chainF[0]
                wstg_last[1] = chainF[1]
                if out_dma:
                    for nt in range(NT):
                        for oc in range(8):
                            p.wait("gpsimd", "VE", stt_tok[(oc, nt)])
                            p.op("gpsimd", "dma_start",
                                 out=yd[oc * 128:(oc + 1) * 128, nt * 512:(nt + 1) * 512],
                                 in_=xs[:, oc * T + nt * 512:oc * T + nt * 512 + 512],
                                 inc="GP", inck=16)
                    p.wait("gpsimd", "GP", cnt["GP"])

                barrier()

        pf1 = P
        phase_entry(pf1)
        emit_ffn2(pf1, 0, out_dma=False)

        pa = P
        p = pa
        phase_entry(pa, tensor_too=False)
        with ExitStack() as fs:
            wk_hi = fs.enter_context(nc.sbuf_tensor("wk_hi", [128, 8192], BF16))
            wk_lo = fs.enter_context(nc.sbuf_tensor("wk_lo", [128, 8192], BF16))
            wv_hi = fs.enter_context(nc.sbuf_tensor("wv_hi", [128, 8192], BF16))
            wv_lo = fs.enter_context(nc.sbuf_tensor("wv_lo", [128, 8192], BF16))
            y2h = fs.enter_context(nc.sbuf_tensor("y2h", [128, 2 * 2048], BF16))
            y2l = fs.enter_context(nc.sbuf_tensor("y2l", [128, 2 * 2048], BF16))
            kfT = fs.enter_context(nc.sbuf_tensor("kfT", [128, 2048], F32))
            vT = fs.enter_context(nc.sbuf_tensor("vT", [128, 2048], F32))
            scrA = [fs.enter_context(nc.sbuf_tensor(f"sA{i}", [128, 512], F32)) for i in range(4)]
            pbA = [(f"b{i}", fs.enter_context(nc.psum_tensor(f"ap{i}", [128, 512], F32)))
                   for i in range(8)]
            kv_b, ks_b = pbA[4], pbA[5]

            split_tok = {}
            chainA = [0, 0]
            hA = {}
            tok_bcA = {}
            tok_bcA[0] = emit_ln_stats(p, pbA[6], pbA[7], pbA[6], pbA[7], 0,
                                       scrA[2], scrA[3], first_ln=False, save=True,
                                       lnexp=True)
            stage_split2(p, wkd, wk_hi, wk_lo, split_tok, "k", chainA, bar["MM"])

            def emit_ln_split_pair(nt, off, w, mb, rb, t1, t2, tok_bc, war_mm, slot):
                c0 = nt * 512 + off
                s0 = slot * 2048
                p.wait("vector", "MM", tok_bc)
                if war_mm:
                    p.wait("vector", "MM", war_mm)
                    p.wait("scalar", "MM", war_mm)
                last = None
                t1d = {0: t1, 1: s5}
                for kc in range(KC):
                    xsl = xs[:, kc * T + c0:kc * T + c0 + w]
                    t1k = t1d[kc % 2]
                    p.wait("vector", "VE", xs_last.get((kc, nt), 0))
                    p.wait("vector", "VE", ln_war.get(f"t1rd{kc % 2}", 0))
                    ts_ = p.op("vector", "tensor_sub", t1k[:, 0:w], xsl, mb[1][:, off:off + w],
                               inc="VE")
                    p.wait("vector", "VE", ts_)
                    p.wait("vector", "SC", ln_war.get("t2_rd_sc", 0))
                    p.wait("vector", "VE", ln_war.get("t2_rd_ve", 0))
                    tf = p.op("vector", "tensor_mul", t2[:, 0:w], t1k[:, 0:w],
                              rb[1][:, off:off + w], inc="VE")
                    ln_war[f"t1rd{kc % 2}"] = tf
                    p.wait("scalar", "VE", tf)
                    th = p.op("scalar", "copy", out=y2h[:, s0 + kc * w:s0 + (kc + 1) * w],
                              in_=t2[:, 0:w], inc="SC")
                    ln_war["t2_rd_sc"] = th
                    p.wait("vector", "SC", th)
                    last = p.op("vector", "tensor_sub", y2l[:, s0 + kc * w:s0 + (kc + 1) * w],
                                t2[:, 0:w], y2h[:, s0 + kc * w:s0 + (kc + 1) * w], inc="VE")
                    ln_war["split_ve"] = last
                    ln_war["t2_rd_ve"] = last
                bank_free[mb[0]] = ("VE", last)
                bank_free[rb[0]] = ("VE", last)
                return last

            projA_i = [0]
            eluA = {"t1_rd": 0, "t2_rd": 0}
            kv_started = {0: False, 64: False}
            ks_started = [False]
            mm_proj_last = [bar["MM"], bar["MM"]]
            halves = [(nt, h2) for nt in range(NT) for h2 in range(2)]
            kvacc_tok = [bar["MM"]]

            def emit_projA_mm(key, hi, lo, y2tok, slot):
                banks = []
                s0 = slot * 2048
                for tp in range(2):
                    for dh in range(2):
                        bank = pbA[projA_i[0] % 4]; projA_i[0] += 1
                        wait_bank(p, bank[0])
                        p.wait("tensor", "VE", y2tok)
                        p.wait("tensor", "SC", split_tok[(key, 8 * dh + 7)][0])
                        p.wait("tensor", "VE", split_tok[(key, 8 * dh + 7)][1])
                        n = 0
                        for ysrc, wsrc_ in ((y2h, hi), (y2l, hi), (y2h, lo)):
                            for kc in range(KC):
                                tok = p.op(
                                    "tensor", "matmul", bank[1][:, :],
                                    ysrc[:, s0 + kc * 256 + tp * 128:
                                         s0 + kc * 256 + (tp + 1) * 128],
                                    wsrc_[:, kc * 1024 + dh * 512:kc * 1024 + (dh + 1) * 512],
                                    start=(n == 0), stop=(n == 23),
                                    inc=("MM" if n == 23 else None))
                                n += 1
                        banks.append((tp, dh, bank, tok))
                mm_proj_last[slot] = banks[-1][3]
                return banks

            def emit_projA_evac(key, banks):
                last = None
                for (tp, dh, bank, tok) in banks:
                    dstc = tp * 1024 + dh * 512
                    dst = kfT if key == "k" else vT
                    p.wait("vector", "MM", tok)
                    p.wait("vector", "MM", kvacc_tok[0])
                    if key == "k":
                        t1, t2 = scrA[0], scrA[1]
                        p.wait("vector", "SC", eluA["t1_rd"])
                        tok_min = p.op("vector", "tensor_scalar_min", t1[:, :], bank[1][:, :],
                                       0.0, inc="VE")
                        p.wait("scalar", "VE", tok_min)
                        p.wait("scalar", "VE", eluA["t2_rd"])
                        eluA["t1_rd"] = p.op("scalar", "activation", out=t2[:, :], in_=t1[:, :],
                                             func=AF.Exp, inc="SC")
                        p.wait("vector", "SC", eluA["t1_rd"])
                        last = p.op("vector", "scalar_tensor_tensor", out=dst[:, dstc:dstc + 512],
                                    in0=bank[1][:, :], scalar=0.0, in1=t2[:, :],
                                    op0=ALU.max, op1=ALU.add, inc="VE")
                        eluA["t2_rd"] = last
                    else:
                        last = p.op("vector", "tensor_copy", out=dst[:, dstc:dstc + 512],
                                    in_=bank[1][:, :], inc="VE")
                    bank_free[bank[0]] = ("VE", last)
                return last

            def emit_kvacc(ev_tok, is_last):
                p.wait("tensor", "VE", ev_tok)
                tok = None
                for tp in range(2):
                    for hp in range(8):
                        for par in (0, 64):
                            h = 2 * hp + (1 if par else 0)
                            stt_ = not kv_started[par]
                            kv_started[par] = True
                            p.op("tensor", "matmul",
                                 kv_b[1][par:par + 64, hp * 64:(hp + 1) * 64],
                                 kfT[:, tp * 1024 + h * 64:tp * 1024 + (h + 1) * 64],
                                 vT[:, tp * 1024 + h * 64:tp * 1024 + (h + 1) * 64],
                                 start=stt_, stop=(is_last and tp == 1 and hp == 7),
                                 tile_position=(0, par),
                                 skip_group_check=True)
                    for hp in range(8):
                        stt_ = not ks_started[0]
                        ks_started[0] = True
                        tok = p.op("tensor", "matmul", ks_b[1][:, hp:hp + 1],
                                   kfT[:, tp * 1024 + hp * 128:tp * 1024 + (hp + 1) * 128],
                                   ones_col,
                                   start=stt_, stop=(is_last and tp == 1 and hp == 7),
                                   skip_group_check=True,
                                   inc=("MM" if (tp == 1 and hp == 7) else None))
                kvacc_tok[0] = tok
                return tok

            pendA = None
            for hi_, (nt, h2) in enumerate(halves):
                slot = hi_ % 2
                if pendA is not None and h2 == 1:
                    emit_kvacc(pendA, is_last=False)
                    pendA = None
                y2tok = emit_ln_split_pair(nt, h2 * 256, 256, pbA[6], pbA[7],
                                           scrA[2], scrA[3], tok_bcA[nt],
                                           mm_proj_last[slot], slot)
                bk = emit_projA_mm("k", wk_hi, wk_lo, y2tok, slot)
                if hi_ == 7:
                    hA["kproj_mm"] = mm_proj_last[slot]
                if hi_ == 0:
                    stage_split2(p, wvd, wv_hi, wv_lo, split_tok, "v", chainA, bar["MM"])
                if h2 == 1 and nt + 1 < NT:
                    tok_bcA[nt + 1] = emit_ln_stats(p, pbA[6], pbA[7], pbA[6], pbA[7],
                                                    nt + 1, scrA[2], scrA[3],
                                                    first_ln=False, save=True,
                                                    lnexp=True)
                if pendA is not None:
                    emit_kvacc(pendA, is_last=False)
                    pendA = None
                evk = emit_projA_evac("k", bk)
                bv = emit_projA_mm("v", wv_hi, wv_lo, y2tok, slot)
                if hi_ == 7:
                    hA["vproj_mm"] = mm_proj_last[slot]
                evv = emit_projA_evac("v", bv)
                pendA = evv
            split_tokB = {}
            stage_split2(p, wqd, wk_hi, wk_lo, split_tokB, "q", chainA,
                         hA["kproj_mm"], 0, 8)
            stB = {"ty1_rd": m2r2_tok[3], "ty2_sc": 0, "ty2_ve": m2r2_tok[3]}
            y2b0_tok = emit_y2b_core(p, pbA[6], pbA[7], 0, 0,
                                     y2h[:, 0:2048], y2h[:, 2048:4096],
                                     mm_proj_last[0], mm_proj_last[1], stB)
            stage_split2(p, wqd, wk_hi, wk_lo, split_tokB, "q", chainA,
                         hA["kproj_mm"], 8, 16)
            emit_kvacc(pendA, is_last=True)

            p.wait("vector", "MM", cnt["MM"])
            p.op("vector", "tensor_copy", out=scrA[0][:, :], in_=kv_b[1][:, :], inc="VE")
            tok_ccs = p.op("vector", "tensor_copy", out=scrA[1][:, 0:8], in_=ks_b[1][:, 0:8],
                           inc="VE")
            p.wait("gpsimd", "VE", tok_ccs)
            p.op("gpsimd", "dma_start", out=cc_in.ap()[:, 0:512], in_=scrA[0][:, :],
                 inc="GP", inck=16)
            tok_d = p.op("gpsimd", "dma_start", out=cc_in.ap()[:, 512:520],
                         in_=scrA[1][:, 0:8], inc="GP", inck=16)
            p.wait("gpsimd", "GP", tok_d)
            p.op("gpsimd", "collective_compute", "AllReduce", ALU.add,
                 replica_groups=[[0, 1], [2, 3], [4, 5], [6, 7]],
                 ins=[cc_in.ap().opt()], outs=[cc_out.ap().opt()], inc="CC")
            p.wait("gpsimd", "CC", cnt["CC"])
            p.op("gpsimd", "dma_start", out=kvks[:, :], in_=cc_out.ap(), inc="GP", inck=16)

            barrier()
        ar_tok = cnt["GP"]
        a_ve_end = tok_ccs
        a_sc_end = eluA["t1_rd"]
        y2b0_hi = y2h[:, 0:2048]
        y2b0_lo = y2h[:, 2048:4096]
        wqs_hi, wqs_lo = wk_hi, wk_lo

        pb_ = P
        p = pb_
        with ExitStack() as fs:
            wq_hi = fs.enter_context(nc.sbuf_tensor("wq_hi", [128, 8192], BF16))
            wq_lo = fs.enter_context(nc.sbuf_tensor("wq_lo", [128, 8192], BF16))
            wo_hi = fs.enter_context(nc.sbuf_tensor("wo_hi", [128, 8192], BF16))
            wo_lo = fs.enter_context(nc.sbuf_tensor("wo_lo", [128, 8192], BF16))
            y2h = fs.enter_context(nc.sbuf_tensor("y2hB", [128, 2048], BF16))
            y2l = fs.enter_context(nc.sbuf_tensor("y2lB", [128, 2048], BF16))
            qf = fs.enter_context(nc.sbuf_tensor("qf", [128, 2 * 2048], F32))
            at_hi = fs.enter_context(nc.sbuf_tensor("at_hi", [128, 2048], BF16))
            at_lo = fs.enter_context(nc.sbuf_tensor("at_lo", [128, 2048], BF16))
            ksB = fs.enter_context(nc.sbuf_tensor("ksB", [128, 8 * 128], F32))

            tq1 = rows[:, 512:768]
            tq2 = rows[:, 768:1024]
            ta1 = rows[:, 1024:1280]
            dne = rows[:, 1280:1536]
            scB = {"tq1_rd": a_sc_end, "tq2_rd": a_ve_end,
                   "ta1_rd": a_sc_end, "rcp_rd": a_ve_end}
            pbB = [(f"b{i}", fs.enter_context(nc.psum_tensor(f"bp{i}", [128, 512], F32)))
                   for i in range(8)]

            chainB = [chainA[0], chainA[1]]

            qp_i = [0]
            op_i = [0]
            mm_qproj_last = [bar["MM"]]
            mm_oproj_last = [bar["MM"]]
            mm_hp_last = [bar["MM"]]

            def emit_y2b(nt, h2):
                return emit_y2b_core(p, pbB[6], pbB[7], nt, h2, y2h, y2l,
                                     mm_qproj_last[0], mm_qproj_last[0], stB)

            ar_waited = [False]
            at_last = [a_ve_end]
            ksb_tok = [0]

            def emit_hp_step(hp, qftok, slot):
                qf0 = slot * 2048
                dn, qk = pbB[4], pbB[5]
                if not ar_waited[0]:
                    p.wait("tensor", "GP", ar_tok)
                    p.wait("tensor", "VE", ksb_tok[0])
                    ar_waited[0] = True
                wait_bank(p, dn[0])
                p.wait("tensor", "VE", qftok)
                kb = ksB[:, hp * 128:(hp + 1) * 128]
                p.op("tensor", "matmul", dn[1][0:64, 0:256], kb[0:64, 0:64],
                     qf[0:64, qf0 + hp * 256:qf0 + (hp + 1) * 256], start=True, stop=True,
                     tile_position=(0, 0), skip_group_check=True)
                tok_dn = p.op("tensor", "matmul", dn[1][64:128, 0:256], kb[64:128, 64:128],
                              qf[64:128, qf0 + hp * 256:qf0 + (hp + 1) * 256],
                              start=True, stop=True,
                              tile_position=(64, 64), skip_group_check=True, inc="MM")
                wait_bank(p, qk[0])
                p.op("tensor", "matmul", qk[1][0:64, 0:256], kvks[0:64, hp * 64:(hp + 1) * 64],
                     qf[0:64, qf0 + hp * 256:qf0 + (hp + 1) * 256], start=True, stop=True,
                     tile_position=(0, 0), skip_group_check=True)
                tok_qk = p.op("tensor", "matmul", qk[1][64:128, 0:256],
                              kvks[64:128, hp * 64:(hp + 1) * 64],
                              qf[64:128, qf0 + hp * 256:qf0 + (hp + 1) * 256],
                              start=True, stop=True,
                              tile_position=(64, 64), skip_group_check=True, inc="MM")
                mm_hp_last[0] = tok_qk
                p.wait("scalar", "MM", tok_dn)
                p.wait("scalar", "SC", scB["ta1_rd"])
                tok_eps = p.op("scalar", "activation", out=dne, in_=dn[1][:, 0:256],
                               func=AF.Ln, bias=float(EPS_AT), scale=1.0, inc="SC")
                bank_free[dn[0]] = ("SC", tok_eps)
                p.wait("scalar", "VE", scB.get("rcp_rd", 0))
                p.wait("scalar", "SC", tok_eps)
                tok_rcp = p.op("scalar", "activation", out=dne, in_=dne,
                               func=AF.Exp, scale=-1.0, inc="SC")
                p.wait("vector", "MM", tok_qk)
                p.wait("vector", "SC", tok_rcp)
                p.wait("vector", "MM", mm_oproj_last[0])
                p.wait("vector", "SC", scB["ta1_rd"])
                p.wait("vector", "VE", at_last[0])
                tok_at = p.op("vector", "tensor_mul", ta1, qk[1][:, 0:256], dne,
                              inc="VE")
                scB["rcp_rd"] = tok_at
                bank_free[qk[0]] = ("VE", tok_at)
                p.wait("scalar", "VE", tok_at)
                p.wait("scalar", "MM", mm_oproj_last[0])
                scB["ta1_rd"] = p.op("scalar", "copy", out=at_hi[:, hp * 256:(hp + 1) * 256],
                                     in_=ta1, inc="SC")
                p.wait("vector", "SC", scB["ta1_rd"])
                at_last[0] = p.op("vector", "tensor_sub", at_lo[:, hp * 256:(hp + 1) * 256],
                                  ta1, at_hi[:, hp * 256:(hp + 1) * 256], inc="VE")
                return at_last[0]

            def emit_qproj(y2tok, slot, srcs, hp_cb=None):
                qf0 = slot * 2048
                p.wait("vector", "VE", a_ve_end)
                last = None
                for oc in range(8):
                    bank = pbB[qp_i[0] % 2]; qp_i[0] += 1
                    wait_bank(p, bank[0])
                    p.wait("tensor", "SC", split_tokB[("q", 2 * oc + 1)][0])
                    p.wait("tensor", "VE", split_tokB[("q", 2 * oc + 1)][1])
                    p.wait("tensor", "VE", y2tok)
                    n = 0
                    for ysrc, wsrc_ in ((srcs[0], wqs_hi), (srcs[1], wqs_hi),
                                        (srcs[0], wqs_lo)):
                        for kc in range(KC):
                            tok = p.op("tensor", "matmul", bank[1][:, 0:256],
                                       wsrc_[:, kc * 1024 + oc * 128:kc * 1024 + (oc + 1) * 128],
                                       ysrc[:, kc * 256:(kc + 1) * 256],
                                       start=(n == 0), stop=(n == 23),
                                       inc=("MM" if n == 23 else None))
                            n += 1
                    mm_qproj_last[0] = tok
                    if hp_cb is not None:
                        hp_cb(oc)
                    p.wait("vector", "MM", tok)
                    p.wait("vector", "MM", mm_hp_last[0])
                    p.wait("vector", "SC", scB["tq1_rd"])
                    tok_min = p.op("vector", "tensor_scalar_min", tq1, bank[1][:, 0:256],
                                   0.0, inc="VE")
                    p.wait("scalar", "VE", tok_min)
                    p.wait("scalar", "VE", scB["tq2_rd"])
                    scB["tq1_rd"] = p.op("scalar", "activation", out=tq2, in_=tq1,
                                         func=AF.Exp, inc="SC")
                    p.wait("vector", "SC", scB["tq1_rd"])
                    last = p.op("vector", "scalar_tensor_tensor",
                                out=qf[:, qf0 + oc * 256:qf0 + (oc + 1) * 256],
                                in0=bank[1][:, 0:256], scalar=0.0, in1=tq2,
                                op0=ALU.max, op1=ALU.add, inc="VE")
                    scB["tq2_rd"] = last
                    bank_free[bank[0]] = ("VE", last)
                return last

            def emit_oproj(nt, h2, attok):
                c0 = nt * 512 + h2 * 256
                p.wait("vector", "VE", a_ve_end)
                for oc in range(8):
                    bank = pbB[2 + op_i[0] % 2]; op_i[0] += 1
                    wait_bank(p, bank[0])
                    p.wait("tensor", "SC", split_tokB[("o", 2 * oc + 1)][0])
                    p.wait("tensor", "VE", split_tokB[("o", 2 * oc + 1)][1])
                    p.wait("tensor", "VE", attok)
                    n = 0
                    for asrc, wsrc_ in ((at_hi, wo_hi), (at_lo, wo_hi), (at_hi, wo_lo)):
                        for kc in range(KC):
                            tok = p.op("tensor", "matmul", bank[1][:, 0:256],
                                       wsrc_[:, kc * 1024 + oc * 128:kc * 1024 + (oc + 1) * 128],
                                       asrc[:, kc * 256:(kc + 1) * 256],
                                       start=(n == 0), stop=(n == 23),
                                       inc=("MM" if n == 23 else None))
                            n += 1
                    mm_oproj_last[0] = tok
                    xsl = xs[:, oc * T + c0:oc * T + c0 + 256]
                    p.wait("vector", "MM", tok)
                    p.wait("vector", "LD", tok_bo)
                    tk = p.op("vector", "scalar_tensor_tensor", out=xsl, in0=bank[1][:, 0:256],
                              scalar=bo_sb[:, oc:oc + 1], in1=xsl, op0=ALU.add, op1=ALU.add,
                              inc="VE")
                    xs_last[(oc, nt)] = tk
                    bank_free[bank[0]] = ("VE", tk)

            halvesB = [(nt, h2) for nt in range(NT) for h2 in range(2)]
            pend1 = None
            pend2 = None
            y2next = y2b0_tok
            for i, (nt, h2) in enumerate(halvesB):
                y2t = y2next
                slot = i % 2
                if i == 2:
                    p.wait("vector", "GP", ar_tok)
                    for _hp in range(8):
                        p.op("vector", "tensor_scalar",
                             ksB[:, _hp * 128:(_hp + 1) * 128], kvks[:, 0:128], 0.0,
                             kvks[:, 512 + _hp:513 + _hp],
                             op0=ALU.mult, op1=ALU.add, inc="VE")
                    ksb_tok[0] = cnt["VE"]
                ysrcs = (y2b0_hi, y2b0_lo) if i == 0 else (y2h, y2l)
                if pend2 is None:
                    qt = emit_qproj(y2t, slot, ysrcs)
                    if i == 0:
                        stage_split2(p, wod, wo_hi, wo_lo, split_tokB, "o", chainB,
                                     hA["vproj_mm"])
                else:
                    (pnt2, ph22, pq2, ps2) = pend2
                    qt = emit_qproj(y2t, slot, ysrcs,
                                    hp_cb=lambda g, pq=pq2, ps=ps2: emit_hp_step(g, pq, ps))
                if i + 1 < len(halvesB):
                    y2next = emit_y2b(*halvesB[i + 1])
                if pend2 is not None:
                    emit_oproj(pnt2, ph22, at_last[0])
                pend2 = pend1
                pend1 = (nt, h2, qt, slot)
            for (pnt2, ph22, pq2, ps2) in (pend2, pend1):
                for hp in range(8):
                    emit_hp_step(hp, pq2, ps2)
                emit_oproj(pnt2, ph22, at_last[0])

            barrier()

        pf2 = P
        phase_entry(pf2, tensor_too=False)
        emit_ffn2(pf2, 1, out_dma=True)

        P.run(nc)

    return nc


_NC_CACHE = []


def kernel(L0_attn_x, L0_attn_wq, L0_attn_wk, L0_attn_wv, L0_attn_wo, L0_attn_bo,
           L0_ffn_w_gate, L0_ffn_w_up, L0_ffn_w_down):
    x = np.asarray(L0_attn_x)
    wqT = np.ascontiguousarray(np.asarray(L0_attn_wq).T)
    wkT = np.ascontiguousarray(np.asarray(L0_attn_wk).T)
    wvT = np.ascontiguousarray(np.asarray(L0_attn_wv).T)
    woT = np.ascontiguousarray(np.asarray(L0_attn_wo).T)
    bo2 = np.ascontiguousarray(np.asarray(L0_attn_bo).reshape(8, 128).T)
    wgT = np.ascontiguousarray(np.asarray(L0_ffn_w_gate).T)
    wuT = np.ascontiguousarray(np.asarray(L0_ffn_w_up).T)
    wdT = np.ascontiguousarray(np.asarray(L0_ffn_w_down).T)

    cons = np.zeros((128, 130), np.float32)
    cons[:, 0] = 1.0
    cons[:, 1] = EPS_AT
    cons[0, 2:130] = 1.0

    in_maps = []
    for c in range(8):
        b, h = c // 2, c % 2
        xsh = np.ascontiguousarray(x[b, h * T:(h + 1) * T, :].T)
        in_maps.append({"x": xsh, "wq": wqT, "wk": wkT, "wv": wvT, "wo": woT,
                        "bo": bo2, "wg": wgT, "wu": wuT, "wd": wdT, "cons": cons})

    if not _NC_CACHE:
        _NC_CACHE.append(build_nc())
    nc = _NC_CACHE[0]
    res = run_bass_kernel_spmd(nc, in_maps, core_ids=list(range(8)))

    out = np.empty((4, 4096, 1024), np.float32)
    for c in range(8):
        b, h = c // 2, c % 2
        out[b, h * T:(h + 1) * T, :] = res.results[c]["y"].T
    return out


# version 19 (final)
# speedup vs baseline: 1.1234x; 1.0158x over previous
import numpy as np
from contextlib import ExitStack

import concourse.bass as bass
import concourse.mybir as mybir
from concourse.bass_utils import run_bass_kernel_spmd

F32 = mybir.dt.float32
BF16 = mybir.dt.bfloat16
I32 = mybir.dt.int32
AF = mybir.ActivationFunctionType
ALU = mybir.AluOpType

D = 1024
T = 2048
HID = 3072
KC = 8
NT = 4
HB = 256
NHB = HID // HB
S_FFN = 1.0 / 127.0
C_FFN = 0.5 * S_FFN * S_FFN
EPS_LN = 1e-5
EPS_AT = 1e-6

ENGS = ("sync", "tensor", "vector", "scalar", "gpsimd")


class Plan:

    def __init__(self, sems, cnt):
        self.q = {e: [] for e in ENGS}
        self.sems = sems
        self.cnt = cnt

    def wait(self, eng, sem, val):
        if val is None or val <= 0:
            return
        assert val <= self.cnt[sem], f"wait {sem}>={val} but only {self.cnt[sem]} emitted"
        h = self.sems[sem]
        self.q[eng].append(lambda e, h=h, v=val: e.wait_ge(h, v))

    def op(self, eng, name, *args, inc=None, inck=1, **kw):
        h = self.sems[inc] if inc else None

        def f(e, name=name, args=args, kw=kw, h=h, inck=inck):
            ins = getattr(e, name)(*args, **kw)
            if h is not None:
                ins.then_inc(h, inck)
        self.q[eng].append(f)
        if inc:
            self.cnt[inc] += inck
            return self.cnt[inc]
        return None

    def run(self, nc):
        with nc.Block(no_gpsimd_drain=True) as b:
            for e in ENGS:
                if self.q[e]:
                    getattr(b, e)(lambda eng, q=tuple(self.q[e]): [f(eng) for f in q])


def build_nc():
    nc = bass.Bass("TRN2", target_bir_lowering=False)

    xd = nc.dram_tensor("x", [D, T], F32, kind="ExternalInput").ap()
    wqd = nc.dram_tensor("wq", [D, D], F32, kind="ExternalInput").ap()
    wkd = nc.dram_tensor("wk", [D, D], F32, kind="ExternalInput").ap()
    wvd = nc.dram_tensor("wv", [D, D], F32, kind="ExternalInput").ap()
    wod = nc.dram_tensor("wo", [D, D], F32, kind="ExternalInput").ap()
    bod = nc.dram_tensor("bo", [128, 8], F32, kind="ExternalInput").ap()
    wgd = nc.dram_tensor("wg", [D, HID], I32, kind="ExternalInput").ap()
    wud = nc.dram_tensor("wu", [D, HID], I32, kind="ExternalInput").ap()
    wdd = nc.dram_tensor("wd", [HID, D], I32, kind="ExternalInput").ap()
    cnd = nc.dram_tensor("cons", [128, 130], F32, kind="ExternalInput").ap()
    yd = nc.dram_tensor("y", [D, T], F32, kind="ExternalOutput").ap()

    cc_in = nc.dram_tensor("cc_in", [128, 520], F32)
    cc_out = nc.dram_tensor("cc_out", [128, 520], F32)

    st = ExitStack()
    with st:
        xs = st.enter_context(nc.sbuf_tensor("xs", [128, KC * T], F32))
        wstg = st.enter_context(nc.sbuf_tensor("wstg", [128, 2048], I32))
        fstg = st.enter_context(nc.sbuf_tensor("fstg", [128, 1024], F32))
        cons = st.enter_context(nc.sbuf_tensor("cons_sb", [128, 130], F32))
        bo_sb = st.enter_context(nc.sbuf_tensor("bo_sb", [128, 8], F32))
        m2r2 = st.enter_context(nc.sbuf_tensor("m2r2", [1, 2 * T], F32))
        rows = st.enter_context(nc.sbuf_tensor("rows", [128, 2048], F32))
        kvks = st.enter_context(nc.sbuf_tensor("kvks", [128, 520], F32))

        ones_col = cons[:, 0:1]
        ones_row = cons[0:1, 2:130]
        nc.const_aps.register(F32, float(EPS_AT), cons[:, 1:2])

        sem_names = ("IN", "I2", "I3", "I4", "LD", "GP", "CC", "MM", "VE", "SC")
        sems = {n: st.enter_context(nc.semaphore(n)) for n in sem_names}
        cnt = {n: 0 for n in sem_names}

        bank_free = {}
        wstg_last = [0, 0]
        xs_last = {}
        m2r2_tok = {}
        ln_war = {"split_ve": 0}

        def wait_bank(p, bname):
            if bname in bank_free:
                s, t = bank_free[bname]
                p.wait("tensor", s, t)

        bar = {n: 0 for n in sem_names}

        def barrier():
            bar.update({n: cnt[n] for n in sem_names})

        def phase_entry(p, tensor_too=True):
            engs = ("tensor", "vector", "scalar") if tensor_too else ("vector", "scalar")
            for e in engs:
                for s in ("MM", "VE", "SC"):
                    p.wait(e, s, bar[s])

        P = Plan(sems, cnt)
        p0 = P
        tok_x = []
        for kc in range(KC):
            tok_x.append(p0.op("sync", "dma_start", out=xs[:, kc * T:(kc + 1) * T],
                               in_=xd[kc * 128:(kc + 1) * 128, :], inc="LD", inck=16))
        tok_cons = p0.op("sync", "dma_start", out=cons[:, :], in_=cnd, inc="LD", inck=16)
        tok_bo = p0.op("sync", "dma_start", out=bo_sb[:, :], in_=bod, inc="LD", inck=16)
        barrier()

        m_r = rows[0:1, 0:512]
        a_r = rows[0:1, 512:1024]
        b_r = rows[0:1, 1024:1536]
        s5 = rows[:, 1536:2048]

        def emit_ln_stats(p, s1, s2, mb, rb, nt, sq0, sq1, first_ln, save, lnexp=False):
            c0 = nt * 512
            wait_bank(p, s1[0]); wait_bank(p, s2[0])
            p.wait("scalar", "VE", ln_war["split_ve"])
            for kc in range(KC):
                p.wait("scalar", "VE", xs_last.get((kc, nt), 0))
                p.wait("tensor", "VE", xs_last.get((kc, nt), 0))
            tok_sq = []
            for kc in range(KC):
                sq = sq0 if kc % 2 == 0 else sq1
                if first_ln and kc == 0:
                    p.wait("scalar", "LD", tok_bo)
                if kc >= 2:
                    p.wait("scalar", "MM", tok_sq[kc - 2])
                p.op("scalar", "square", out=sq[:, :], in_=xs[:, kc * T + c0:kc * T + c0 + 512],
                     inc="SC")
                p.wait("tensor", "SC", cnt["SC"])
                if first_ln and kc == 0:
                    p.wait("tensor", "LD", tok_bo)
                p.op("tensor", "matmul", s1[1][0:1, :], ones_col,
                     xs[:, kc * T + c0:kc * T + c0 + 512],
                     start=(kc == 0), stop=(kc == KC - 1))
                tok_sq.append(p.op("tensor", "matmul", s2[1][0:1, :], ones_col, sq[:, :],
                                   start=(kc == 0), stop=(kc == KC - 1), inc="MM"))
            p.wait("vector", "MM", tok_sq[-1])
            tm = p.op("vector", "tensor_scalar_mul", m_r, s1[1][0:1, :], 1.0 / D, inc="VE")
            ta = p.op("vector", "tensor_scalar_mul", a_r, s2[1][0:1, :], 1.0 / D, inc="VE")
            bank_free[s1[0]] = ("VE", cnt["VE"])
            bank_free[s2[0]] = ("VE", cnt["VE"])
            p.wait("vector", "VE", tm)
            tb = p.op("vector", "tensor_mul", b_r, m_r, m_r, inc="VE")
            p.wait("vector", "VE", tb)
            tv = p.op("vector", "tensor_sub", a_r, a_r, b_r, inc="VE")
            p.wait("vector", "VE", tv)
            tok_v = p.op("vector", "tensor_scalar_add", b_r, a_r, EPS_LN, inc="VE")
            p.wait("scalar", "VE", tok_v)
            p.wait("scalar", "MM", ln_war.get("bc_mm", 0))
            if lnexp:
                p.op("scalar", "activation", out=a_r, in_=b_r, func=AF.Ln, inc="SC")
                p.wait("scalar", "SC", cnt["SC"])
                tok_q = p.op("scalar", "activation", out=b_r, in_=a_r, func=AF.Exp,
                             scale=-0.5, inc="SC")
                tok_rv = None
            else:
                tok_q = p.op("scalar", "sqrt", out=a_r, in_=b_r, inc="SC")
                p.wait("vector", "SC", tok_q)
                p.wait("vector", "MM", ln_war.get("bc_mm", 0))
                tok_rv = p.op("vector", "reciprocal", out=b_r, in_=a_r, inc="VE")
            ln_war["rows_sc"] = tok_q
            if save:
                p.op("vector", "tensor_copy", out=m2r2[0:1, c0:c0 + 512], in_=m_r, inc="VE")
                if tok_rv is None:
                    p.wait("vector", "SC", tok_q)
                else:
                    p.wait("vector", "VE", tok_rv)
                tok_rv = m2r2_tok[nt] = p.op("vector", "tensor_copy",
                                             out=m2r2[0:1, T + c0:T + c0 + 512], in_=b_r,
                                             inc="VE")
            wait_bank(p, mb[0]); wait_bank(p, rb[0])
            if tok_rv is not None:
                p.wait("tensor", "VE", tok_rv)
            else:
                p.wait("tensor", "SC", tok_q)
                p.wait("tensor", "VE", tm)
            p.wait("tensor", "LD", tok_bo)
            p.op("tensor", "matmul", mb[1][:, :], ones_row, m_r, start=True, stop=True)
            ln_war["bc_mm"] = p.op("tensor", "matmul", rb[1][:, :], ones_row, b_r,
                                   start=True, stop=True, inc="MM")
            return ln_war["bc_mm"]

        def emit_ln_split(p, mb, rb, nt, off, w, y_hi, y_lo, ycol, t1, t2, tok_bc, war_mm):
            c0 = nt * 512 + off
            p.wait("vector", "MM", tok_bc)
            if war_mm:
                p.wait("vector", "MM", war_mm)
                p.wait("scalar", "MM", war_mm)
            last = None
            for kc in range(KC):
                xsl = xs[:, kc * T + c0:kc * T + c0 + w]
                p.wait("vector", "SC", cnt["SC"])
                ts_ = p.op("vector", "tensor_sub", t1[:, 0:w], xsl, mb[1][:, off:off + w], inc="VE")
                p.wait("vector", "VE", ts_)
                tok_t2 = p.op("vector", "tensor_mul", t2[:, 0:w], t1[:, 0:w],
                              rb[1][:, off:off + w], inc="VE")
                p.wait("scalar", "VE", tok_t2)
                p.op("scalar", "copy", out=y_hi[:, ycol + kc * w:ycol + (kc + 1) * w],
                     in_=t2[:, 0:w], inc="SC")
                p.wait("vector", "SC", cnt["SC"])
                last = p.op("vector", "tensor_sub", y_lo[:, ycol + kc * w:ycol + (kc + 1) * w],
                            t2[:, 0:w], y_hi[:, ycol + kc * w:ycol + (kc + 1) * w], inc="VE")
            bank_free[mb[0]] = ("VE", last)
            bank_free[rb[0]] = ("VE", last)
            return last

        def emit_ln_split_ffn(p, mb, rb, nt, y_hi, y_lo, t1, t2, tok_bc):
            c0 = nt * 512
            p.wait("vector", "MM", tok_bc)
            last = None
            t1d = {0: t1, 1: s5}
            for kc in range(KC):
                xsl = xs[:, kc * T + c0:kc * T + c0 + 512]
                t1k = t1d[kc % 2]
                p.wait("vector", "VE", xs_last.get((kc, nt), 0))
                p.wait("vector", "VE", ln_war.get(f"t1rd{kc % 2}", 0))
                ts_ = p.op("vector", "tensor_sub", t1k[:, :], xsl, mb[1][:, :], inc="VE")
                p.wait("vector", "VE", ts_)
                p.op("vector", "tensor_mul", y_hi[:, kc * T + c0:kc * T + c0 + 512],
                     t1k[:, :], rb[1][:, :], inc="VE")
                p.wait("vector", "VE", ln_war.get("t2_rd_ve", 0))
                tf = p.op("vector", "tensor_mul", t2[:, :], t1k[:, :], rb[1][:, :], inc="VE")
                ln_war[f"t1rd{kc % 2}"] = tf
                p.wait("vector", "VE", tf)
                last = p.op("vector", "tensor_sub", y_lo[:, kc * T + c0:kc * T + c0 + 512],
                            t2[:, :], y_hi[:, kc * T + c0:kc * T + c0 + 512], inc="VE")
                ln_war["split_ve"] = last
                ln_war["t2_rd_ve"] = last
            bank_free[mb[0]] = ("VE", last)
            bank_free[rb[0]] = ("VE", last)
            return last

        ty1 = rows[:, 0:256]
        ty2 = rows[:, 256:512]

        def emit_y2b_core(p, mb, rb, nt, h2, hi_ap, lo_ap, war_hi, war_lo, st):
            c0 = nt * 512 + h2 * 256
            wait_bank(p, mb[0]); wait_bank(p, rb[0])
            p.wait("tensor", "VE", m2r2_tok.get(nt, 0))
            p.op("tensor", "matmul", mb[1][:, 0:256], ones_row, m2r2[0:1, c0:c0 + 256],
                 start=True, stop=True)
            tok_bc = p.op("tensor", "matmul", rb[1][:, 0:256], ones_row,
                          m2r2[0:1, T + c0:T + c0 + 256], start=True, stop=True, inc="MM")
            p.wait("vector", "MM", tok_bc)
            p.wait("vector", "MM", ln_war.get("bc_mm", 0))
            p.wait("vector", "MM", war_hi)
            p.wait("vector", "MM", war_lo)
            p.wait("vector", "VE", ln_war["split_ve"])
            last = None
            for kc in range(KC):
                xsl = xs[:, kc * T + c0:kc * T + c0 + 256]
                p.wait("vector", "VE", xs_last.get((kc, nt), 0))
                p.wait("vector", "VE", st["ty1_rd"])
                ts_ = p.op("vector", "tensor_sub", ty1, xsl, mb[1][:, 0:256], inc="VE")
                p.wait("vector", "VE", ts_)
                p.op("vector", "tensor_mul", hi_ap[:, kc * 256:(kc + 1) * 256],
                     ty1, rb[1][:, 0:256], inc="VE")
                p.wait("vector", "VE", st["ty2_ve"])
                tok_t2 = p.op("vector", "tensor_mul", ty2, ty1, rb[1][:, 0:256], inc="VE")
                st["ty1_rd"] = tok_t2
                p.wait("vector", "VE", tok_t2)
                last = p.op("vector", "tensor_sub", lo_ap[:, kc * 256:(kc + 1) * 256],
                            ty2, hi_ap[:, kc * 256:(kc + 1) * 256], inc="VE")
                st["ty2_ve"] = last
            bank_free[mb[0]] = ("VE", last)
            bank_free[rb[0]] = ("VE", last)
            return last

        def stage_split2(p, mat_d, hi, lo, split_tok, key, chain_state, war_mm,
                         s_lo=0, s_hi=16):
            csem = ("I2", "I3")
            for s in range(s_lo, s_hi):
                c = s % 2
                half = fstg[:, c * 512:(c + 1) * 512]
                if chain_state[c]:
                    p.wait("sync", "VE", chain_state[c])
                tok_d = p.op("sync", "dma_start",
                             out=half.rearrange("p (a c) -> p a c", a=8),
                             in_=mat_d[:, s * 64:(s + 1) * 64].rearrange(
                                 "(a p) c -> p a c", p=128),
                             inc=csem[c], inck=16)
                hi3 = hi[:, :].rearrange("p (a c) -> p a c", a=8)[:, :, s * 64:(s + 1) * 64]
                lo3 = lo[:, :].rearrange("p (a c) -> p a c", a=8)[:, :, s * 64:(s + 1) * 64]
                f3 = half.rearrange("p (a c) -> p a c", a=8)
                p.wait("scalar", csem[c], tok_d)
                p.wait("scalar", "MM", war_mm)
                tok_h = p.op("scalar", "copy", out=hi3, in_=f3, inc="SC")
                p.wait("vector", "SC", tok_h)
                p.wait("vector", "MM", war_mm)
                tok_l = p.op("vector", "tensor_sub", lo3, f3, hi3, inc="VE")
                split_tok[(key, s)] = (tok_h, tok_l)
                chain_state[c] = tok_l

        def emit_ffn2(p, ci, out_dma):
            with ExitStack() as fs:
                y_hi = fs.enter_context(nc.sbuf_tensor(f"yh{ci}", [128, KC * T], BF16))
                y_lo = fs.enter_context(nc.sbuf_tensor(f"yl{ci}", [128, KC * T], BF16))
                wg_t = fs.enter_context(nc.sbuf_tensor(f"wg{ci}", [128, 2 * 2048], BF16))
                wu_t = fs.enter_context(nc.sbuf_tensor(f"wu{ci}", [128, 2 * 2048], BF16))
                wd_t = fs.enter_context(nc.sbuf_tensor(f"wdt{ci}", [128, 2 * 2048], BF16))
                h_hi = fs.enter_context(nc.sbuf_tensor(f"hh{ci}", [128, 2 * 1024], BF16))
                h_lo = fs.enter_context(nc.sbuf_tensor(f"hl{ci}", [128, 2 * 1024], BF16))
                scr = [fs.enter_context(nc.sbuf_tensor(f"sc{ci}_{i}", [128, 512], F32))
                       for i in range(4)]
                pbk = [(f"b{i}", fs.enter_context(nc.psum_tensor(f"fp{ci}_{i}",
                                                                   [128, 512], F32)))
                       for i in range(8)]

                wsrc = {"g": wgd, "u": wud, "d": wdd}
                wtile = {"g": wg_t, "u": wu_t}
                conv_tok = {}
                slot_last_mm = {}
                chainF = {0: wstg_last[0], 1: wstg_last[1]}
                csemF = ("IN", "I4")

                dma_toks = {}

                def emit_stage_dma(hb, m):
                    toks = []
                    for h in range(2):
                        if chainF[h]:
                            p.wait("sync", "SC", chainF[h])
                        half = wstg[:, h * 1024:(h + 1) * 1024]
                        if m == "d":
                            srcap = wsrc[m][hb * HB + h * 128:hb * HB + (h + 1) * 128, :]
                            dstap = half
                        else:
                            srcap = wsrc[m][:, hb * HB + h * 128:hb * HB + (h + 1) * 128]\
                                .rearrange("(a p) c -> p a c", p=128)
                            dstap = half.rearrange("p (a c) -> p a c", a=8)
                        toks.append(p.op("sync", "dma_start", out=dstap, in_=srcap,
                                         inc=csemF[h], inck=16))
                        chainF[h] = 0
                    dma_toks[(hb, m)] = toks

                def emit_stage_conv(hb, m):
                    slot = hb % 2
                    toks = dma_toks[(hb, m)]
                    key = (m, slot)
                    tok_c = None
                    for h in range(2):
                        p.wait("scalar", csemF[h], toks[h])
                        if key in slot_last_mm:
                            p.wait("scalar", "MM", slot_last_mm[key])
                        half = wstg[:, h * 1024:(h + 1) * 1024]
                        if m == "d":
                            tok_c = p.op("scalar", "copy",
                                         out=wd_t[:, slot * 2048 + h * 1024:
                                                  slot * 2048 + (h + 1) * 1024],
                                         in_=half, inc="SC")
                        else:
                            cdst = wtile[m][:, slot * 2048:(slot + 1) * 2048].rearrange(
                                "p (k c) -> p k c", k=8)[:, :, h * 128:(h + 1) * 128]
                            tok_c = p.op("scalar", "copy", out=cdst,
                                         in_=half.rearrange("p (k c) -> p k c", k=8),
                                         inc="SC")
                        chainF[h] = tok_c
                    conv_tok[(hb, m)] = tok_c

                def emit_stage(hb, m):
                    emit_stage_dma(hb, m)
                    emit_stage_conv(hb, m)

                y_tok = {}

                def emit_ln_tile(nt):
                    tb = emit_ln_stats(p, pbk[6], pbk[7], pbk[6], pbk[7], nt,
                                       scr[2], scr[3], first_ln=(ci == 0), save=False,
                                       lnexp=True)
                    y_tok[nt] = emit_ln_split_ffn(p, pbk[6], pbk[7], nt, y_hi, y_lo,
                                                  scr[2], scr[3], tb)

                emit_stage_dma(0, "g")
                emit_ln_tile(0)
                emit_stage_conv(0, "g")
                emit_stage(0, "u")
                emit_stage(0, "d")
                emit_stage(1, "g")
                emit_stage(1, "u")
                emit_stage(1, "d")

                gu_i = [0]
                d_i = [0]
                sil_rd = [0]
                hf_rd = [0]
                h_tok = {}
                h_slot_rd = {}
                stt_tok = {}
                bank_mm = {}

                def emit_gu2(hb, nt, hslot):
                    slot = hb % 2
                    sil, hf = scr[0], scr[1]
                    for hl in range(2):
                        bg = pbk[gu_i[0] % 4]; gu_i[0] += 1
                        bu = pbk[gu_i[0] % 4]; gu_i[0] += 1
                        for m, bank in (("g", bg), ("u", bu)):
                            wait_bank(p, bank[0])
                            p.wait("tensor", "SC", conv_tok[(hb, m)])
                            p.wait("tensor", "VE", y_tok[nt])
                            n = 0
                            for ysrc in (y_hi, y_lo):
                                for kc in range(KC):
                                    tok = p.op(
                                        "tensor", "matmul", bank[1][:, :],
                                        wtile[m][:, slot * 2048 + kc * 256 + hl * 128:
                                                 slot * 2048 + kc * 256 + (hl + 1) * 128],
                                        ysrc[:, kc * T + nt * 512:kc * T + nt * 512 + 512],
                                        start=(n == 0), stop=(n == 15),
                                        inc=("MM" if n == 15 else None))
                                    n += 1
                            slot_last_mm[(m, slot)] = tok
                            bank_mm[bank[0]] = tok
                        hc0 = hslot * 1024 + hl * 512
                        p.wait("scalar", "MM", bank_mm[bg[0]])
                        p.wait("scalar", "VE", sil_rd[0])
                        if hslot in h_slot_rd:
                            p.wait("scalar", "MM", h_slot_rd[hslot])
                        tok_sil = p.op("scalar", "activation", out=sil[:, :], in_=bg[1][:, :],
                                       func=AF.Silu, scale=float(S_FFN), inc="SC")
                        bank_free[bg[0]] = ("SC", tok_sil)
                        p.wait("vector", "SC", tok_sil)
                        p.wait("vector", "MM", bank_mm[bu[0]])
                        if hslot in h_slot_rd:
                            p.wait("vector", "MM", h_slot_rd[hslot])
                        p.wait("vector", "VE", hf_rd[0])
                        tok_hf = p.op("vector", "tensor_mul", hf[:, :], sil[:, :], bu[1][:, :],
                                      inc="VE")
                        sil_rd[0] = tok_hf
                        bank_free[bu[0]] = ("VE", tok_hf)
                        p.wait("scalar", "VE", tok_hf)
                        tok_hi = p.op("scalar", "copy", out=h_hi[:, hc0:hc0 + 512],
                                      in_=hf[:, :], inc="SC")
                        p.wait("vector", "SC", tok_hi)
                        h_tok[(hb, nt)] = p.op("vector", "tensor_sub", h_lo[:, hc0:hc0 + 512],
                                               hf[:, :], h_hi[:, hc0:hc0 + 512], inc="VE")
                        hf_rd[0] = h_tok[(hb, nt)]

                def emit_down(hb, nt, hslot):
                    dslot = hb % 2
                    last_mm = None
                    for oc in range(8):
                        bank = pbk[4 + d_i[0] % 2]; d_i[0] += 1
                        wait_bank(p, bank[0])
                        p.wait("tensor", "SC", conv_tok[(hb, "d")])
                        p.wait("tensor", "VE", h_tok[(hb, nt)])
                        n = 0
                        for hsrc in (h_hi, h_lo):
                            for hl in range(2):
                                tok = p.op(
                                    "tensor", "matmul", bank[1][:, :],
                                    wd_t[:, dslot * 2048 + hl * 1024 + oc * 128:
                                         dslot * 2048 + hl * 1024 + (oc + 1) * 128],
                                    hsrc[:, hslot * 1024 + hl * 512:hslot * 1024 + (hl + 1) * 512],
                                    start=(n == 0), stop=(n == 3),
                                    inc=("MM" if n == 3 else None))
                                n += 1
                        last_mm = tok
                        slot_last_mm[("d", dslot)] = tok
                        xsl = xs[:, oc * T + nt * 512:oc * T + nt * 512 + 512]
                        p.wait("vector", "MM", tok)
                        if (oc, nt) in stt_tok:
                            p.wait("vector", "VE", stt_tok[(oc, nt)])
                        stt_tok[(oc, nt)] = p.op(
                            "vector", "scalar_tensor_tensor", out=xsl, in0=bank[1][:, :],
                            scalar=float(C_FFN), in1=xsl, op0=ALU.mult, op1=ALU.add, inc="VE")
                        xs_last[(oc, nt)] = stt_tok[(oc, nt)]
                        bank_free[bank[0]] = ("VE", stt_tok[(oc, nt)])
                    h_slot_rd[hslot] = last_mm

                pend = None
                items = []
                for nt in range(NT):
                    items.append((0, nt))
                    items.append((1, nt))
                for hb in range(2, NHB):
                    for nt in range(NT):
                        items.append((hb, nt))
                for idx, (hb, nt) in enumerate(items):
                    if idx == 1:
                        emit_stage_dma(2, "g")
                    if idx == 7:
                        emit_stage_conv(2, "g")
                        emit_stage_dma(2, "u")
                    if hb == 2 and nt == 0:
                        emit_stage_conv(2, "u")
                        emit_stage_dma(2, "d")
                    if hb >= 2 and nt == 1 and hb + 1 < NHB:
                        emit_stage_conv(hb + 1, "g")
                        emit_stage_dma(hb + 1, "u")
                    if hb >= 2 and nt == 2 and hb + 1 < NHB:
                        emit_stage_conv(hb + 1, "u")
                        emit_stage_dma(hb + 1, "d")
                    emit_gu2(hb, nt, idx % 2)
                    if hb == 0 and nt + 1 < NT:
                        emit_ln_tile(nt + 1)
                    if pend is not None:
                        emit_down(*pend)
                    pend = (hb, nt, idx % 2)
                    if hb >= 2 and nt == 0:
                        emit_stage_conv(hb, "d")
                        if hb + 1 < NHB:
                            emit_stage_dma(hb + 1, "g")
                emit_down(*pend)

                wstg_last[0] = chainF[0]
                wstg_last[1] = chainF[1]
                if out_dma:
                    for nt in range(NT):
                        for oc in range(8):
                            p.wait("gpsimd", "VE", stt_tok[(oc, nt)])
                            p.op("gpsimd", "dma_start",
                                 out=yd[oc * 128:(oc + 1) * 128, nt * 512:(nt + 1) * 512],
                                 in_=xs[:, oc * T + nt * 512:oc * T + nt * 512 + 512],
                                 inc="GP", inck=16)
                    p.wait("gpsimd", "GP", cnt["GP"])

                barrier()

        pf1 = P
        phase_entry(pf1)
        emit_ffn2(pf1, 0, out_dma=False)

        pa = P
        p = pa
        phase_entry(pa, tensor_too=False)
        with ExitStack() as fs:
            wk_hi = fs.enter_context(nc.sbuf_tensor("wk_hi", [128, 8192], BF16))
            wk_lo = fs.enter_context(nc.sbuf_tensor("wk_lo", [128, 8192], BF16))
            wv_hi = fs.enter_context(nc.sbuf_tensor("wv_hi", [128, 8192], BF16))
            wv_lo = fs.enter_context(nc.sbuf_tensor("wv_lo", [128, 8192], BF16))
            y2h = fs.enter_context(nc.sbuf_tensor("y2h", [128, 2 * 2048], BF16))
            y2l = fs.enter_context(nc.sbuf_tensor("y2l", [128, 2 * 2048], BF16))
            kfT = fs.enter_context(nc.sbuf_tensor("kfT", [128, 2048], F32))
            vT = fs.enter_context(nc.sbuf_tensor("vT", [128, 2048], F32))
            scrA = [fs.enter_context(nc.sbuf_tensor(f"sA{i}", [128, 512], F32)) for i in range(4)]
            pbA = [(f"b{i}", fs.enter_context(nc.psum_tensor(f"ap{i}", [128, 512], F32)))
                   for i in range(8)]
            kv_b, ks_b = pbA[4], pbA[5]

            split_tok = {}
            split_tokB = {}
            chainA = [0, 0]
            hA = {}
            stage_split2(p, wkd, wk_hi, wk_lo, split_tok, "k", chainA, bar["MM"])
            tok_bcA = {}
            tok_bcA[0] = emit_ln_stats(p, pbA[6], pbA[7], pbA[6], pbA[7], 0,
                                       scrA[2], scrA[3], first_ln=False, save=True,
                                       lnexp=True)

            def emit_ln_split_pair(nt, off, w, mb, rb, t1, t2, tok_bc, war_mm, slot):
                c0 = nt * 512 + off
                s0 = slot * 2048
                p.wait("vector", "MM", tok_bc)
                if war_mm:
                    p.wait("vector", "MM", war_mm)
                    p.wait("scalar", "MM", war_mm)
                last = None
                t1d = {0: t1, 1: s5}
                for kc in range(KC):
                    xsl = xs[:, kc * T + c0:kc * T + c0 + w]
                    t1k = t1d[kc % 2]
                    p.wait("vector", "VE", xs_last.get((kc, nt), 0))
                    p.wait("vector", "VE", ln_war.get(f"t1rd{kc % 2}", 0))
                    ts_ = p.op("vector", "tensor_sub", t1k[:, 0:w], xsl, mb[1][:, off:off + w],
                               inc="VE")
                    p.wait("vector", "VE", ts_)
                    p.op("vector", "tensor_mul", y2h[:, s0 + kc * w:s0 + (kc + 1) * w],
                         t1k[:, 0:w], rb[1][:, off:off + w], inc="VE")
                    p.wait("vector", "VE", ln_war.get("t2_rd_ve", 0))
                    tf = p.op("vector", "tensor_mul", t2[:, 0:w], t1k[:, 0:w],
                              rb[1][:, off:off + w], inc="VE")
                    ln_war[f"t1rd{kc % 2}"] = tf
                    p.wait("vector", "VE", tf)
                    last = p.op("vector", "tensor_sub", y2l[:, s0 + kc * w:s0 + (kc + 1) * w],
                                t2[:, 0:w], y2h[:, s0 + kc * w:s0 + (kc + 1) * w], inc="VE")
                    ln_war["split_ve"] = last
                    ln_war["t2_rd_ve"] = last
                bank_free[mb[0]] = ("VE", last)
                bank_free[rb[0]] = ("VE", last)
                return last

            projA_i = [0]
            eluA = {"t1_rd": 0, "t2_rd": 0}
            kv_started = {0: False, 64: False}
            ks_started = [False]
            mm_proj_last = [bar["MM"], bar["MM"]]
            halves = [(nt, h2) for nt in range(NT) for h2 in range(2)]
            kvacc_tok = [bar["MM"]]

            def emit_projA_mm(key, hi, lo, y2tok, slot):
                banks = []
                s0 = slot * 2048
                for tp in range(2):
                    for dh in range(2):
                        bank = pbA[projA_i[0] % 4]; projA_i[0] += 1
                        wait_bank(p, bank[0])
                        p.wait("tensor", "VE", y2tok)
                        p.wait("tensor", "SC", split_tok[(key, 8 * dh + 7)][0])
                        p.wait("tensor", "VE", split_tok[(key, 8 * dh + 7)][1])
                        n = 0
                        for ysrc, wsrc_ in ((y2h, hi), (y2l, hi), (y2h, lo)):
                            for kc in range(KC):
                                tok = p.op(
                                    "tensor", "matmul", bank[1][:, :],
                                    ysrc[:, s0 + kc * 256 + tp * 128:
                                         s0 + kc * 256 + (tp + 1) * 128],
                                    wsrc_[:, kc * 1024 + dh * 512:kc * 1024 + (dh + 1) * 512],
                                    start=(n == 0), stop=(n == 23),
                                    inc=("MM" if n == 23 else None))
                                n += 1
                        banks.append((tp, dh, bank, tok))
                mm_proj_last[slot] = banks[-1][3]
                return banks

            def emit_projA_evac(key, banks):
                last = None
                for (tp, dh, bank, tok) in banks:
                    dstc = tp * 1024 + dh * 512
                    dst = kfT if key == "k" else vT
                    p.wait("vector", "MM", tok)
                    p.wait("vector", "MM", kvacc_tok[0])
                    if key == "k":
                        t1, t2 = scrA[0], scrA[1]
                        p.wait("vector", "SC", eluA["t1_rd"])
                        tok_min = p.op("vector", "tensor_scalar_min", t1[:, :], bank[1][:, :],
                                       0.0, inc="VE")
                        p.wait("scalar", "VE", tok_min)
                        p.wait("scalar", "VE", eluA["t2_rd"])
                        eluA["t1_rd"] = p.op("scalar", "activation", out=t2[:, :], in_=t1[:, :],
                                             func=AF.Exp, inc="SC")
                        p.wait("vector", "SC", eluA["t1_rd"])
                        last = p.op("vector", "scalar_tensor_tensor", out=dst[:, dstc:dstc + 512],
                                    in0=bank[1][:, :], scalar=0.0, in1=t2[:, :],
                                    op0=ALU.max, op1=ALU.add, inc="VE")
                        eluA["t2_rd"] = last
                    else:
                        last = p.op("vector", "tensor_copy", out=dst[:, dstc:dstc + 512],
                                    in_=bank[1][:, :], inc="VE")
                    bank_free[bank[0]] = ("VE", last)
                return last

            def emit_kvacc(ev_tok, is_last):
                p.wait("tensor", "VE", ev_tok)
                tok = None
                for tp in range(2):
                    for hp in range(8):
                        for par in (0, 64):
                            h = 2 * hp + (1 if par else 0)
                            stt_ = not kv_started[par]
                            kv_started[par] = True
                            p.op("tensor", "matmul",
                                 kv_b[1][par:par + 64, hp * 64:(hp + 1) * 64],
                                 kfT[:, tp * 1024 + h * 64:tp * 1024 + (h + 1) * 64],
                                 vT[:, tp * 1024 + h * 64:tp * 1024 + (h + 1) * 64],
                                 start=stt_, stop=(is_last and tp == 1 and hp == 7),
                                 tile_position=(0, par),
                                 skip_group_check=True)
                    for hp in range(8):
                        stt_ = not ks_started[0]
                        ks_started[0] = True
                        tok = p.op("tensor", "matmul", ks_b[1][:, hp:hp + 1],
                                   kfT[:, tp * 1024 + hp * 128:tp * 1024 + (hp + 1) * 128],
                                   ones_col,
                                   start=stt_, stop=(is_last and tp == 1 and hp == 7),
                                   skip_group_check=True,
                                   inc=("MM" if (tp == 1 and hp == 7) else None))
                kvacc_tok[0] = tok
                return tok

            pendA = None
            for hi_, (nt, h2) in enumerate(halves):
                slot = hi_ % 2
                if pendA is not None and h2 == 1:
                    emit_kvacc(pendA, is_last=False)
                    pendA = None
                y2tok = emit_ln_split_pair(nt, h2 * 256, 256, pbA[6], pbA[7],
                                           scrA[2], scrA[3], tok_bcA[nt],
                                           mm_proj_last[slot], slot)
                bk = emit_projA_mm("k", wk_hi, wk_lo, y2tok, slot)
                if hi_ == 7:
                    hA["kproj_mm"] = mm_proj_last[slot]
                if hi_ == 0:
                    stage_split2(p, wvd, wv_hi, wv_lo, split_tok, "v", chainA, bar["MM"])
                if h2 == 1 and nt + 1 < NT:
                    tok_bcA[nt + 1] = emit_ln_stats(p, pbA[6], pbA[7], pbA[6], pbA[7],
                                                    nt + 1, scrA[2], scrA[3],
                                                    first_ln=False, save=True,
                                                    lnexp=True)
                if pendA is not None:
                    emit_kvacc(pendA, is_last=False)
                    pendA = None
                evk = emit_projA_evac("k", bk)
                if hi_ == 7:
                    stage_split2(p, wqd, wk_hi, wk_lo, split_tokB, "q", chainA,
                                 hA["kproj_mm"], 0, 8)
                bv = emit_projA_mm("v", wv_hi, wv_lo, y2tok, slot)
                if hi_ == 7:
                    hA["vproj_mm"] = mm_proj_last[slot]
                evv = emit_projA_evac("v", bv)
                pendA = evv
            emit_kvacc(pendA, is_last=True)
            stage_split2(p, wqd, wk_hi, wk_lo, split_tokB, "q", chainA,
                         hA["kproj_mm"], 8, 16)

            p.wait("vector", "MM", cnt["MM"])
            p.op("vector", "tensor_copy", out=scrA[0][:, :], in_=kv_b[1][:, :], inc="VE")
            tok_ccs = p.op("vector", "tensor_copy", out=scrA[1][:, 0:8], in_=ks_b[1][:, 0:8],
                           inc="VE")
            p.wait("sync", "VE", tok_ccs)
            p.op("sync", "dma_start", out=cc_in.ap()[:, 0:512], in_=scrA[0][:, :],
                 inc="LD", inck=16)
            tok_d = p.op("sync", "dma_start", out=cc_in.ap()[:, 512:520],
                         in_=scrA[1][:, 0:8], inc="LD", inck=16)
            p.wait("gpsimd", "LD", tok_d)
            p.op("gpsimd", "collective_compute", "AllReduce", ALU.add,
                 replica_groups=[[0, 1], [2, 3], [4, 5], [6, 7]],
                 ins=[cc_in.ap().opt()], outs=[cc_out.ap().opt()], inc="CC")
            p.wait("gpsimd", "CC", cnt["CC"])
            p.op("gpsimd", "dma_start", out=kvks[:, :], in_=cc_out.ap(), inc="GP", inck=16)

            stB = {"ty1_rd": m2r2_tok[3], "ty2_ve": m2r2_tok[3]}
            y2b0_tok = emit_y2b_core(p, pbA[6], pbA[7], 0, 0,
                                     y2h[:, 0:2048], y2h[:, 2048:4096],
                                     mm_proj_last[0], mm_proj_last[1], stB)
            barrier()
        ar_tok = cnt["GP"]
        a_ve_end = tok_ccs
        a_sc_end = eluA["t1_rd"]
        y2b0_hi = y2h[:, 0:2048]
        y2b0_lo = y2h[:, 2048:4096]
        wqs_hi, wqs_lo = wk_hi, wk_lo

        pb_ = P
        p = pb_
        with ExitStack() as fs:
            wq_hi = fs.enter_context(nc.sbuf_tensor("wq_hi", [128, 8192], BF16))
            wq_lo = fs.enter_context(nc.sbuf_tensor("wq_lo", [128, 8192], BF16))
            wo_hi = fs.enter_context(nc.sbuf_tensor("wo_hi", [128, 8192], BF16))
            wo_lo = fs.enter_context(nc.sbuf_tensor("wo_lo", [128, 8192], BF16))
            y2h = fs.enter_context(nc.sbuf_tensor("y2hB", [128, 2048], BF16))
            y2l = fs.enter_context(nc.sbuf_tensor("y2lB", [128, 2048], BF16))
            qf = fs.enter_context(nc.sbuf_tensor("qf", [128, 2 * 2048], F32))
            at_hi = fs.enter_context(nc.sbuf_tensor("at_hi", [128, 2048], BF16))
            at_lo = fs.enter_context(nc.sbuf_tensor("at_lo", [128, 2048], BF16))
            ksB = fs.enter_context(nc.sbuf_tensor("ksB", [128, 8 * 128], F32))

            tq1 = rows[:, 512:768]
            tq2 = rows[:, 768:1024]
            ta1 = rows[:, 1024:1280]
            dne = rows[:, 1280:1536]
            scB = {"tq1_rd": a_sc_end, "tq2_rd": a_ve_end,
                   "ta1_rd": a_sc_end, "rcp_rd": a_ve_end}
            pbB = [(f"b{i}", fs.enter_context(nc.psum_tensor(f"bp{i}", [128, 512], F32)))
                   for i in range(8)]

            chainB = [chainA[0], chainA[1]]

            qp_i = [0]
            op_i = [0]
            mm_qproj_last = [bar["MM"]]
            mm_oproj_last = [bar["MM"]]
            mm_hp_last = [bar["MM"]]

            def emit_y2b(nt, h2):
                return emit_y2b_core(p, pbB[6], pbB[7], nt, h2, y2h, y2l,
                                     mm_qproj_last[0], mm_qproj_last[0], stB)

            ar_waited = [False]
            at_last = [a_ve_end]
            ksb_tok = [0]

            def emit_hp_step(hp, qftok, slot):
                qf0 = slot * 2048
                dn, qk = pbB[4], pbB[5]
                if not ar_waited[0]:
                    p.wait("tensor", "GP", ar_tok)
                    p.wait("tensor", "VE", ksb_tok[0])
                    ar_waited[0] = True
                wait_bank(p, dn[0])
                p.wait("tensor", "VE", qftok)
                kb = ksB[:, hp * 128:(hp + 1) * 128]
                p.op("tensor", "matmul", dn[1][0:64, 0:256], kb[0:64, 0:64],
                     qf[0:64, qf0 + hp * 256:qf0 + (hp + 1) * 256], start=True, stop=True,
                     tile_position=(0, 0), skip_group_check=True)
                tok_dn = p.op("tensor", "matmul", dn[1][64:128, 0:256], kb[64:128, 64:128],
                              qf[64:128, qf0 + hp * 256:qf0 + (hp + 1) * 256],
                              start=True, stop=True,
                              tile_position=(64, 64), skip_group_check=True, inc="MM")
                wait_bank(p, qk[0])
                p.op("tensor", "matmul", qk[1][0:64, 0:256], kvks[0:64, hp * 64:(hp + 1) * 64],
                     qf[0:64, qf0 + hp * 256:qf0 + (hp + 1) * 256], start=True, stop=True,
                     tile_position=(0, 0), skip_group_check=True)
                tok_qk = p.op("tensor", "matmul", qk[1][64:128, 0:256],
                              kvks[64:128, hp * 64:(hp + 1) * 64],
                              qf[64:128, qf0 + hp * 256:qf0 + (hp + 1) * 256],
                              start=True, stop=True,
                              tile_position=(64, 64), skip_group_check=True, inc="MM")
                mm_hp_last[0] = tok_qk
                p.wait("scalar", "MM", tok_dn)
                p.wait("scalar", "SC", scB["ta1_rd"])
                tok_eps = p.op("scalar", "activation", out=dne, in_=dn[1][:, 0:256],
                               func=AF.Ln, bias=float(EPS_AT), scale=1.0, inc="SC")
                bank_free[dn[0]] = ("SC", tok_eps)
                p.wait("scalar", "VE", scB.get("rcp_rd", 0))
                p.wait("scalar", "SC", tok_eps)
                tok_rcp = p.op("scalar", "activation", out=dne, in_=dne,
                               func=AF.Exp, scale=-1.0, inc="SC")
                p.wait("vector", "MM", tok_qk)
                p.wait("vector", "SC", tok_rcp)
                p.wait("vector", "MM", mm_oproj_last[0])
                p.wait("vector", "SC", scB["ta1_rd"])
                p.wait("vector", "VE", at_last[0])
                tok_at = p.op("vector", "tensor_mul", ta1, qk[1][:, 0:256], dne,
                              inc="VE")
                scB["rcp_rd"] = tok_at
                bank_free[qk[0]] = ("VE", tok_at)
                p.wait("scalar", "VE", tok_at)
                p.wait("scalar", "MM", mm_oproj_last[0])
                scB["ta1_rd"] = p.op("scalar", "copy", out=at_hi[:, hp * 256:(hp + 1) * 256],
                                     in_=ta1, inc="SC")
                p.wait("vector", "SC", scB["ta1_rd"])
                at_last[0] = p.op("vector", "tensor_sub", at_lo[:, hp * 256:(hp + 1) * 256],
                                  ta1, at_hi[:, hp * 256:(hp + 1) * 256], inc="VE")
                return at_last[0]

            def emit_qproj(y2tok, slot, srcs, hp_cb=None, nbk=2):
                qf0 = slot * 2048
                p.wait("vector", "VE", a_ve_end)
                last = None
                for oc in range(8):
                    bank = pbB[qp_i[0] % nbk]; qp_i[0] += 1
                    wait_bank(p, bank[0])
                    p.wait("tensor", "SC", split_tokB[("q", 2 * oc + 1)][0])
                    p.wait("tensor", "VE", split_tokB[("q", 2 * oc + 1)][1])
                    p.wait("tensor", "VE", y2tok)
                    n = 0
                    for ysrc, wsrc_ in ((srcs[0], wqs_hi), (srcs[1], wqs_hi),
                                        (srcs[0], wqs_lo)):
                        for kc in range(KC):
                            tok = p.op("tensor", "matmul", bank[1][:, 0:256],
                                       wsrc_[:, kc * 1024 + oc * 128:kc * 1024 + (oc + 1) * 128],
                                       ysrc[:, kc * 256:(kc + 1) * 256],
                                       start=(n == 0), stop=(n == 23),
                                       inc=("MM" if n == 23 else None))
                            n += 1
                    mm_qproj_last[0] = tok
                    if hp_cb is not None:
                        hp_cb(oc)
                    p.wait("vector", "MM", tok)
                    p.wait("vector", "MM", mm_hp_last[0])
                    p.wait("vector", "SC", scB["tq1_rd"])
                    tok_min = p.op("vector", "tensor_scalar_min", tq1, bank[1][:, 0:256],
                                   0.0, inc="VE")
                    p.wait("scalar", "VE", tok_min)
                    p.wait("scalar", "VE", scB["tq2_rd"])
                    scB["tq1_rd"] = p.op("scalar", "activation", out=tq2, in_=tq1,
                                         func=AF.Exp, inc="SC")
                    p.wait("vector", "SC", scB["tq1_rd"])
                    last = p.op("vector", "scalar_tensor_tensor",
                                out=qf[:, qf0 + oc * 256:qf0 + (oc + 1) * 256],
                                in0=bank[1][:, 0:256], scalar=0.0, in1=tq2,
                                op0=ALU.max, op1=ALU.add, inc="VE")
                    scB["tq2_rd"] = last
                    bank_free[bank[0]] = ("VE", last)
                return last

            def emit_oproj(nt, h2, attok):
                c0 = nt * 512 + h2 * 256
                p.wait("vector", "VE", a_ve_end)
                for oc in range(8):
                    bank = pbB[2 + op_i[0] % 2]; op_i[0] += 1
                    wait_bank(p, bank[0])
                    p.wait("tensor", "SC", split_tokB[("o", 2 * oc + 1)][0])
                    p.wait("tensor", "VE", split_tokB[("o", 2 * oc + 1)][1])
                    p.wait("tensor", "VE", attok)
                    n = 0
                    for asrc, wsrc_ in ((at_hi, wo_hi), (at_lo, wo_hi), (at_hi, wo_lo)):
                        for kc in range(KC):
                            tok = p.op("tensor", "matmul", bank[1][:, 0:256],
                                       wsrc_[:, kc * 1024 + oc * 128:kc * 1024 + (oc + 1) * 128],
                                       asrc[:, kc * 256:(kc + 1) * 256],
                                       start=(n == 0), stop=(n == 23),
                                       inc=("MM" if n == 23 else None))
                            n += 1
                    mm_oproj_last[0] = tok
                    xsl = xs[:, oc * T + c0:oc * T + c0 + 256]
                    p.wait("vector", "MM", tok)
                    p.wait("vector", "LD", tok_bo)
                    tk = p.op("vector", "scalar_tensor_tensor", out=xsl, in0=bank[1][:, 0:256],
                              scalar=bo_sb[:, oc:oc + 1], in1=xsl, op0=ALU.add, op1=ALU.add,
                              inc="VE")
                    xs_last[(oc, nt)] = tk
                    bank_free[bank[0]] = ("VE", tk)

            halvesB = [(nt, h2) for nt in range(NT) for h2 in range(2)]
            pend1 = None
            pend2 = None
            y2next = y2b0_tok
            for i, (nt, h2) in enumerate(halvesB):
                y2t = y2next
                slot = i % 2
                if i == 2:
                    p.wait("vector", "GP", ar_tok)
                    for _hp in range(8):
                        p.op("vector", "tensor_scalar",
                             ksB[:, _hp * 128:(_hp + 1) * 128], kvks[:, 0:128], 0.0,
                             kvks[:, 512 + _hp:513 + _hp],
                             op0=ALU.mult, op1=ALU.add, inc="VE")
                    ksb_tok[0] = cnt["VE"]
                ysrcs = (y2b0_hi, y2b0_lo) if i == 0 else (y2h, y2l)
                if pend2 is None:
                    qt = emit_qproj(y2t, slot, ysrcs, nbk=4)
                    if i == 1:
                        qp_i[0] = 0
                    if i == 0:
                        stage_split2(p, wod, wo_hi, wo_lo, split_tokB, "o", chainB,
                                     hA["vproj_mm"])
                else:
                    (pnt2, ph22, pq2, ps2) = pend2
                    qt = emit_qproj(y2t, slot, ysrcs,
                                    hp_cb=lambda g, pq=pq2, ps=ps2: emit_hp_step(g, pq, ps))
                if i + 1 < len(halvesB):
                    y2next = emit_y2b(*halvesB[i + 1])
                if pend2 is not None:
                    emit_oproj(pnt2, ph22, at_last[0])
                pend2 = pend1
                pend1 = (nt, h2, qt, slot)
            for (pnt2, ph22, pq2, ps2) in (pend2, pend1):
                for hp in range(8):
                    emit_hp_step(hp, pq2, ps2)
                emit_oproj(pnt2, ph22, at_last[0])

            barrier()

        pf2 = P
        phase_entry(pf2, tensor_too=False)
        emit_ffn2(pf2, 1, out_dma=True)

        P.run(nc)

    return nc


_NC_CACHE = []


def kernel(L0_attn_x, L0_attn_wq, L0_attn_wk, L0_attn_wv, L0_attn_wo, L0_attn_bo,
           L0_ffn_w_gate, L0_ffn_w_up, L0_ffn_w_down):
    x = np.asarray(L0_attn_x)
    wqT = np.ascontiguousarray(np.asarray(L0_attn_wq).T)
    wkT = np.ascontiguousarray(np.asarray(L0_attn_wk).T)
    wvT = np.ascontiguousarray(np.asarray(L0_attn_wv).T)
    woT = np.ascontiguousarray(np.asarray(L0_attn_wo).T)
    bo2 = np.ascontiguousarray(np.asarray(L0_attn_bo).reshape(8, 128).T)
    wgT = np.ascontiguousarray(np.asarray(L0_ffn_w_gate).T)
    wuT = np.ascontiguousarray(np.asarray(L0_ffn_w_up).T)
    wdT = np.ascontiguousarray(np.asarray(L0_ffn_w_down).T)

    cons = np.zeros((128, 130), np.float32)
    cons[:, 0] = 1.0
    cons[:, 1] = EPS_AT
    cons[0, 2:130] = 1.0

    in_maps = []
    for c in range(8):
        b, h = c // 2, c % 2
        xsh = np.ascontiguousarray(x[b, h * T:(h + 1) * T, :].T)
        in_maps.append({"x": xsh, "wq": wqT, "wk": wkT, "wv": wvT, "wo": woT,
                        "bo": bo2, "wg": wgT, "wu": wuT, "wd": wdT, "cons": cons})

    if not _NC_CACHE:
        _NC_CACHE.append(build_nc())
    nc = _NC_CACHE[0]
    res = run_bass_kernel_spmd(nc, in_maps, core_ids=list(range(8)))

    out = np.empty((4, 4096, 1024), np.float32)
    for c in range(8):
        b, h = c // 2, c % 2
        out[b, h * T:(h + 1) * T, :] = res.results[c]["y"].T
    return out


# version 20
# speedup vs baseline: 1.1284x; 1.0044x over previous
import numpy as np
from contextlib import ExitStack

import concourse.bass as bass
import concourse.mybir as mybir
from concourse.bass_utils import run_bass_kernel_spmd

F32 = mybir.dt.float32
BF16 = mybir.dt.bfloat16
I32 = mybir.dt.int32
AF = mybir.ActivationFunctionType
ALU = mybir.AluOpType

D = 1024
T = 2048
HID = 3072
KC = 8
NT = 4
HB = 256
NHB = HID // HB
S_FFN = 1.0 / 127.0
C_FFN = 0.5 * S_FFN * S_FFN
EPS_LN = 1e-5
EPS_AT = 1e-6

ENGS = ("sync", "tensor", "vector", "scalar", "gpsimd")


class Plan:

    def __init__(self, sems, cnt):
        self.q = {e: [] for e in ENGS}
        self.sems = sems
        self.cnt = cnt

    def wait(self, eng, sem, val):
        if val is None or val <= 0:
            return
        assert val <= self.cnt[sem], f"wait {sem}>={val} but only {self.cnt[sem]} emitted"
        h = self.sems[sem]
        self.q[eng].append(lambda e, h=h, v=val: e.wait_ge(h, v))

    def op(self, eng, name, *args, inc=None, inck=1, **kw):
        h = self.sems[inc] if inc else None

        def f(e, name=name, args=args, kw=kw, h=h, inck=inck):
            ins = getattr(e, name)(*args, **kw)
            if h is not None:
                ins.then_inc(h, inck)
        self.q[eng].append(f)
        if inc:
            self.cnt[inc] += inck
            return self.cnt[inc]
        return None

    def run(self, nc):
        with nc.Block(no_gpsimd_drain=True) as b:
            for e in ENGS:
                if self.q[e]:
                    getattr(b, e)(lambda eng, q=tuple(self.q[e]): [f(eng) for f in q])


def build_nc():
    nc = bass.Bass("TRN2", target_bir_lowering=False)

    xd = nc.dram_tensor("x", [D, T], F32, kind="ExternalInput").ap()
    wqd = nc.dram_tensor("wq", [D, D], F32, kind="ExternalInput").ap()
    wkd = nc.dram_tensor("wk", [D, D], F32, kind="ExternalInput").ap()
    wvd = nc.dram_tensor("wv", [D, D], F32, kind="ExternalInput").ap()
    wod = nc.dram_tensor("wo", [D, D], F32, kind="ExternalInput").ap()
    bod = nc.dram_tensor("bo", [128, 8], F32, kind="ExternalInput").ap()
    wgd = nc.dram_tensor("wg", [D, HID], I32, kind="ExternalInput").ap()
    wud = nc.dram_tensor("wu", [D, HID], I32, kind="ExternalInput").ap()
    wdd = nc.dram_tensor("wd", [HID, D], I32, kind="ExternalInput").ap()
    cnd = nc.dram_tensor("cons", [128, 130], F32, kind="ExternalInput").ap()
    yd = nc.dram_tensor("y", [D, T], F32, kind="ExternalOutput").ap()

    cc_in = nc.dram_tensor("cc_in", [128, 520], F32)
    cc_out = nc.dram_tensor("cc_out", [128, 520], F32)

    st = ExitStack()
    with st:
        xs = st.enter_context(nc.sbuf_tensor("xs", [128, KC * T], F32))
        wstg = st.enter_context(nc.sbuf_tensor("wstg", [128, 2048], I32))
        fstg = st.enter_context(nc.sbuf_tensor("fstg", [128, 1024], F32))
        cons = st.enter_context(nc.sbuf_tensor("cons_sb", [128, 130], F32))
        bo_sb = st.enter_context(nc.sbuf_tensor("bo_sb", [128, 8], F32))
        m2r2 = st.enter_context(nc.sbuf_tensor("m2r2", [1, 2 * T], F32))
        rows = st.enter_context(nc.sbuf_tensor("rows", [128, 2048], F32))
        kvks = st.enter_context(nc.sbuf_tensor("kvks", [128, 520], F32))

        ones_col = cons[:, 0:1]
        ones_row = cons[0:1, 2:130]
        nc.const_aps.register(F32, float(EPS_AT), cons[:, 1:2])

        sem_names = ("IN", "I2", "I3", "I4", "LD", "GP", "CC", "MM", "VE", "SC")
        sems = {n: st.enter_context(nc.semaphore(n)) for n in sem_names}
        cnt = {n: 0 for n in sem_names}

        bank_free = {}
        wstg_last = [0, 0]
        xs_last = {}
        m2r2_tok = {}
        ln_war = {"split_ve": 0}

        def wait_bank(p, bname):
            if bname in bank_free:
                s, t = bank_free[bname]
                p.wait("tensor", s, t)

        bar = {n: 0 for n in sem_names}

        def barrier():
            bar.update({n: cnt[n] for n in sem_names})

        def phase_entry(p, tensor_too=True):
            engs = ("tensor", "vector", "scalar") if tensor_too else ("vector", "scalar")
            for e in engs:
                for s in ("MM", "VE", "SC"):
                    p.wait(e, s, bar[s])

        P = Plan(sems, cnt)
        p0 = P
        tok_cons = p0.op("sync", "dma_start", out=cons[:, :], in_=cnd, inc="LD", inck=16)
        tok_bo = p0.op("sync", "dma_start", out=bo_sb[:, :], in_=bod, inc="LD", inck=16)
        tok_xt = {}
        for nt in range(NT):
            p0.wait("sync", "LD", tok_bo if nt == 0 else tok_xt[nt - 1])
            for kc in range(KC):
                tok_xt[nt] = p0.op(
                    "sync", "dma_start",
                    out=xs[:, kc * T + nt * 512:kc * T + (nt + 1) * 512],
                    in_=xd[kc * 128:(kc + 1) * 128, nt * 512:(nt + 1) * 512],
                    inc="LD", inck=16)
        barrier()

        m_r = rows[0:1, 0:512]
        a_r = rows[0:1, 512:1024]
        b_r = rows[0:1, 1024:1536]
        s5 = rows[:, 1536:2048]

        def emit_ln_stats(p, s1, s2, mb, rb, nt, sq0, sq1, first_ln, save, lnexp=False):
            c0 = nt * 512
            wait_bank(p, s1[0]); wait_bank(p, s2[0])
            p.wait("scalar", "VE", ln_war["split_ve"])
            for kc in range(KC):
                p.wait("scalar", "VE", xs_last.get((kc, nt), 0))
                p.wait("tensor", "VE", xs_last.get((kc, nt), 0))
            tok_sq = []
            for kc in range(KC):
                sq = sq0 if kc % 2 == 0 else sq1
                if first_ln and kc == 0:
                    p.wait("scalar", "LD", tok_xt[nt])
                if kc >= 2:
                    p.wait("scalar", "MM", tok_sq[kc - 2])
                p.op("scalar", "square", out=sq[:, :], in_=xs[:, kc * T + c0:kc * T + c0 + 512],
                     inc="SC")
                p.wait("tensor", "SC", cnt["SC"])
                if first_ln and kc == 0:
                    p.wait("tensor", "LD", tok_xt[nt])
                p.op("tensor", "matmul", s1[1][0:1, :], ones_col,
                     xs[:, kc * T + c0:kc * T + c0 + 512],
                     start=(kc == 0), stop=(kc == KC - 1))
                tok_sq.append(p.op("tensor", "matmul", s2[1][0:1, :], ones_col, sq[:, :],
                                   start=(kc == 0), stop=(kc == KC - 1), inc="MM"))
            p.wait("vector", "MM", tok_sq[-1])
            tm = p.op("vector", "tensor_scalar_mul", m_r, s1[1][0:1, :], 1.0 / D, inc="VE")
            ta = p.op("vector", "tensor_scalar_mul", a_r, s2[1][0:1, :], 1.0 / D, inc="VE")
            bank_free[s1[0]] = ("VE", cnt["VE"])
            bank_free[s2[0]] = ("VE", cnt["VE"])
            p.wait("vector", "VE", tm)
            tb = p.op("vector", "tensor_mul", b_r, m_r, m_r, inc="VE")
            p.wait("vector", "VE", tb)
            tv = p.op("vector", "tensor_sub", a_r, a_r, b_r, inc="VE")
            p.wait("vector", "VE", tv)
            tok_v = p.op("vector", "tensor_scalar_add", b_r, a_r, EPS_LN, inc="VE")
            p.wait("scalar", "VE", tok_v)
            p.wait("scalar", "MM", ln_war.get("bc_mm", 0))
            if lnexp:
                p.op("scalar", "activation", out=a_r, in_=b_r, func=AF.Ln, inc="SC")
                p.wait("scalar", "SC", cnt["SC"])
                tok_q = p.op("scalar", "activation", out=b_r, in_=a_r, func=AF.Exp,
                             scale=-0.5, inc="SC")
                tok_rv = None
            else:
                tok_q = p.op("scalar", "sqrt", out=a_r, in_=b_r, inc="SC")
                p.wait("vector", "SC", tok_q)
                p.wait("vector", "MM", ln_war.get("bc_mm", 0))
                tok_rv = p.op("vector", "reciprocal", out=b_r, in_=a_r, inc="VE")
            ln_war["rows_sc"] = tok_q
            if save:
                p.op("vector", "tensor_copy", out=m2r2[0:1, c0:c0 + 512], in_=m_r, inc="VE")
                if tok_rv is None:
                    p.wait("vector", "SC", tok_q)
                else:
                    p.wait("vector", "VE", tok_rv)
                tok_rv = m2r2_tok[nt] = p.op("vector", "tensor_copy",
                                             out=m2r2[0:1, T + c0:T + c0 + 512], in_=b_r,
                                             inc="VE")
            wait_bank(p, mb[0]); wait_bank(p, rb[0])
            if tok_rv is not None:
                p.wait("tensor", "VE", tok_rv)
            else:
                p.wait("tensor", "SC", tok_q)
                p.wait("tensor", "VE", tm)
            p.wait("tensor", "LD", tok_bo)
            p.op("tensor", "matmul", mb[1][:, :], ones_row, m_r, start=True, stop=True)
            ln_war["bc_mm"] = p.op("tensor", "matmul", rb[1][:, :], ones_row, b_r,
                                   start=True, stop=True, inc="MM")
            return ln_war["bc_mm"]

        def emit_ln_split(p, mb, rb, nt, off, w, y_hi, y_lo, ycol, t1, t2, tok_bc, war_mm):
            c0 = nt * 512 + off
            p.wait("vector", "MM", tok_bc)
            if war_mm:
                p.wait("vector", "MM", war_mm)
                p.wait("scalar", "MM", war_mm)
            last = None
            for kc in range(KC):
                xsl = xs[:, kc * T + c0:kc * T + c0 + w]
                p.wait("vector", "SC", cnt["SC"])
                ts_ = p.op("vector", "tensor_sub", t1[:, 0:w], xsl, mb[1][:, off:off + w], inc="VE")
                p.wait("vector", "VE", ts_)
                tok_t2 = p.op("vector", "tensor_mul", t2[:, 0:w], t1[:, 0:w],
                              rb[1][:, off:off + w], inc="VE")
                p.wait("scalar", "VE", tok_t2)
                p.op("scalar", "copy", out=y_hi[:, ycol + kc * w:ycol + (kc + 1) * w],
                     in_=t2[:, 0:w], inc="SC")
                p.wait("vector", "SC", cnt["SC"])
                last = p.op("vector", "tensor_sub", y_lo[:, ycol + kc * w:ycol + (kc + 1) * w],
                            t2[:, 0:w], y_hi[:, ycol + kc * w:ycol + (kc + 1) * w], inc="VE")
            bank_free[mb[0]] = ("VE", last)
            bank_free[rb[0]] = ("VE", last)
            return last

        def emit_ln_split_ffn(p, mb, rb, nt, y_hi, y_lo, t1, t2, tok_bc):
            c0 = nt * 512
            p.wait("vector", "MM", tok_bc)
            last = None
            t1d = {0: t1, 1: s5}
            for kc in range(KC):
                xsl = xs[:, kc * T + c0:kc * T + c0 + 512]
                t1k = t1d[kc % 2]
                p.wait("vector", "VE", xs_last.get((kc, nt), 0))
                p.wait("vector", "VE", ln_war.get(f"t1rd{kc % 2}", 0))
                ts_ = p.op("vector", "tensor_sub", t1k[:, :], xsl, mb[1][:, :], inc="VE")
                p.wait("vector", "VE", ts_)
                p.op("vector", "tensor_mul", y_hi[:, kc * T + c0:kc * T + c0 + 512],
                     t1k[:, :], rb[1][:, :], inc="VE")
                p.wait("vector", "VE", ln_war.get("t2_rd_ve", 0))
                tf = p.op("vector", "tensor_mul", t2[:, :], t1k[:, :], rb[1][:, :], inc="VE")
                ln_war[f"t1rd{kc % 2}"] = tf
                p.wait("vector", "VE", tf)
                last = p.op("vector", "tensor_sub", y_lo[:, kc * T + c0:kc * T + c0 + 512],
                            t2[:, :], y_hi[:, kc * T + c0:kc * T + c0 + 512], inc="VE")
                ln_war["split_ve"] = last
                ln_war["t2_rd_ve"] = last
            bank_free[mb[0]] = ("VE", last)
            bank_free[rb[0]] = ("VE", last)
            return last

        ty1 = rows[:, 0:256]
        ty2 = rows[:, 256:512]

        def emit_y2b_core(p, mb, rb, nt, h2, hi_ap, lo_ap, war_hi, war_lo, st):
            c0 = nt * 512 + h2 * 256
            wait_bank(p, mb[0]); wait_bank(p, rb[0])
            p.wait("tensor", "VE", m2r2_tok.get(nt, 0))
            p.op("tensor", "matmul", mb[1][:, 0:256], ones_row, m2r2[0:1, c0:c0 + 256],
                 start=True, stop=True)
            tok_bc = p.op("tensor", "matmul", rb[1][:, 0:256], ones_row,
                          m2r2[0:1, T + c0:T + c0 + 256], start=True, stop=True, inc="MM")
            p.wait("vector", "MM", tok_bc)
            p.wait("vector", "MM", ln_war.get("bc_mm", 0))
            p.wait("vector", "MM", war_hi)
            p.wait("vector", "MM", war_lo)
            p.wait("vector", "VE", ln_war["split_ve"])
            last = None
            for kc in range(KC):
                xsl = xs[:, kc * T + c0:kc * T + c0 + 256]
                p.wait("vector", "VE", xs_last.get((kc, nt), 0))
                p.wait("vector", "VE", st["ty1_rd"])
                ts_ = p.op("vector", "tensor_sub", ty1, xsl, mb[1][:, 0:256], inc="VE")
                p.wait("vector", "VE", ts_)
                p.op("vector", "tensor_mul", hi_ap[:, kc * 256:(kc + 1) * 256],
                     ty1, rb[1][:, 0:256], inc="VE")
                p.wait("vector", "VE", st["ty2_ve"])
                tok_t2 = p.op("vector", "tensor_mul", ty2, ty1, rb[1][:, 0:256], inc="VE")
                st["ty1_rd"] = tok_t2
                p.wait("vector", "VE", tok_t2)
                last = p.op("vector", "tensor_sub", lo_ap[:, kc * 256:(kc + 1) * 256],
                            ty2, hi_ap[:, kc * 256:(kc + 1) * 256], inc="VE")
                st["ty2_ve"] = last
            bank_free[mb[0]] = ("VE", last)
            bank_free[rb[0]] = ("VE", last)
            return last

        def stage_split2(p, mat_d, hi, lo, split_tok, key, chain_state, war_mm,
                         s_lo=0, s_hi=16):
            csem = ("I2", "I3")
            for s in range(s_lo, s_hi):
                c = s % 2
                half = fstg[:, c * 512:(c + 1) * 512]
                if chain_state[c]:
                    p.wait("sync", "VE", chain_state[c])
                tok_d = p.op("sync", "dma_start",
                             out=half.rearrange("p (a c) -> p a c", a=8),
                             in_=mat_d[:, s * 64:(s + 1) * 64].rearrange(
                                 "(a p) c -> p a c", p=128),
                             inc=csem[c], inck=16)
                hi3 = hi[:, :].rearrange("p (a c) -> p a c", a=8)[:, :, s * 64:(s + 1) * 64]
                lo3 = lo[:, :].rearrange("p (a c) -> p a c", a=8)[:, :, s * 64:(s + 1) * 64]
                f3 = half.rearrange("p (a c) -> p a c", a=8)
                p.wait("scalar", csem[c], tok_d)
                p.wait("scalar", "MM", war_mm)
                tok_h = p.op("scalar", "copy", out=hi3, in_=f3, inc="SC")
                p.wait("vector", "SC", tok_h)
                p.wait("vector", "MM", war_mm)
                tok_l = p.op("vector", "tensor_sub", lo3, f3, hi3, inc="VE")
                split_tok[(key, s)] = (tok_h, tok_l)
                chain_state[c] = tok_l

        def emit_ffn2(p, ci, out_dma):
            with ExitStack() as fs:
                y_hi = fs.enter_context(nc.sbuf_tensor(f"yh{ci}", [128, KC * T], BF16))
                y_lo = fs.enter_context(nc.sbuf_tensor(f"yl{ci}", [128, KC * T], BF16))
                wg_t = fs.enter_context(nc.sbuf_tensor(f"wg{ci}", [128, 2 * 2048], BF16))
                wu_t = fs.enter_context(nc.sbuf_tensor(f"wu{ci}", [128, 2 * 2048], BF16))
                wd_t = fs.enter_context(nc.sbuf_tensor(f"wdt{ci}", [128, 2 * 2048], BF16))
                h_hi = fs.enter_context(nc.sbuf_tensor(f"hh{ci}", [128, 2 * 1024], BF16))
                h_lo = fs.enter_context(nc.sbuf_tensor(f"hl{ci}", [128, 2 * 1024], BF16))
                scr = [fs.enter_context(nc.sbuf_tensor(f"sc{ci}_{i}", [128, 512], F32))
                       for i in range(4)]
                pbk = [(f"b{i}", fs.enter_context(nc.psum_tensor(f"fp{ci}_{i}",
                                                                   [128, 512], F32)))
                       for i in range(8)]

                wsrc = {"g": wgd, "u": wud, "d": wdd}
                wtile = {"g": wg_t, "u": wu_t}
                conv_tok = {}
                slot_last_mm = {}
                chainF = {0: wstg_last[0], 1: wstg_last[1]}
                csemF = ("IN", "I4")

                dma_toks = {}

                def emit_stage_dma(hb, m):
                    toks = []
                    for h in range(2):
                        if chainF[h]:
                            p.wait("sync", "SC", chainF[h])
                        half = wstg[:, h * 1024:(h + 1) * 1024]
                        if m == "d":
                            srcap = wsrc[m][hb * HB + h * 128:hb * HB + (h + 1) * 128, :]
                            dstap = half
                        else:
                            srcap = wsrc[m][:, hb * HB + h * 128:hb * HB + (h + 1) * 128]\
                                .rearrange("(a p) c -> p a c", p=128)
                            dstap = half.rearrange("p (a c) -> p a c", a=8)
                        toks.append(p.op("sync", "dma_start", out=dstap, in_=srcap,
                                         inc=csemF[h], inck=16))
                        chainF[h] = 0
                    dma_toks[(hb, m)] = toks

                def emit_stage_conv(hb, m):
                    slot = hb % 2
                    toks = dma_toks[(hb, m)]
                    key = (m, slot)
                    tok_c = None
                    for h in range(2):
                        p.wait("scalar", csemF[h], toks[h])
                        if key in slot_last_mm:
                            p.wait("scalar", "MM", slot_last_mm[key])
                        half = wstg[:, h * 1024:(h + 1) * 1024]
                        if m == "d":
                            tok_c = p.op("scalar", "copy",
                                         out=wd_t[:, slot * 2048 + h * 1024:
                                                  slot * 2048 + (h + 1) * 1024],
                                         in_=half, inc="SC")
                        else:
                            cdst = wtile[m][:, slot * 2048:(slot + 1) * 2048].rearrange(
                                "p (k c) -> p k c", k=8)[:, :, h * 128:(h + 1) * 128]
                            tok_c = p.op("scalar", "copy", out=cdst,
                                         in_=half.rearrange("p (k c) -> p k c", k=8),
                                         inc="SC")
                        chainF[h] = tok_c
                    conv_tok[(hb, m)] = tok_c

                def emit_stage(hb, m):
                    emit_stage_dma(hb, m)
                    emit_stage_conv(hb, m)

                y_tok = {}

                def emit_ln_tile(nt):
                    tb = emit_ln_stats(p, pbk[6], pbk[7], pbk[6], pbk[7], nt,
                                       scr[2], scr[3], first_ln=(ci == 0), save=False,
                                       lnexp=True)
                    y_tok[nt] = emit_ln_split_ffn(p, pbk[6], pbk[7], nt, y_hi, y_lo,
                                                  scr[2], scr[3], tb)

                emit_stage_dma(0, "g")
                emit_ln_tile(0)
                emit_stage_conv(0, "g")
                emit_stage(0, "u")
                emit_stage(0, "d")
                emit_stage(1, "g")
                emit_stage(1, "u")
                emit_stage(1, "d")

                gu_i = [0]
                d_i = [0]
                sil_rd = [0]
                hf_rd = [0]
                h_tok = {}
                h_slot_rd = {}
                stt_tok = {}
                bank_mm = {}

                def emit_gu2(hb, nt, hslot):
                    slot = hb % 2
                    sil, hf = scr[0], scr[1]
                    for hl in range(2):
                        bg = pbk[gu_i[0] % 4]; gu_i[0] += 1
                        bu = pbk[gu_i[0] % 4]; gu_i[0] += 1
                        for m, bank in (("g", bg), ("u", bu)):
                            wait_bank(p, bank[0])
                            p.wait("tensor", "SC", conv_tok[(hb, m)])
                            p.wait("tensor", "VE", y_tok[nt])
                            n = 0
                            for ysrc in (y_hi, y_lo):
                                for kc in range(KC):
                                    tok = p.op(
                                        "tensor", "matmul", bank[1][:, :],
                                        wtile[m][:, slot * 2048 + kc * 256 + hl * 128:
                                                 slot * 2048 + kc * 256 + (hl + 1) * 128],
                                        ysrc[:, kc * T + nt * 512:kc * T + nt * 512 + 512],
                                        start=(n == 0), stop=(n == 15),
                                        inc=("MM" if n == 15 else None))
                                    n += 1
                            slot_last_mm[(m, slot)] = tok
                            bank_mm[bank[0]] = tok
                        hc0 = hslot * 1024 + hl * 512
                        p.wait("scalar", "MM", bank_mm[bg[0]])
                        p.wait("scalar", "VE", sil_rd[0])
                        if hslot in h_slot_rd:
                            p.wait("scalar", "MM", h_slot_rd[hslot])
                        tok_sil = p.op("scalar", "activation", out=sil[:, :], in_=bg[1][:, :],
                                       func=AF.Silu, scale=float(S_FFN), inc="SC")
                        bank_free[bg[0]] = ("SC", tok_sil)
                        p.wait("vector", "SC", tok_sil)
                        p.wait("vector", "MM", bank_mm[bu[0]])
                        if hslot in h_slot_rd:
                            p.wait("vector", "MM", h_slot_rd[hslot])
                        p.wait("vector", "VE", hf_rd[0])
                        tok_hf = p.op("vector", "tensor_mul", hf[:, :], sil[:, :], bu[1][:, :],
                                      inc="VE")
                        sil_rd[0] = tok_hf
                        bank_free[bu[0]] = ("VE", tok_hf)
                        p.wait("scalar", "VE", tok_hf)
                        tok_hi = p.op("scalar", "copy", out=h_hi[:, hc0:hc0 + 512],
                                      in_=hf[:, :], inc="SC")
                        p.wait("vector", "SC", tok_hi)
                        h_tok[(hb, nt)] = p.op("vector", "tensor_sub", h_lo[:, hc0:hc0 + 512],
                                               hf[:, :], h_hi[:, hc0:hc0 + 512], inc="VE")
                        hf_rd[0] = h_tok[(hb, nt)]

                def emit_down(hb, nt, hslot):
                    dslot = hb % 2
                    last_mm = None
                    for oc in range(8):
                        bank = pbk[4 + d_i[0] % 2]; d_i[0] += 1
                        wait_bank(p, bank[0])
                        p.wait("tensor", "SC", conv_tok[(hb, "d")])
                        p.wait("tensor", "VE", h_tok[(hb, nt)])
                        n = 0
                        for hsrc in (h_hi, h_lo):
                            for hl in range(2):
                                tok = p.op(
                                    "tensor", "matmul", bank[1][:, :],
                                    wd_t[:, dslot * 2048 + hl * 1024 + oc * 128:
                                         dslot * 2048 + hl * 1024 + (oc + 1) * 128],
                                    hsrc[:, hslot * 1024 + hl * 512:hslot * 1024 + (hl + 1) * 512],
                                    start=(n == 0), stop=(n == 3),
                                    inc=("MM" if n == 3 else None))
                                n += 1
                        last_mm = tok
                        slot_last_mm[("d", dslot)] = tok
                        xsl = xs[:, oc * T + nt * 512:oc * T + nt * 512 + 512]
                        p.wait("vector", "MM", tok)
                        if (oc, nt) in stt_tok:
                            p.wait("vector", "VE", stt_tok[(oc, nt)])
                        stt_tok[(oc, nt)] = p.op(
                            "vector", "scalar_tensor_tensor", out=xsl, in0=bank[1][:, :],
                            scalar=float(C_FFN), in1=xsl, op0=ALU.mult, op1=ALU.add, inc="VE")
                        xs_last[(oc, nt)] = stt_tok[(oc, nt)]
                        bank_free[bank[0]] = ("VE", stt_tok[(oc, nt)])
                    h_slot_rd[hslot] = last_mm

                pend = None
                items = []
                for nt in range(NT):
                    items.append((0, nt))
                    items.append((1, nt))
                for hb in range(2, NHB):
                    for nt in range(NT):
                        items.append((hb, nt))
                for idx, (hb, nt) in enumerate(items):
                    if idx == 1:
                        emit_stage_dma(2, "g")
                    if idx == 7:
                        emit_stage_conv(2, "g")
                        emit_stage_dma(2, "u")
                    if hb == 2 and nt == 0:
                        emit_stage_conv(2, "u")
                        emit_stage_dma(2, "d")
                    if hb >= 2 and nt == 1 and hb + 1 < NHB:
                        emit_stage_conv(hb + 1, "g")
                        emit_stage_dma(hb + 1, "u")
                    if hb >= 2 and nt == 2 and hb + 1 < NHB:
                        emit_stage_conv(hb + 1, "u")
                        emit_stage_dma(hb + 1, "d")
                    emit_gu2(hb, nt, idx % 2)
                    if hb == 0 and nt + 1 < NT:
                        emit_ln_tile(nt + 1)
                    if pend is not None:
                        emit_down(*pend)
                    pend = (hb, nt, idx % 2)
                    if hb >= 2 and nt == 0:
                        emit_stage_conv(hb, "d")
                        if hb + 1 < NHB:
                            emit_stage_dma(hb + 1, "g")
                emit_down(*pend)

                wstg_last[0] = chainF[0]
                wstg_last[1] = chainF[1]
                if out_dma:
                    for nt in range(NT):
                        for oc in range(8):
                            p.wait("gpsimd", "VE", stt_tok[(oc, nt)])
                            p.op("gpsimd", "dma_start",
                                 out=yd[oc * 128:(oc + 1) * 128, nt * 512:(nt + 1) * 512],
                                 in_=xs[:, oc * T + nt * 512:oc * T + nt * 512 + 512],
                                 inc="GP", inck=16)
                    p.wait("gpsimd", "GP", cnt["GP"])

                barrier()

        pf1 = P
        phase_entry(pf1)
        emit_ffn2(pf1, 0, out_dma=False)

        pa = P
        p = pa
        phase_entry(pa, tensor_too=False)
        with ExitStack() as fs:
            wk_hi = fs.enter_context(nc.sbuf_tensor("wk_hi", [128, 8192], BF16))
            wk_lo = fs.enter_context(nc.sbuf_tensor("wk_lo", [128, 8192], BF16))
            wv_hi = fs.enter_context(nc.sbuf_tensor("wv_hi", [128, 8192], BF16))
            wv_lo = fs.enter_context(nc.sbuf_tensor("wv_lo", [128, 8192], BF16))
            y2h = fs.enter_context(nc.sbuf_tensor("y2h", [128, 2 * 2048], BF16))
            y2l = fs.enter_context(nc.sbuf_tensor("y2l", [128, 2 * 2048], BF16))
            kfT = fs.enter_context(nc.sbuf_tensor("kfT", [128, 2048], F32))
            vT = fs.enter_context(nc.sbuf_tensor("vT", [128, 2048], F32))
            scrA = [fs.enter_context(nc.sbuf_tensor(f"sA{i}", [128, 512], F32)) for i in range(4)]
            pbA = [(f"b{i}", fs.enter_context(nc.psum_tensor(f"ap{i}", [128, 512], F32)))
                   for i in range(8)]
            kv_b, ks_b = pbA[4], pbA[5]

            split_tok = {}
            split_tokB = {}
            chainA = [0, 0]
            hA = {}
            stage_split2(p, wkd, wk_hi, wk_lo, split_tok, "k", chainA, bar["MM"])
            tok_bcA = {}
            tok_bcA[0] = emit_ln_stats(p, pbA[6], pbA[7], pbA[6], pbA[7], 0,
                                       scrA[2], scrA[3], first_ln=False, save=True,
                                       lnexp=True)

            def emit_ln_split_pair(nt, off, w, mb, rb, t1, t2, tok_bc, war_mm, slot):
                c0 = nt * 512 + off
                s0 = slot * 2048
                p.wait("vector", "MM", tok_bc)
                if war_mm:
                    p.wait("vector", "MM", war_mm)
                    p.wait("scalar", "MM", war_mm)
                last = None
                t1d = {0: t1, 1: s5}
                for kc in range(KC):
                    xsl = xs[:, kc * T + c0:kc * T + c0 + w]
                    t1k = t1d[kc % 2]
                    p.wait("vector", "VE", xs_last.get((kc, nt), 0))
                    p.wait("vector", "VE", ln_war.get(f"t1rd{kc % 2}", 0))
                    ts_ = p.op("vector", "tensor_sub", t1k[:, 0:w], xsl, mb[1][:, off:off + w],
                               inc="VE")
                    p.wait("vector", "VE", ts_)
                    p.op("vector", "tensor_mul", y2h[:, s0 + kc * w:s0 + (kc + 1) * w],
                         t1k[:, 0:w], rb[1][:, off:off + w], inc="VE")
                    p.wait("vector", "VE", ln_war.get("t2_rd_ve", 0))
                    tf = p.op("vector", "tensor_mul", t2[:, 0:w], t1k[:, 0:w],
                              rb[1][:, off:off + w], inc="VE")
                    ln_war[f"t1rd{kc % 2}"] = tf
                    p.wait("vector", "VE", tf)
                    last = p.op("vector", "tensor_sub", y2l[:, s0 + kc * w:s0 + (kc + 1) * w],
                                t2[:, 0:w], y2h[:, s0 + kc * w:s0 + (kc + 1) * w], inc="VE")
                    ln_war["split_ve"] = last
                    ln_war["t2_rd_ve"] = last
                bank_free[mb[0]] = ("VE", last)
                bank_free[rb[0]] = ("VE", last)
                return last

            projA_i = [0]
            eluA = {"t1_rd": 0, "t2_rd": 0}
            kv_started = {0: False, 64: False}
            ks_started = [False]
            mm_proj_last = [bar["MM"], bar["MM"]]
            halves = [(nt, h2) for nt in range(NT) for h2 in range(2)]
            kvacc_tok = [bar["MM"]]

            def emit_projA_mm(key, hi, lo, y2tok, slot):
                banks = []
                s0 = slot * 2048
                for tp in range(2):
                    for dh in range(2):
                        bank = pbA[projA_i[0] % 4]; projA_i[0] += 1
                        wait_bank(p, bank[0])
                        p.wait("tensor", "VE", y2tok)
                        p.wait("tensor", "SC", split_tok[(key, 8 * dh + 7)][0])
                        p.wait("tensor", "VE", split_tok[(key, 8 * dh + 7)][1])
                        n = 0
                        for ysrc, wsrc_ in ((y2h, hi), (y2l, hi), (y2h, lo)):
                            for kc in range(KC):
                                tok = p.op(
                                    "tensor", "matmul", bank[1][:, :],
                                    ysrc[:, s0 + kc * 256 + tp * 128:
                                         s0 + kc * 256 + (tp + 1) * 128],
                                    wsrc_[:, kc * 1024 + dh * 512:kc * 1024 + (dh + 1) * 512],
                                    start=(n == 0), stop=(n == 23),
                                    inc=("MM" if n == 23 else None))
                                n += 1
                        banks.append((tp, dh, bank, tok))
                mm_proj_last[slot] = banks[-1][3]
                return banks

            def emit_projA_evac(key, banks):
                last = None
                for (tp, dh, bank, tok) in banks:
                    dstc = tp * 1024 + dh * 512
                    dst = kfT if key == "k" else vT
                    p.wait("vector", "MM", tok)
                    p.wait("vector", "MM", kvacc_tok[0])
                    if key == "k":
                        t1, t2 = scrA[0], scrA[1]
                        p.wait("vector", "SC", eluA["t1_rd"])
                        tok_min = p.op("vector", "tensor_scalar_min", t1[:, :], bank[1][:, :],
                                       0.0, inc="VE")
                        p.wait("scalar", "VE", tok_min)
                        p.wait("scalar", "VE", eluA["t2_rd"])
                        eluA["t1_rd"] = p.op("scalar", "activation", out=t2[:, :], in_=t1[:, :],
                                             func=AF.Exp, inc="SC")
                        p.wait("vector", "SC", eluA["t1_rd"])
                        last = p.op("vector", "scalar_tensor_tensor", out=dst[:, dstc:dstc + 512],
                                    in0=bank[1][:, :], scalar=0.0, in1=t2[:, :],
                                    op0=ALU.max, op1=ALU.add, inc="VE")
                        eluA["t2_rd"] = last
                    else:
                        last = p.op("vector", "tensor_copy", out=dst[:, dstc:dstc + 512],
                                    in_=bank[1][:, :], inc="VE")
                    bank_free[bank[0]] = ("VE", last)
                return last

            def emit_kvacc(ev_tok, is_last):
                p.wait("tensor", "VE", ev_tok)
                tok = None
                for tp in range(2):
                    for hp in range(8):
                        for par in (0, 64):
                            h = 2 * hp + (1 if par else 0)
                            stt_ = not kv_started[par]
                            kv_started[par] = True
                            p.op("tensor", "matmul",
                                 kv_b[1][par:par + 64, hp * 64:(hp + 1) * 64],
                                 kfT[:, tp * 1024 + h * 64:tp * 1024 + (h + 1) * 64],
                                 vT[:, tp * 1024 + h * 64:tp * 1024 + (h + 1) * 64],
                                 start=stt_, stop=(is_last and tp == 1 and hp == 7),
                                 tile_position=(0, par),
                                 skip_group_check=True)
                    for hp in range(8):
                        stt_ = not ks_started[0]
                        ks_started[0] = True
                        tok = p.op("tensor", "matmul", ks_b[1][:, hp:hp + 1],
                                   kfT[:, tp * 1024 + hp * 128:tp * 1024 + (hp + 1) * 128],
                                   ones_col,
                                   start=stt_, stop=(is_last and tp == 1 and hp == 7),
                                   skip_group_check=True,
                                   inc=("MM" if (tp == 1 and hp == 7) else None))
                kvacc_tok[0] = tok
                return tok

            pendA = None
            for hi_, (nt, h2) in enumerate(halves):
                slot = hi_ % 2
                if pendA is not None and h2 == 1:
                    emit_kvacc(pendA, is_last=False)
                    pendA = None
                y2tok = emit_ln_split_pair(nt, h2 * 256, 256, pbA[6], pbA[7],
                                           scrA[2], scrA[3], tok_bcA[nt],
                                           mm_proj_last[slot], slot)
                bk = emit_projA_mm("k", wk_hi, wk_lo, y2tok, slot)
                if hi_ == 7:
                    hA["kproj_mm"] = mm_proj_last[slot]
                if hi_ == 0:
                    stage_split2(p, wvd, wv_hi, wv_lo, split_tok, "v", chainA, bar["MM"])
                if h2 == 1 and nt + 1 < NT:
                    tok_bcA[nt + 1] = emit_ln_stats(p, pbA[6], pbA[7], pbA[6], pbA[7],
                                                    nt + 1, scrA[2], scrA[3],
                                                    first_ln=False, save=True,
                                                    lnexp=True)
                if pendA is not None:
                    emit_kvacc(pendA, is_last=False)
                    pendA = None
                evk = emit_projA_evac("k", bk)
                if hi_ == 7:
                    stage_split2(p, wqd, wk_hi, wk_lo, split_tokB, "q", chainA,
                                 hA["kproj_mm"], 0, 8)
                bv = emit_projA_mm("v", wv_hi, wv_lo, y2tok, slot)
                if hi_ == 7:
                    hA["vproj_mm"] = mm_proj_last[slot]
                evv = emit_projA_evac("v", bv)
                pendA = evv
            emit_kvacc(pendA, is_last=True)
            stage_split2(p, wqd, wk_hi, wk_lo, split_tokB, "q", chainA,
                         hA["kproj_mm"], 8, 16)

            p.wait("vector", "MM", cnt["MM"])
            p.op("vector", "tensor_copy", out=scrA[0][:, :], in_=kv_b[1][:, :], inc="VE")
            tok_ccs = p.op("vector", "tensor_copy", out=scrA[1][:, 0:8], in_=ks_b[1][:, 0:8],
                           inc="VE")
            p.wait("sync", "VE", tok_ccs)
            p.op("sync", "dma_start", out=cc_in.ap()[:, 0:512], in_=scrA[0][:, :],
                 inc="LD", inck=16)
            tok_d = p.op("sync", "dma_start", out=cc_in.ap()[:, 512:520],
                         in_=scrA[1][:, 0:8], inc="LD", inck=16)
            p.wait("gpsimd", "LD", tok_d)
            p.op("gpsimd", "collective_compute", "AllReduce", ALU.add,
                 replica_groups=[[0, 1], [2, 3], [4, 5], [6, 7]],
                 ins=[cc_in.ap().opt()], outs=[cc_out.ap().opt()], inc="CC")
            p.wait("gpsimd", "CC", cnt["CC"])
            p.op("gpsimd", "dma_start", out=kvks[:, :], in_=cc_out.ap(), inc="GP", inck=16)

            stB = {"ty1_rd": m2r2_tok[3], "ty2_ve": m2r2_tok[3]}
            y2b0_tok = emit_y2b_core(p, pbA[6], pbA[7], 0, 0,
                                     y2h[:, 0:2048], y2h[:, 2048:4096],
                                     mm_proj_last[0], mm_proj_last[1], stB)
            barrier()
        ar_tok = cnt["GP"]
        a_ve_end = tok_ccs
        a_sc_end = eluA["t1_rd"]
        y2b0_hi = y2h[:, 0:2048]
        y2b0_lo = y2h[:, 2048:4096]
        wqs_hi, wqs_lo = wk_hi, wk_lo

        pb_ = P
        p = pb_
        with ExitStack() as fs:
            wq_hi = fs.enter_context(nc.sbuf_tensor("wq_hi", [128, 8192], BF16))
            wq_lo = fs.enter_context(nc.sbuf_tensor("wq_lo", [128, 8192], BF16))
            wo_hi = fs.enter_context(nc.sbuf_tensor("wo_hi", [128, 8192], BF16))
            wo_lo = fs.enter_context(nc.sbuf_tensor("wo_lo", [128, 8192], BF16))
            y2h = fs.enter_context(nc.sbuf_tensor("y2hB", [128, 2048], BF16))
            y2l = fs.enter_context(nc.sbuf_tensor("y2lB", [128, 2048], BF16))
            qf = fs.enter_context(nc.sbuf_tensor("qf", [128, 2 * 2048], F32))
            at_hi = fs.enter_context(nc.sbuf_tensor("at_hi", [128, 2048], BF16))
            at_lo = fs.enter_context(nc.sbuf_tensor("at_lo", [128, 2048], BF16))
            ksB = fs.enter_context(nc.sbuf_tensor("ksB", [128, 8 * 128], F32))

            tq1 = rows[:, 512:768]
            tq2 = rows[:, 768:1024]
            ta1 = rows[:, 1024:1280]
            dne = rows[:, 1280:1536]
            scB = {"tq1_rd": a_sc_end, "tq2_rd": a_ve_end,
                   "ta1_rd": a_sc_end, "rcp_rd": a_ve_end}
            pbB = [(f"b{i}", fs.enter_context(nc.psum_tensor(f"bp{i}", [128, 512], F32)))
                   for i in range(8)]

            chainB = [chainA[0], chainA[1]]

            qp_i = [0]
            op_i = [0]
            mm_qproj_last = [bar["MM"]]
            mm_oproj_last = [bar["MM"]]
            mm_hp_last = [bar["MM"]]

            def emit_y2b(nt, h2):
                return emit_y2b_core(p, pbB[6], pbB[7], nt, h2, y2h, y2l,
                                     mm_qproj_last[0], mm_qproj_last[0], stB)

            ar_waited = [False]
            at_last = [a_ve_end]
            ksb_tok = [0]

            def emit_hp_step(hp, qftok, slot):
                qf0 = slot * 2048
                dn, qk = pbB[4], pbB[5]
                if not ar_waited[0]:
                    p.wait("tensor", "GP", ar_tok)
                    p.wait("tensor", "VE", ksb_tok[0])
                    ar_waited[0] = True
                wait_bank(p, dn[0])
                p.wait("tensor", "VE", qftok)
                kb = ksB[:, hp * 128:(hp + 1) * 128]
                p.op("tensor", "matmul", dn[1][0:64, 0:256], kb[0:64, 0:64],
                     qf[0:64, qf0 + hp * 256:qf0 + (hp + 1) * 256], start=True, stop=True,
                     tile_position=(0, 0), skip_group_check=True)
                tok_dn = p.op("tensor", "matmul", dn[1][64:128, 0:256], kb[64:128, 64:128],
                              qf[64:128, qf0 + hp * 256:qf0 + (hp + 1) * 256],
                              start=True, stop=True,
                              tile_position=(64, 64), skip_group_check=True, inc="MM")
                wait_bank(p, qk[0])
                p.op("tensor", "matmul", qk[1][0:64, 0:256], kvks[0:64, hp * 64:(hp + 1) * 64],
                     qf[0:64, qf0 + hp * 256:qf0 + (hp + 1) * 256], start=True, stop=True,
                     tile_position=(0, 0), skip_group_check=True)
                tok_qk = p.op("tensor", "matmul", qk[1][64:128, 0:256],
                              kvks[64:128, hp * 64:(hp + 1) * 64],
                              qf[64:128, qf0 + hp * 256:qf0 + (hp + 1) * 256],
                              start=True, stop=True,
                              tile_position=(64, 64), skip_group_check=True, inc="MM")
                mm_hp_last[0] = tok_qk
                p.wait("scalar", "MM", tok_dn)
                p.wait("scalar", "SC", scB["ta1_rd"])
                tok_eps = p.op("scalar", "activation", out=dne, in_=dn[1][:, 0:256],
                               func=AF.Ln, bias=float(EPS_AT), scale=1.0, inc="SC")
                bank_free[dn[0]] = ("SC", tok_eps)
                p.wait("scalar", "VE", scB.get("rcp_rd", 0))
                p.wait("scalar", "SC", tok_eps)
                tok_rcp = p.op("scalar", "activation", out=dne, in_=dne,
                               func=AF.Exp, scale=-1.0, inc="SC")
                p.wait("vector", "MM", tok_qk)
                p.wait("vector", "SC", tok_rcp)
                p.wait("vector", "MM", mm_oproj_last[0])
                p.wait("vector", "SC", scB["ta1_rd"])
                p.wait("vector", "VE", at_last[0])
                tok_at = p.op("vector", "tensor_mul", ta1, qk[1][:, 0:256], dne,
                              inc="VE")
                scB["rcp_rd"] = tok_at
                bank_free[qk[0]] = ("VE", tok_at)
                p.wait("scalar", "VE", tok_at)
                p.wait("scalar", "MM", mm_oproj_last[0])
                scB["ta1_rd"] = p.op("scalar", "copy", out=at_hi[:, hp * 256:(hp + 1) * 256],
                                     in_=ta1, inc="SC")
                p.wait("vector", "SC", scB["ta1_rd"])
                at_last[0] = p.op("vector", "tensor_sub", at_lo[:, hp * 256:(hp + 1) * 256],
                                  ta1, at_hi[:, hp * 256:(hp + 1) * 256], inc="VE")
                return at_last[0]

            def emit_qproj(y2tok, slot, srcs, hp_cb=None, nbk=2):
                qf0 = slot * 2048
                p.wait("vector", "VE", a_ve_end)
                last = None
                for oc in range(8):
                    bank = pbB[qp_i[0] % nbk]; qp_i[0] += 1
                    wait_bank(p, bank[0])
                    p.wait("tensor", "SC", split_tokB[("q", 2 * oc + 1)][0])
                    p.wait("tensor", "VE", split_tokB[("q", 2 * oc + 1)][1])
                    p.wait("tensor", "VE", y2tok)
                    n = 0
                    for ysrc, wsrc_ in ((srcs[0], wqs_hi), (srcs[1], wqs_hi),
                                        (srcs[0], wqs_lo)):
                        for kc in range(KC):
                            tok = p.op("tensor", "matmul", bank[1][:, 0:256],
                                       wsrc_[:, kc * 1024 + oc * 128:kc * 1024 + (oc + 1) * 128],
                                       ysrc[:, kc * 256:(kc + 1) * 256],
                                       start=(n == 0), stop=(n == 23),
                                       inc=("MM" if n == 23 else None))
                            n += 1
                    mm_qproj_last[0] = tok
                    if hp_cb is not None:
                        hp_cb(oc)
                    p.wait("vector", "MM", tok)
                    p.wait("vector", "MM", mm_hp_last[0])
                    p.wait("vector", "SC", scB["tq1_rd"])
                    tok_min = p.op("vector", "tensor_scalar_min", tq1, bank[1][:, 0:256],
                                   0.0, inc="VE")
                    p.wait("scalar", "VE", tok_min)
                    p.wait("scalar", "VE", scB["tq2_rd"])
                    scB["tq1_rd"] = p.op("scalar", "activation", out=tq2, in_=tq1,
                                         func=AF.Exp, inc="SC")
                    p.wait("vector", "SC", scB["tq1_rd"])
                    last = p.op("vector", "scalar_tensor_tensor",
                                out=qf[:, qf0 + oc * 256:qf0 + (oc + 1) * 256],
                                in0=bank[1][:, 0:256], scalar=0.0, in1=tq2,
                                op0=ALU.max, op1=ALU.add, inc="VE")
                    scB["tq2_rd"] = last
                    bank_free[bank[0]] = ("VE", last)
                return last

            def emit_oproj(nt, h2, attok):
                c0 = nt * 512 + h2 * 256
                p.wait("vector", "VE", a_ve_end)
                for oc in range(8):
                    bank = pbB[2 + op_i[0] % 2]; op_i[0] += 1
                    wait_bank(p, bank[0])
                    p.wait("tensor", "SC", split_tokB[("o", 2 * oc + 1)][0])
                    p.wait("tensor", "VE", split_tokB[("o", 2 * oc + 1)][1])
                    p.wait("tensor", "VE", attok)
                    n = 0
                    for asrc, wsrc_ in ((at_hi, wo_hi), (at_lo, wo_hi), (at_hi, wo_lo)):
                        for kc in range(KC):
                            tok = p.op("tensor", "matmul", bank[1][:, 0:256],
                                       wsrc_[:, kc * 1024 + oc * 128:kc * 1024 + (oc + 1) * 128],
                                       asrc[:, kc * 256:(kc + 1) * 256],
                                       start=(n == 0), stop=(n == 23),
                                       inc=("MM" if n == 23 else None))
                            n += 1
                    mm_oproj_last[0] = tok
                    xsl = xs[:, oc * T + c0:oc * T + c0 + 256]
                    p.wait("vector", "MM", tok)
                    p.wait("vector", "LD", tok_bo)
                    tk = p.op("vector", "scalar_tensor_tensor", out=xsl, in0=bank[1][:, 0:256],
                              scalar=bo_sb[:, oc:oc + 1], in1=xsl, op0=ALU.add, op1=ALU.add,
                              inc="VE")
                    xs_last[(oc, nt)] = tk
                    bank_free[bank[0]] = ("VE", tk)

            halvesB = [(nt, h2) for nt in range(NT) for h2 in range(2)]
            pend1 = None
            pend2 = None
            y2next = y2b0_tok
            for i, (nt, h2) in enumerate(halvesB):
                y2t = y2next
                slot = i % 2
                if i == 2:
                    p.wait("vector", "GP", ar_tok)
                    for _hp in range(8):
                        p.op("vector", "tensor_scalar",
                             ksB[:, _hp * 128:(_hp + 1) * 128], kvks[:, 0:128], 0.0,
                             kvks[:, 512 + _hp:513 + _hp],
                             op0=ALU.mult, op1=ALU.add, inc="VE")
                    ksb_tok[0] = cnt["VE"]
                ysrcs = (y2b0_hi, y2b0_lo) if i == 0 else (y2h, y2l)
                if pend2 is None:
                    qt = emit_qproj(y2t, slot, ysrcs, nbk=4)
                    if i == 1:
                        qp_i[0] = 0
                    if i == 0:
                        stage_split2(p, wod, wo_hi, wo_lo, split_tokB, "o", chainB,
                                     hA["vproj_mm"])
                else:
                    (pnt2, ph22, pq2, ps2) = pend2
                    qt = emit_qproj(y2t, slot, ysrcs,
                                    hp_cb=lambda g, pq=pq2, ps=ps2: emit_hp_step(g, pq, ps))
                if i + 1 < len(halvesB):
                    y2next = emit_y2b(*halvesB[i + 1])
                if pend2 is not None:
                    emit_oproj(pnt2, ph22, at_last[0])
                pend2 = pend1
                pend1 = (nt, h2, qt, slot)
            for (pnt2, ph22, pq2, ps2) in (pend2, pend1):
                for hp in range(8):
                    emit_hp_step(hp, pq2, ps2)
                emit_oproj(pnt2, ph22, at_last[0])

            barrier()

        pf2 = P
        phase_entry(pf2, tensor_too=False)
        emit_ffn2(pf2, 1, out_dma=True)

        P.run(nc)

    return nc


_NC_CACHE = []


def kernel(L0_attn_x, L0_attn_wq, L0_attn_wk, L0_attn_wv, L0_attn_wo, L0_attn_bo,
           L0_ffn_w_gate, L0_ffn_w_up, L0_ffn_w_down):
    x = np.asarray(L0_attn_x)
    wqT = np.ascontiguousarray(np.asarray(L0_attn_wq).T)
    wkT = np.ascontiguousarray(np.asarray(L0_attn_wk).T)
    wvT = np.ascontiguousarray(np.asarray(L0_attn_wv).T)
    woT = np.ascontiguousarray(np.asarray(L0_attn_wo).T)
    bo2 = np.ascontiguousarray(np.asarray(L0_attn_bo).reshape(8, 128).T)
    wgT = np.ascontiguousarray(np.asarray(L0_ffn_w_gate).T)
    wuT = np.ascontiguousarray(np.asarray(L0_ffn_w_up).T)
    wdT = np.ascontiguousarray(np.asarray(L0_ffn_w_down).T)

    cons = np.zeros((128, 130), np.float32)
    cons[:, 0] = 1.0
    cons[:, 1] = EPS_AT
    cons[0, 2:130] = 1.0

    in_maps = []
    for c in range(8):
        b, h = c // 2, c % 2
        xsh = np.ascontiguousarray(x[b, h * T:(h + 1) * T, :].T)
        in_maps.append({"x": xsh, "wq": wqT, "wk": wkT, "wv": wvT, "wo": woT,
                        "bo": bo2, "wg": wgT, "wu": wuT, "wd": wdT, "cons": cons})

    if not _NC_CACHE:
        _NC_CACHE.append(build_nc())
    nc = _NC_CACHE[0]
    res = run_bass_kernel_spmd(nc, in_maps, core_ids=list(range(8)))

    out = np.empty((4, 4096, 1024), np.float32)
    for c in range(8):
        b, h = c // 2, c % 2
        out[b, h * T:(h + 1) * T, :] = res.results[c]["y"].T
    return out
